# version 03
# baseline (speedup 1.0000x reference)
import numpy as np
import concourse.bass as bass
import concourse.mybir as mybir
from concourse.bass_utils import run_bass_kernel_spmd
from concourse.alu_op_type import AluOpType as ALU
from contextlib import ExitStack

F32 = mybir.dt.float32
BF16 = mybir.dt.bfloat16
U8 = mybir.dt.uint8
AF = mybir.ActivationFunctionType
AX = mybir.AxisListType

RC = float(3 * 2 ** 22)
D = 1024
TOTAL = 2080
KH = 32
NCT = 17
DFF = 4096
B_PER_CORE = 2

def _rows(ct):
    return 32 if ct == NCT - 1 else 128


class Sem:
    def __init__(self, handle, step):
        self.h, self.step, self.n = handle, step, 0


class Buf:

    def __init__(self, view):
        self.v = view
        self.free = None


class Arena:
    def __init__(self, ap_u8, nbytes, name, off=0):
        self.ap, self.nbytes, self.off, self.name = ap_u8, nbytes, off, name

    def alloc(self, cols, dt=F32, rows=128):
        nbyt = cols * mybir.dt.size(dt)
        assert self.off + nbyt <= self.nbytes, (
            f"arena {self.name} overflow at +{nbyt}: {self.off} > {self.nbytes}")
        v = self.ap[0:rows, self.off:self.off + nbyt].bitcast(dt)
        self.off = (self.off + nbyt + 63) & ~63
        return v

    def buf(self, cols, dt=F32, rows=128):
        return Buf(self.alloc(cols, dt, rows))


class KB:
    def __init__(self, L=2048, TA=128, TB1=1024, TB2=512, debug=False, taps=None):
        self.debug = debug
        self.taps = taps
        self.dbg_names = []
        self.L, self.TA = L, TA
        self.TB1, self.TB2 = min(TB1, L), min(TB2, L)
        nc = bass.Bass("TRN2", target_bir_lowering=False, debug=False)
        self.nc = nc
        self.S = {k: [] for k in ("sync", "vector", "scalar", "tensor", "gpsimd")}
        self.stack = ExitStack()

        dr = lambda n, s, dt=F32, k="ExternalInput": nc.dram_tensor(n, s, dt, kind=k).ap()
        self.x_h = dr("x", [B_PER_CORE, L, D])
        self.c_h = dr("c2", [B_PER_CORE, D])
        self.wA_h = dr("w_inproj", [D, TOTAL])
        self.cw_h = dr("conv_w_l", [128, 4 * NCT])
        self.cb_h = dr("conv_b_l", [128, NCT])
        self.th_h = dr("theta2", [1, 2])
        self.dsl_h = dr("dslope", [28, 1])
        self.asl_h = dr("aslope", [4, 1])
        self.sc_h = dr("score", [KH, 1])
        self.dlg_h = dr("deriv", [1, 3])
        self.ns_h = dr("nsc", [128, 2])
        self.wre_h = dr("w_re", [128, KH])
        self.wim_h = dr("w_im", [128, KH])
        self.wO_h = dr("w_outproj", [D, D])
        self.wiT_h = dr("w_in_T", [DFF, D])
        self.bi_h = dr("b_in_t", [128, 32])
        self.wo_h = dr("w_out_f", [DFF, D])
        self.bo_h = dr("b_out_r", [1, D])
        self.wm_h = dr("wmod", [D, 6 * D])
        self.y_h = dr("y", [B_PER_CORE, L, D], F32, "ExternalOutput")
        self.xres_h = dr("xres_s", [B_PER_CORE, L, D], F32, "Internal")
        self.hid_h = dr("hidden_s", [DFF, B_PER_CORE * L], F32, "Internal")
        self.wqi_h = dr("wqi_s", [D, DFF], BF16, "Internal")
        self.wqo_h = dr("wqo_s", [DFF, D], BF16, "Internal")
        self.bc_h = dr("bcast_s", [16, D], F32, "Internal")
        self.cc1i_h = dr("cc1_in", [1, 4], F32, "Internal")
        self.cc1o_h = dr("cc1_out", [1, 4], F32, "Internal")
        self.cc2i_h = dr("cc2_in", [1, 4], F32, "Internal")
        self.cc2o_h = dr("cc2_out", [1, 4], F32, "Internal")
        self.tw_h = dr("tw_s", [KH, L], F32, "Internal")

        mks = lambda n, st: Sem(self.stack.enter_context(nc.semaphore(n)), st)
        self.sv = mks("sv", 1)
        self.ss = mks("ss", 1)
        self.st_ = mks("st", 1)
        self.sg = mks("sg", 1)
        self.cc = mks("ccs", 1)
        self.pool = {"sync": [mks(f"dw{i}", 16) for i in range(7)],
                     "gpsimd": [mks(f"dg{i}", 16) for i in range(7)]}
        self.pool_rr = {"sync": 0, "gpsimd": 0}
        self.allsems = ([self.sv, self.ss, self.st_, self.sg, self.cc]
                        + self.pool["sync"] + self.pool["gpsimd"])

        AR = 204 * 1024
        self.arena_t = self.stack.enter_context(nc.sbuf_tensor("arena", [128, AR], U8))
        self.AR = AR
        self.psum = [self.stack.enter_context(nc.psum_tensor(f"pb{i}", [128, 512], F32))
                     for i in range(8)]
        self.psum_free = [None] * 8
        self._residents()

    def em(self, eng, fn, inc=None):
        if inc is None:
            self.S[eng].append(fn)
            return None
        inc.n += inc.step
        h, s = inc.h, inc.step
        self.S[eng].append(lambda e: fn(e).then_inc(h, s))
        return inc.n

    def wait(self, eng, sem, val):
        assert 0 <= val <= sem.n, f"wait for future value {val} > {sem.n} ({eng})"
        if val > 0:
            self.S[eng].append(lambda e: e.wait_ge(sem.h, val))

    def waitf(self, eng, free):
        if free is not None:
            self.wait(eng, free[0], free[1])

    def dma(self, qeng, _sem_ignored, out, in_, slow=False):
        p = self.pool[qeng]
        i = self.pool_rr[qeng] % len(p)
        self.pool_rr[qeng] += 1
        s = p[i]
        if s.n > 0:
            self.wait(qeng, s, s.n)
        if slow:
            self.em(qeng, lambda e: e.dma_start(out=out, in_=in_,
                                                allow_slow_non_contiguous=True), inc=s)
        else:
            self.em(qeng, lambda e: e.dma_start(out=out, in_=in_), inc=s)
        return (s, s.n)

    def dbg(self, name, view, src_sem=None):
        if not self.debug:
            return
        if self.taps is not None and name not in self.taps:
            return
        rows, cols = view.shape[0], view.shape[1]
        dtt = view.dtype
        t = self.nc.dram_tensor("dbg_" + name, [rows, cols], dtt, kind="ExternalOutput").ap()
        self.dbg_names.append("dbg_" + name)
        for s in (self.sv, self.ss, self.st_, self.sg):
            if s.n:
                self.wait("gpsimd", s, s.n)
        tok = self.dma("gpsimd", None, t[:, :], view, slow=True)
        for eng in self.S:
            self.waitf(eng, tok)

    def barrier(self):
        for eng in self.S:
            for s in self.allsems:
                if s.n > 0:
                    self.wait(eng, s, s.n)

    def bank(self, i, eng, sem=None):
        self.waitf(eng, self.psum_free[i])
        return self.psum[i]

    def bank_consumed(self, i, sem, val):
        self.psum_free[i] = (sem, val)

    def _residents(self):
        A = Arena(self.arena_t.ap(), self.AR, "res")
        TA, L = self.TA, self.L
        self.ident = A.alloc(128)
        self.identb = A.alloc(128, BF16)
        self.zrow = A.alloc(max(TA, 64))
        self.iof1 = A.alloc(128)
        self.iop = A.alloc(1)
        self.k4v = A.alloc(1)
        self.selg = [A.alloc(KH) for _ in range(8)]
        self.Ee = [A.alloc(128, rows=32) for _ in range(8)]
        self.wdiag = [A.alloc(128) for _ in range(4)]
        self.cwt = A.alloc(4 * NCT)
        self.cbt = A.alloc(NCT)
        self.scv = A.alloc(1, rows=KH)
        self.w0v = A.alloc(1)
        self.w1v = A.alloc(1)
        self.w2nv = A.alloc(1)
        self.g2v = A.alloc(1)
        self.g2h = A.alloc(1)
        self.cav = [A.alloc(8) for _ in range(2)]
        self.sav = [A.alloc(8) for _ in range(2)]
        self.cbv = [A.alloc(8) for _ in range(2)]
        self.sbv = [A.alloc(8) for _ in range(2)]
        self.cbsxv = [A.alloc(8) for _ in range(2)]
        self.sbsxv = [A.alloc(8) for _ in range(2)]
        self.swv = A.alloc(32)
        self.swov = A.alloc(32)
        self.iswx = A.alloc(32)
        self.vrv = A.alloc(32)
        self.biT = A.alloc(32)
        self.sxv = A.alloc(1)
        self.shv = A.alloc(1)
        self.amaxh = A.alloc(1)
        self.amaxd = A.alloc(1)
        self.am8 = A.alloc(32)
        self.stats = A.alloc(2 * (B_PER_CORE * L // 128))
        self.carry = A.alloc(33)
        self.tails = A.alloc(3 * NCT)
        self.onesr = A.alloc(128, rows=1)
        self.rowt = A.alloc(D, rows=2)
        self.amt = A.alloc(1)
        self.amt2 = A.alloc(1)
        self.amt3 = A.alloc(1)
        self.consts = {}
        for val in (1e-5, -0.5, 1.0 / 128, 1e-4):
            t = A.alloc(1)
            self.nc.const_aps.register(F32, val, t[:, 0:1])
            self.consts[val] = t
        self.res_end = A.off
        self.gaB = A.alloc(D)
        self.WAhi = A.alloc(8 * TOTAL, BF16)
        self.WAlo = A.alloc(8 * TOTAL, BF16)
        self.WOhi = A.alloc(8 * D, BF16)
        self.WOlo = A.alloc(8 * D, BF16)
        self.resA_end = A.off

    def prologue(self):
        A = Arena(self.arena_t.ap(), self.AR, "prol", self.resA_end)
        nc, L = self.nc, self.L
        em, wait, dma, waitf = self.em, self.wait, self.dma, self.waitf
        sv, ss, st_, sg = self.sv, self.ss, self.st_, self.sg
        dW = dG = None

        wrem = [A.alloc(KH, rows=32) for _ in range(4)]
        wimm = [A.alloc(KH, rows=32) for _ in range(4)]
        ns1m = [A.alloc(1, rows=32) for _ in range(4)]
        ns2m = [A.alloc(1, rows=32) for _ in range(4)]
        wef0 = A.alloc(KH, rows=32)
        wef1 = A.alloc(KH, rows=32)
        wef2 = A.alloc(KH, rows=32)
        ef32 = A.alloc(128, rows=32)
        rowv32 = A.alloc(1, rows=32)
        posb = A.alloc(L, rows=64)
        dsl28 = A.alloc(1, rows=28)
        asl4_t = A.alloc(1, rows=36)
        asl4 = asl4_t[32:36, 0:1]
        spl28 = A.alloc(1, rows=28)
        bias28 = A.alloc(1, rows=28)
        spla4_t = A.alloc(1, rows=36)
        spla4 = spla4_t[32:36, 0:1]
        der3 = A.alloc(3, rows=1)
        sct2 = A.alloc(D, rows=2)
        scT = A.alloc(16)
        modsb = A.alloc(6 * D, rows=2)
        tmp1 = A.alloc(1)
        wmb = [A.buf(512) for _ in range(3)]
        wqb = [A.buf(D) for _ in range(2)]
        qtmp = A.alloc(D)
        qtb = qtmp.bitcast(BF16)
        stage = A.buf(D, BF16)
        self.prol_end = A.off

        loads = [(self.cwt, self.cw_h[:, :]), (self.cbt, self.cb_h[:, :]),
                 (dsl28, self.dsl_h[:, :]), (asl4, self.asl_h[:, :]),
                 (self.scv, self.sc_h[:, :]), (der3, self.dlg_h[:, :]),
                 (self.biT, self.bi_h[:, :]),
                 (sct2, self.c_h[:, :])]
        for m in range(4):
            loads += [(wrem[m], self.wre_h.rearrange("(h m) j -> h m j", m=4)[:, m, :]),
                      (wimm[m], self.wim_h.rearrange("(h m) j -> h m j", m=4)[:, m, :]),
                      (ns1m[m], self.ns_h.rearrange("(h m) c -> h m c", m=4)[:, m, 0:1]),
                      (ns2m[m], self.ns_h.rearrange("(h m) c -> h m c", m=4)[:, m, 1:2])]
        tok_loads = [dma("sync", None, dst, src, slow=True) for dst, src in loads]
        t_th = dma("gpsimd", None, self.bc_h[9:10, 0:2], self.th_h[:, :])
        waitf("gpsimd", t_th)
        t_g2 = dma("gpsimd", None, self.g2v, self.bc_h[9:10, 0:1].partition_broadcast(128))

        em("gpsimd", lambda e: e.iota(self.iof1, pattern=[[1, 128]], base=0,
                                      channel_multiplier=0, allow_small_or_imprecise_dtypes=True))
        em("gpsimd", lambda e: e.iota(self.iop, pattern=[[0, 1]], base=0,
                                      channel_multiplier=1, allow_small_or_imprecise_dtypes=True))
        em("gpsimd", lambda e: e.iota(ef32, pattern=[[1, 128]], base=0,
                                      channel_multiplier=0, allow_small_or_imprecise_dtypes=True))
        em("gpsimd", lambda e: e.iota(rowv32, pattern=[[0, 1]], base=0,
                                      channel_multiplier=1, allow_small_or_imprecise_dtypes=True))
        em("gpsimd", lambda e: e.iota(posb, pattern=[[1, L]], base=0,
                                      channel_multiplier=0, allow_small_or_imprecise_dtypes=True),
           inc=sg)
        v_iota = sg.n

        wait("vector", sg, v_iota)
        TS = ALU
        for val, t in self.consts.items():
            em("vector", lambda e, t=t, val=val: e.memset(t, float(val)))
        em("vector", lambda e: e.tensor_scalar(out=self.ident, in0=self.iof1, scalar1=self.iop,
                                               scalar2=None, op0=TS.is_equal, op1=TS.bypass))
        em("vector", lambda e: e.tensor_copy(out=self.identb, in_=self.ident))
        em("vector", lambda e: e.memset(self.zrow, 0.0))
        em("vector", lambda e: e.memset(self.onesr, 1.0))
        em("vector", lambda e: e.memset(self.amaxh, 0.0))
        em("vector", lambda e: e.memset(self.amaxd, 0.0))
        em("vector", lambda e: e.tensor_scalar(out=self.k4v, in0=self.iop, scalar1=1.0 / 32,
                                               scalar2=-0.484375, op0=TS.mult, op1=TS.add))
        em("vector", lambda e: e.drain())
        em("vector", lambda e: e.tensor_scalar(out=self.k4v, in0=self.k4v, scalar1=RC,
                                               scalar2=RC, op0=TS.add, op1=TS.subtract))
        em("vector", lambda e: e.drain())
        for g in range(8):
            em("vector", lambda e, g=g: e.tensor_scalar(out=tmp1, in0=self.k4v, scalar1=1.0,
                                                        scalar2=float(4 * g), op0=TS.mult, op1=TS.add))
            em("vector", lambda e: e.drain())
            em("vector", lambda e, g=g: e.tensor_scalar(out=self.selg[g], in0=self.iof1[:, 0:KH],
                                                        scalar1=tmp1, scalar2=None,
                                                        op0=TS.is_equal, op1=TS.bypass))
        em("vector", lambda e: e.tensor_scalar(out=ef32, in0=ef32, scalar1=1.0 / 32,
                                               scalar2=-0.484375, op0=TS.mult, op1=TS.add))
        em("vector", lambda e: e.drain())
        em("vector", lambda e: e.tensor_scalar(out=ef32, in0=ef32, scalar1=RC,
                                               scalar2=RC, op0=TS.add, op1=TS.subtract))
        em("vector", lambda e: e.drain())
        for g in range(8):
            em("vector", lambda e, g=g: e.tensor_scalar(out=self.Ee[g], in0=ef32, scalar1=1.0,
                                                        scalar2=float(4 * g), op0=TS.mult, op1=TS.add))
            em("vector", lambda e: e.drain())
            em("vector", lambda e, g=g: e.tensor_scalar(out=self.Ee[g], in0=self.Ee[g], scalar1=rowv32,
                                                        scalar2=None, op0=TS.is_equal, op1=TS.bypass))
        waitf("vector", t_g2)
        em("vector", lambda e: e.tensor_scalar(out=self.g2h, in0=self.g2v, scalar1=0.5,
                                               scalar2=None, op0=TS.mult, op1=TS.bypass))

        for t_ in tok_loads:
            waitf("vector", t_)
        place_toks = []
        for X, (wsrc, nssrc, ma, mb, sub) in enumerate([
                (wrem, ns1m, 0, 3, False), (wrem, ns1m, 1, 2, False),
                (wimm, ns2m, 2, 1, True), (wimm, ns2m, 3, 0, True)]):
            for t_ in place_toks:
                waitf("vector", t_)
            place_toks = []
            em("vector", lambda e, w=wsrc[ma], n=nssrc[ma]: e.tensor_scalar(
                out=wef0, in0=w, scalar1=n, scalar2=None, op0=TS.mult, op1=TS.bypass))
            em("vector", lambda e: e.drain())
            em("vector", lambda e, w=wsrc[mb], n=nssrc[mb]: e.tensor_scalar(
                out=wef1, in0=w, scalar1=n, scalar2=None, op0=TS.mult, op1=TS.bypass))
            em("vector", lambda e: e.drain())
            op = TS.subtract if sub else TS.add
            em("vector", lambda e, op=op: e.tensor_tensor(out=wef2, in0=wef0, in1=wef1,
                                                          op=op))
            v = em("vector", lambda e, X=X: e.memset(self.wdiag[X], 0.0), inc=sv)
            wait("gpsimd", sv, v)
            for r in range(4):
                place_toks.append(dma("gpsimd", None,
                                      self.wdiag[X][32 * r:32 * r + 32, 32 * r:32 * r + 32], wef2))

        for t_ in tok_loads:
            waitf("scalar", t_)
        wait("scalar", sg, v_iota)
        em("scalar", lambda e: e.activation(out=spl28, in_=dsl28, func=AF.Exp))
        em("scalar", lambda e: e.activation(out=spl28, in_=spl28, func=AF.Ln, bias=1.0))
        em("scalar", lambda e: e.activation(out=spla4, in_=asl4, func=AF.Exp))
        v = em("scalar", lambda e: e.activation(out=spla4, in_=spla4, func=AF.Ln, bias=1.0), inc=ss)
        wait("vector", ss, v)
        em("vector", lambda e: e.tensor_scalar(out=bias28, in0=spl28, scalar1=float(-(L - 1)),
                                               scalar2=None, op0=TS.mult, op1=TS.bypass))
        v = em("vector", lambda e: e.tensor_scalar(out=spla4, in0=spla4, scalar1=-1.0,
                                                   scalar2=None, op0=TS.mult, op1=TS.bypass), inc=sv)
        wait("scalar", sv, v)
        em("scalar", lambda e: e.activation(out=posb[0:28, :], in_=posb[0:28, :], func=AF.Exp,
                                            scale=spl28, bias=bias28))
        v = em("scalar", lambda e: e.activation(out=posb[32:36, :], in_=posb[32:36, :], func=AF.Exp,
                                                scale=spla4), inc=ss)
        wait("gpsimd", ss, v)
        dma("gpsimd", None, self.tw_h[0:28, :], posb[0:28, :])
        dma("gpsimd", None, self.tw_h[28:32, :], posb[32:36, :])

        der128 = A.alloc(4)
        der2 = A.alloc(4)
        t_der = dma("gpsimd", None, der128[:, 0:3], self.dlg_h[0:1, :].partition_broadcast(128))
        waitf("scalar", t_der)
        v = em("scalar", lambda e: e.activation(out=der128[:, 0:3], in_=der128[:, 0:3],
                                                func=AF.Exp), inc=ss)
        wait("vector", ss, v)
        em("vector", lambda e: e.tensor_reduce(out=der128[:, 3:4], in_=der128[:, 0:3],
                                               axis=AX.X, op=TS.add))
        em("vector", lambda e: e.drain())
        em("vector", lambda e: e.reciprocal(out=der128[:, 3:4], in_=der128[:, 3:4]))
        em("vector", lambda e: e.drain())
        em("vector", lambda e: e.tensor_scalar(out=self.w0v, in0=der128[:, 0:1],
                                               scalar1=der128[:, 3:4], scalar2=None,
                                               op0=TS.mult, op1=TS.bypass))
        em("vector", lambda e: e.tensor_scalar(out=self.w1v, in0=der128[:, 1:2],
                                               scalar1=der128[:, 3:4], scalar2=None,
                                               op0=TS.mult, op1=TS.bypass))
        em("vector", lambda e: e.tensor_scalar(out=der2[:, 0:1], in0=der128[:, 3:4],
                                               scalar1=-1.0, scalar2=None,
                                               op0=TS.mult, op1=TS.bypass))
        em("vector", lambda e: e.drain())
        em("vector", lambda e: e.tensor_scalar(out=self.w2nv, in0=der128[:, 2:3],
                                               scalar1=der2[:, 0:1], scalar2=None,
                                               op0=TS.mult, op1=TS.bypass))
        em("vector", lambda e: e.drain())
        self.dbg("softrow", der128)
        self.dbg("w2n_early", self.w2nv)

        for t_ in tok_loads:
            waitf("scalar", t_)
        v = em("scalar", lambda e: e.activation(out=sct2, in_=sct2, func=AF.Silu), inc=ss)
        wait("tensor", ss, v)
        wait("tensor", sv, sv.n)
        b0 = self.bank(0, "tensor")
        for j in range(8):
            em("tensor", lambda e, j=j, b0=b0: e.transpose(b0[0:128, 2 * j:2 * j + 2],
                                                           sct2[0:2, 128 * j:128 * j + 128],
                                                           self.ident[0:2, 0:2]),
               inc=(st_ if j == 7 else None))
        wait("scalar", st_, st_.n)
        v = em("scalar", lambda e, b0=b0: e.copy(out=scT, in_=b0[0:128, 0:16]), inc=ss)
        self.bank_consumed(0, ss, v)
        v_scT = ss.n
        self.dbg("scT", scT)
        modsbT = A.alloc(96)
        mview = modsbT.rearrange("p (c b) -> p c b", b=2)
        for cg in range(12):
            bis = [1 + q for q in range(4)]
            bks = [self.bank(bi, "tensor") for bi in bis]
            for k in range(8):
                bu = wmb[(cg * 8 + k) % 3]
                self.waitf("sync", bu.free)
                t_ = dma("sync", None, bu.v,
                         self.wm_h[128 * k:128 * k + 128, 512 * cg:512 * cg + 512])
                waitf("tensor", t_)
                if cg == 0 and k == 0:
                    wait("tensor", ss, v_scT)
                for q in range(4):
                    v = em("tensor", lambda e, bk=bks[q], k=k, bu=bu, q=q: e.matmul(
                        bk[0:128, 0:2], bu.v[:, 128 * q:128 * q + 128], scT[:, 2 * k:2 * k + 2],
                        start=(k == 0), stop=(k == 7)),
                        inc=(st_ if q == 3 else None))
                bu.free = (st_, st_.n)
            wait("scalar", st_, st_.n)
            for q in range(4):
                ctblk = 4 * cg + q
                v = em("scalar", lambda e, bk=bks[q], ctblk=ctblk: e.copy(
                    out=modsbT[:, 2 * ctblk:2 * ctblk + 2], in_=bk[0:128, 0:2]), inc=ss)
                self.bank_consumed(bis[q], ss, v)
        v_mod = ss.n
        self.dbg("modsbT", modsbT)

        wait("vector", ss, v_mod)
        t_gprev = None
        gacol = A.alloc(8)
        gacol2 = A.alloc(8)
        gacol3 = A.alloc(8)
        gacol4 = A.alloc(8)
        gacol5 = A.alloc(8)
        for b in range(2):
            for (dst, blk0, plus1) in ((self.cav[b], 8, True), (self.cbv[b], 32, True),
                                       (self.sav[b], 0, False), (self.sbv[b], 24, False)):
                em("vector", lambda e, dst=dst, blk0=blk0, p1=plus1, b=b: e.tensor_scalar(
                    out=dst, in0=mview[:, blk0:blk0 + 8, b],
                    scalar1=(1.0 if p1 else 0.0), scalar2=None, op0=TS.add, op1=TS.bypass))
            for (blk0, row) in ((16, b), (40, 2 + b)):
                em("vector", lambda e, b=b, blk0=blk0: e.tensor_scalar(
                    out=gacol2, in0=mview[:, blk0:blk0 + 8, b], scalar1=0.0,
                    scalar2=None, op0=TS.is_ge, op1=TS.bypass))
                em("vector", lambda e: e.drain())
                em("vector", lambda e: e.tensor_scalar(
                    out=gacol, in0=gacol2, scalar1=2.0, scalar2=-1.0,
                    op0=TS.mult, op1=TS.add))
                v = em("vector", lambda e: e.drain(), inc=sv)
                wait("scalar", sv, v)
                v = em("scalar", lambda e, b=b, blk0=blk0: e.activation(
                    out=gacol5, in_=mview[:, blk0:blk0 + 8, b], func=AF.Abs), inc=ss)
                wait("vector", ss, v)
                em("vector", lambda e: e.tensor_scalar(
                    out=gacol4, in0=gacol5, scalar1=0.25, scalar2=None,
                    op0=TS.max, op1=TS.bypass))
                em("vector", lambda e: e.drain())
                waitf("vector", t_gprev)
                em("vector", lambda e: e.tensor_tensor(
                    out=gacol3, in0=gacol, in1=gacol4, op=TS.mult))
                v = em("vector", lambda e: e.drain(), inc=sv)
                wait("gpsimd", sv, v)
                t_gprev = dma("gpsimd", None, self.bc_h[row:row + 1, :].rearrange(
                    "o (j p) -> o p j", p=128), gacol3, slow=True)

        for (w_h, out_h, svec, do_T) in ((self.wiT_h, self.wqi_h, self.swv, True),
                                         (self.wo_h, self.wqo_h, self.swov, False)):
            for s in range(32):
                bu = wqb[s % 2]
                self.waitf("sync", bu.free)
                t_ = dma("sync", None, bu.v, w_h[128 * s:128 * s + 128, :])
                waitf("vector", t_)
                waitf("vector", stage.free)
                em("vector", lambda e, bu=bu: e.tensor_reduce(
                    out=self.amt, in_=bu.v, axis=AX.X, op=TS.max, apply_absolute_value=True))
                em("vector", lambda e: e.drain())
                em("vector", lambda e: e.tensor_scalar(
                    out=self.amt2, in0=self.amt, scalar1=1e-8, scalar2=None, op0=TS.max, op1=TS.bypass))
                em("vector", lambda e: e.drain())
                em("vector", lambda e: e.reciprocal(out=self.amt3, in_=self.amt2))
                em("vector", lambda e: e.drain())
                em("vector", lambda e, svec=svec, s=s: e.tensor_scalar(
                    out=svec[:, s:s + 1], in0=self.amt3, scalar1=127.0, scalar2=None,
                    op0=TS.mult, op1=TS.bypass))
                em("vector", lambda e: e.drain())
                if s == 0 and do_T:
                    self.dbg("wslab0", bu.v)
                    self.dbg("amt0", self.amt3)
                em("vector", lambda e, bu=bu, svec=svec, s=s: e.tensor_scalar(
                    out=qtmp, in0=bu.v, scalar1=svec[:, s:s + 1], scalar2=RC,
                    op0=TS.mult, op1=TS.add))
                em("vector", lambda e: e.drain())
                em("vector", lambda e: e.tensor_scalar(
                    out=stage.v, in0=qtmp, scalar1=RC, scalar2=None,
                    op0=TS.subtract, op1=TS.bypass))
                v = em("vector", lambda e: e.drain(), inc=sv)
                bu.free = (sv, v)
                if not do_T:
                    wait("gpsimd", sv, v)
                    if s == 0:
                        self.dbg("woslab0", bu.v)
                        self.dbg("wostage0", stage.v)
                        self.dbg("wosvec", svec)
                    stage.free = dma("gpsimd", None, out_h[128 * s:128 * s + 128, :], stage.v)
                else:
                    wait("tensor", sv, v)
                    bk = self.bank(3, "tensor")
                    bkb = self.psum[3][:, :].bitcast(BF16)
                    for j in range(8):
                        em("tensor", lambda e, j=j, bkb=bkb: e.transpose(
                            bkb[:, 128 * j:128 * j + 128], stage.v[:, 128 * j:128 * j + 128],
                            self.identb), inc=(st_ if j == 7 else None))
                    vT = st_.n
                    stage.free = (st_, vT)
                    wait("scalar", st_, vT)
                    wait("scalar", sv, sv.n)
                    v2 = em("scalar", lambda e, bkb=bkb: e.copy(out=qtb[:, 0:1024], in_=bkb),
                            inc=ss)
                    self.bank_consumed(3, ss, v2)
                    wait("gpsimd", ss, v2)
                    qtoks = [dma("gpsimd", None,
                                 out_h[128 * j:128 * j + 128, 128 * s:128 * s + 128],
                                 qtb[:, 128 * j:128 * j + 128]) for j in range(8)]
                    for t_ in qtoks:
                        waitf("vector", t_)

        for (w_h, ncol, hi, lo) in ((self.wA_h, TOTAL, self.WAhi, self.WAlo),
                                    (self.wO_h, D, self.WOhi, self.WOlo)):
            for k in range(8):
                parts = [(o, min(1024, ncol - o)) for o in range(0, ncol, 1024)]
                for (o0, w) in parts:
                    bu = wqb[(k + o0) % 2]
                    self.waitf("sync", bu.free)
                    t_ = dma("sync", None, bu.v[:, 0:w], w_h[128 * k:128 * k + 128, o0:o0 + w])
                    waitf("scalar", t_)
                    vhi = em("scalar", lambda e, bu=bu, hi=hi, c0=ncol * k + o0, w=w: e.copy(
                        out=hi[:, c0:c0 + w], in_=bu.v[:, 0:w]), inc=ss)
                    wait("vector", ss, vhi)
                    v = em("vector", lambda e, bu=bu, hi=hi, lo=lo, c0=ncol * k + o0, w=w:
                           e.tensor_tensor(out=lo[:, c0:c0 + w], in0=bu.v[:, 0:w],
                                           in1=hi[:, c0:c0 + w], op=TS.subtract), inc=sv)
                    bu.free = (sv, v)

        self.dbg("cav0", self.cav[0])
        self.dbg("sav0", self.sav[0])
        self.dbg("cbv0", self.cbv[0])
        self.dbg("sbv0", self.sbv[0])
        self.dbg("swv", self.swv)
        self.dbg("swov", self.swov)
        self.dbg("w012", self.w0v)
        self.dbg("w1", self.w1v)
        self.dbg("w2n", self.w2nv)
        self.dbg("g2", self.g2v)
        self.dbg("wdiag0", self.wdiag[0])
        self.dbg("selg1", self.selg[1])
        self.dbg("Ee1", self.Ee[1])
        self.dbg("WAhi", self.WAhi[:, 0:TOTAL])
        self.dbg("WAlo", self.WAlo[:, 0:TOTAL])
        self.barrier()

    def alloc_A(self):
        A = Arena(self.arena_t.ap(), self.AR, "A", self.resA_end)
        TA = self.TA
        self.x_tm = [A.buf(D) for _ in range(2)]
        self.t_tm = [A.buf(D) for _ in range(2)]
        self.t2_tm = A.buf(D)
        self.t_fm = A.buf(8 * TA)
        self.t2fm = self.t_fm
        self.sq_scr = A.alloc(D)
        self.lnrow = A.alloc(12)
        self.twc = A.buf(TA, rows=KH)
        self.ha_hi = A.buf(8 * TA, BF16)
        self.ha_lo = A.buf(8 * TA, BF16)
        self.ha_t = A.alloc(TA)
        self.haf = A.buf(8 * TA)
        self.zx = [A.buf(TA + 3) for _ in range(3)]
        self.ctmp = [A.alloc(TA) for _ in range(3)]
        self.xv_u = [A.buf(TA) for _ in range(2)]
        self.uu = [A.buf(TA) for _ in range(2)]
        self.silu_g = A.buf(8 * TA)
        self.sraw_u = A.buf(TA, rows=KH)
        self.tmpP = [A.alloc(TA) for _ in range(2)]
        self.s1t = [A.buf(TA) for _ in range(2)]
        self.c1t = [A.buf(TA) for _ in range(2)]
        self.s3t = [A.buf(TA) for _ in range(2)]
        self.c3t = [A.buf(TA) for _ in range(2)]
        self.p1t = A.alloc(TA)
        self.polyt = [A.buf(TA) for _ in range(2)]
        self.ppt = [A.buf(TA) for _ in range(2)]
        self.scin = [[A.buf(TA) for _ in range(4)] for _ in range(2)]
        self.scout = [[A.buf(TA) for _ in range(4)] for _ in range(2)]
        self.sqt = [A.buf(TA) for _ in range(2)]
        self.pwt = A.buf(TA, rows=KH)
        self.dent = A.buf(TA, rows=KH)
        self.g32 = [A.alloc(TA, rows=KH) for _ in range(3)]
        self.gfin = A.buf(TA, rows=KH)
        self.pwE_sb = [A.buf(TA) for _ in range(2)]
        self.y_raw = A.buf(8 * TA)
        self.gft = [A.alloc(TA) for _ in range(2)]
        self.g_hi = A.buf(8 * TA, BF16)
        self.g_lo = A.buf(8 * TA, BF16)
        self.xres_tm = [A.buf(D) for _ in range(2)]
        self.hbt = A.buf(TA)
        assert A.off <= self.AR, A.off
        self.A_end = A.off

    def _inproj_conv(self, b, ct, dst_kind, par):
        TA = self.TA
        em, wait, dma, waitf = self.em, self.wait, self.dma, self.waitf
        sv, ss, st_ = self.sv, self.ss, self.st_
        TS = ALU
        w = _rows(ct)
        bi = 1 + (ct % 2)
        bk = self.bank(bi, "tensor")
        for k in range(8):
            sW = slice(TOTAL * k + 128 * ct, TOTAL * k + 128 * ct + w)
            sA = slice(TA * k, TA * (k + 1))
            em("tensor", lambda e, bk=bk, sW=sW, sA=sA, w=w, k=k: e.matmul(
                bk[0:w, 0:TA], self.WAhi[:, sW], self.ha_hi.v[:, sA],
                start=(k == 0), stop=False))
            em("tensor", lambda e, bk=bk, sW=sW, sA=sA, w=w: e.matmul(
                bk[0:w, 0:TA], self.WAhi[:, sW], self.ha_lo.v[:, sA],
                start=False, stop=False))
            em("tensor", lambda e, bk=bk, sW=sW, sA=sA, w=w, k=k: e.matmul(
                bk[0:w, 0:TA], self.WAlo[:, sW], self.ha_hi.v[:, sA],
                start=False, stop=(k == 7)), inc=(st_ if k == 7 else None))
        vmm = st_.n
        zb = self.zx[ct % 3]
        self.waitf("scalar", zb.free)
        wait("scalar", st_, vmm)
        vz = em("scalar", lambda e, zb=zb, bk=bk, w=w: e.copy(
            out=zb.v[0:w, 3:3 + TA], in_=bk[0:w, 0:TA]), inc=ss)
        self.bank_consumed(bi, ss, vz)
        if getattr(self, "_tap16", False) and ct == 16:
            self.dbg("z16", zb.v[0:32, :])
        wait("vector", ss, vz)
        em("vector", lambda e, zb=zb, ct=ct, w=w: e.tensor_copy(
            out=zb.v[0:w, 0:3], in_=self.tails[0:w, 3 * ct:3 * ct + 3]))
        c_a, c_b = self.ctmp[0], self.ctmp[1]
        em("vector", lambda e, zb=zb, ct=ct, w=w, c_a=c_a: e.tensor_scalar(
            out=c_a[0:w, :], in0=zb.v[0:w, 0:TA], scalar1=self.cwt[0:w, ct:ct + 1],
            scalar2=None, op0=TS.mult, op1=TS.bypass))
        em("vector", lambda e, zb=zb, ct=ct, w=w, c_a=c_a, c_b=c_b: e.scalar_tensor_tensor(
            out=c_b[0:w, :], in0=zb.v[0:w, 1:1 + TA], scalar=self.cwt[0:w, NCT + ct:NCT + ct + 1],
            in1=c_a[0:w, :], op0=TS.mult, op1=TS.add))
        em("vector", lambda e, zb=zb, ct=ct, w=w, c_a=c_a, c_b=c_b: e.scalar_tensor_tensor(
            out=c_a[0:w, :], in0=zb.v[0:w, 2:2 + TA], scalar=self.cwt[0:w, 2 * NCT + ct:2 * NCT + ct + 1],
            in1=c_b[0:w, :], op0=TS.mult, op1=TS.add))
        lastw = self.cwt[0:w, 3 * NCT + ct:3 * NCT + ct + 1]
        if dst_kind == "xv":
            dst = self.xv_u[par]
            self.waitf("vector", dst.free)
            vconv = em("vector", lambda e, zb=zb, w=w, c_a=c_a, dst=dst, lastw=lastw:
                       e.scalar_tensor_tensor(out=dst.v[0:w, :], in0=zb.v[0:w, 3:3 + TA],
                                              scalar=lastw, in1=c_a[0:w, :],
                                              op0=TS.mult, op1=TS.add), inc=sv)
            dst.ready = (sv, vconv)
        elif dst_kind == "gate":
            dst = self.ctmp[2]
            self.waitf("vector", getattr(self, "ctmp2_free", None))
            vconv = em("vector", lambda e, zb=zb, w=w, c_a=c_a, dst=dst, lastw=lastw:
                       e.scalar_tensor_tensor(out=dst[0:w, :], in0=zb.v[0:w, 3:3 + TA],
                                              scalar=lastw, in1=c_a[0:w, :],
                                              op0=TS.mult, op1=TS.add), inc=sv)
            wait("scalar", sv, vconv)
            self.waitf("scalar", self.silu_g.free)
            vsl = em("scalar", lambda e, ct=ct, dst=dst: e.activation(
                out=self.silu_g.v[:, TA * (ct - 8):TA * (ct - 7)], in_=dst,
                func=AF.Silu, bias=self.cbt[:, ct:ct + 1]), inc=ss)
            self.ctmp2_free = (ss, vsl)
            dst = None
        else:
            dst = self.sraw_u
            self.waitf("vector", dst.free)
            vconv = em("vector", lambda e, zb=zb, w=w, c_a=c_a, dst=dst, lastw=lastw:
                       e.scalar_tensor_tensor(out=dst.v, in0=zb.v[0:w, 3:3 + TA],
                                              scalar=lastw, in1=c_a[0:w, :],
                                              op0=TS.mult, op1=TS.add), inc=sv)
            dst.ready = (sv, vconv)
        vtl = em("vector", lambda e, zb=zb, ct=ct, w=w: e.tensor_copy(
            out=self.tails[0:w, 3 * ct:3 * ct + 3], in_=zb.v[0:w, TA:TA + 3]), inc=sv)
        zb.free = (sv, vtl)
        return dst

    def a_chunk(self, b, ci, itile):
        TA = self.TA
        t0 = ci * TA
        em, wait, dma, waitf = self.em, self.wait, self.dma, self.waitf
        sv, ss, st_, sg = self.sv, self.ss, self.st_, self.sg
        TS = ALU

        if ci == 0:
            em("vector", lambda e: e.memset(self.tails, 0.0))
            em("vector", lambda e: e.memset(self.carry, 0.0))
            wait("gpsimd", sv, sv.n)
            self.t_gaB = dma("gpsimd", None, self.gaB,
                             self.bc_h[b:b + 1, :].partition_broadcast(128))
        t_gaB = self.t_gaB

        xb = self.x_tm[ci % 2]
        self.waitf("sync", xb.free)
        t_x = dma("sync", None, xb.v, self.x_h[b, t0:t0 + TA, :])
        self.waitf("sync", self.twc.free)
        t_twc = dma("sync", None, self.twc.v[0:KH, :], self.tw_h[:, t0:t0 + TA])

        waitf("scalar", t_x)
        self.waitf("scalar", getattr(self, "lnrow_free", None))
        v = em("scalar", lambda e, xb=xb: e.activation(out=self.sq_scr, in_=xb.v,
                                                       func=AF.Square), inc=ss)
        waitf("vector", t_x)
        wait("vector", ss, v)
        em("vector", lambda e: e.tensor_reduce(out=self.lnrow[:, 1:2], in_=self.sq_scr,
                                               axis=AX.X, op=TS.add))
        em("vector", lambda e: e.drain())
        em("vector", lambda e, xb=xb: e.tensor_reduce(out=self.lnrow[:, 0:1], in_=xb.v,
                                                      axis=AX.X, op=TS.add))
        em("vector", lambda e: e.drain())
        em("vector", lambda e: e.tensor_scalar(out=self.lnrow[:, 0:1], in0=self.lnrow[:, 0:1],
                                               scalar1=1.0 / D, scalar2=None, op0=TS.mult, op1=TS.bypass))
        em("vector", lambda e: e.drain())
        em("vector", lambda e: e.tensor_tensor(out=self.lnrow[:, 2:3], in0=self.lnrow[:, 0:1],
                                               in1=self.lnrow[:, 0:1], op=TS.mult))
        em("vector", lambda e: e.drain())
        v = em("vector", lambda e: e.scalar_tensor_tensor(out=self.lnrow[:, 3:4], in0=self.lnrow[:, 1:2],
                                                          scalar=1.0 / D, in1=self.lnrow[:, 2:3],
                                                          op0=TS.mult, op1=TS.subtract), inc=sv)
        wait("scalar", sv, v)
        em("scalar", lambda e: e.activation(out=self.lnrow[:, 4:5], in_=self.lnrow[:, 3:4],
                                            func=AF.Ln, bias=1e-5))
        v = em("scalar", lambda e: e.activation(out=self.lnrow[:, 4:5], in_=self.lnrow[:, 4:5],
                                                func=AF.Exp, scale=-0.5), inc=ss)
        tb = self.t_tm[ci % 2]
        self.waitf("vector", tb.free)
        wait("vector", ss, v)
        if b == 0 and ci == 0:
            self.dbg("lnrow", self.lnrow)
            self.dbg("sqscr", self.sq_scr)
            self.dbg("xb0", xb.v)
        vt = em("vector", lambda e, xb=xb, tb=tb: e.tensor_scalar(
            out=tb.v, in0=xb.v, scalar1=self.lnrow[:, 0:1], scalar2=self.lnrow[:, 4:5],
            op0=TS.subtract, op1=TS.mult), inc=sv)

        wait("tensor", sv, vt)
        self.waitf("tensor", self.t_fm.free)
        for r in range(2):
            bk = self.bank(0, "tensor")
            for j in range(4):
                jj = 4 * r + j
                em("tensor", lambda e, bk=bk, j=j, jj=jj, tb=tb: e.transpose(
                    bk[:, 128 * j:128 * j + 128], tb.v[:, 128 * jj:128 * jj + 128], self.ident),
                   inc=(st_ if j == 3 else None))
            wait("scalar", st_, st_.n)
            v = em("scalar", lambda e, bk=bk, r=r: e.copy(
                out=self.t_fm.v[:, 512 * r:512 * r + 512], in_=bk[:, :]), inc=ss)
            self.bank_consumed(0, ss, v)
        tb.free = (self.st_, self.st_.n)
        v_tfm = ss.n
        if b == 0 and ci == 0:
            self.dbg("tfm", self.t_fm.v)

        wait("vector", ss, v_tfm)
        self.waitf("vector", self.ha_hi.free)
        self.waitf("vector", self.ha_lo.free)
        self.waitf("vector", self.haf.free)
        for j in range(8):
            sl = slice(TA * j, TA * (j + 1))
            em("vector", lambda e, j=j, sl=sl: e.tensor_scalar(
                out=self.haf.v[:, sl], in0=self.t_fm.v[:, sl], scalar1=self.cav[b][:, j:j + 1],
                scalar2=self.sav[b][:, j:j + 1], op0=TS.mult, op1=TS.add))
        vf = em("vector", lambda e: e.drain(),
                inc=sv)
        wait("scalar", sv, vf)
        vhi = em("scalar", lambda e: e.copy(out=self.ha_hi.v, in_=self.haf.v), inc=ss)
        wait("vector", ss, vhi)
        vha = em("vector", lambda e: e.tensor_tensor(
            out=self.ha_lo.v, in0=self.haf.v, in1=self.ha_hi.v, op=TS.subtract), inc=sv)
        self.haf.free = (sv, vha)
        self.t_fm.free = (sv, vf)
        wait("tensor", sv, vha)
        wait("tensor", ss, vhi)

        self._tap16 = (b == 0 and ci == 0)
        self._inproj_conv(b, 16, "sraw", 0)
        self._tap16 = False
        em("vector", lambda e: e.tensor_scalar(
            out=self.g32[0][0:KH, :], in0=self.sraw_u.v[0:KH, :],
            scalar1=self.cbt[0:KH, 16:17], scalar2=self.scv,
            op0=TS.add, op1=TS.mult))
        v = em("vector", lambda e: e.tensor_scalar(
            out=self.g32[0][0:KH, :], in0=self.g32[0][0:KH, :], scalar1=20.0, scalar2=-20.0,
            op0=TS.min, op1=TS.max), inc=sv)
        self.sraw_u.free = (sv, v)
        wait("scalar", sv, v)
        self.waitf("scalar", self.pwt.free)
        v = em("scalar", lambda e: e.activation(
            out=self.pwt.v[0:KH, :], in_=self.g32[0][0:KH, :], func=AF.Exp), inc=ss)
        wait("vector", ss, v)
        waitf("vector", t_twc)
        em("vector", lambda e: e.tensor_tensor(
            out=self.pwt.v[0:KH, :], in0=self.pwt.v[0:KH, :],
            in1=self.twc.v[0:KH, :], op=TS.mult))
        self.waitf("vector", self.dent.free)
        em("vector", lambda e: e.tensor_tensor_scan(
            self.dent.v[0:KH, :], self.zrow[0:KH, 0:TA], self.pwt.v[0:KH, :],
            self.carry[0:KH, 32:33], TS.add, TS.add))
        em("vector", lambda e: e.drain())
        v_pw = em("vector", lambda e: e.tensor_copy(
            out=self.carry[0:KH, 32:33], in_=self.dent.v[0:KH, TA - 1:TA]), inc=sv)
        self.twc.free = (sv, v_pw)
        if b == 0 and ci == 0:
            self.dbg("sraw", self.sraw_u.v[0:KH, :])
            self.dbg("pwt", self.pwt.v[0:KH, :])
            self.dbg("dent", self.dent.v[0:KH, :])

        self.bank(3, "tensor")
        gstate = {}

        def stage1a(g):
            par = g % 2
            st = {}
            ub = self._inproj_conv(b, g, "xv", par)
            aux = 4 + par
            bk = self.bank(aux, "tensor")
            wait("tensor", sv, v_pw)
            v = em("tensor", lambda e, bk=bk, g=g: e.matmul(
                bk[:, 0:TA], self.Ee[g], self.pwt.v[0:KH, :], start=True, stop=True), inc=st_)
            st["last_pwE"] = v
            wait("scalar", st_, v)
            pwb = self.pwE_sb[par]
            self.waitf("scalar", pwb.free)
            v = em("scalar", lambda e, bk=bk, pwb=pwb: e.copy(out=pwb.v, in_=bk[:, 0:TA]), inc=ss)
            self.bank_consumed(aux, ss, v)
            st["v_pwE"] = ss.n
            st["pwb"] = pwb
            uub = self.uu[par]
            self.waitf("vector", uub.free)
            wait("vector", sv, ub.ready[1])
            vu = em("vector", lambda e, g=g, ub=ub, uub=uub: e.tensor_scalar(
                out=uub.v, in0=ub.v, scalar1=self.cbt[:, g:g + 1], scalar2=None,
                op0=TS.add, op1=TS.bypass), inc=sv)
            ub.free = (sv, vu)
            st["uub"] = uub
            bufs = (self.s1t[par], self.c1t[par], self.s3t[par], self.c3t[par],
                    self.ppt[par], self.polyt[par])
            st["bufs"] = bufs
            s1b = bufs[0]
            for bb in bufs:
                self.waitf("scalar", bb.free)
            wait("scalar", sv, vu)
            em("scalar", lambda e, uub=uub: e.activation(out=self.tmpP[0], in_=uub.v,
                                                         func=AF.Sin, scale=self.g2h))
            em("scalar", lambda e, uub=uub, s1b=s1b: e.activation(
                out=s1b.v, in_=uub.v, func=AF.Sin, scale=self.g2v))
            v = em("scalar", lambda e: e.activation(out=self.tmpP[0], in_=self.tmpP[0],
                                                    func=AF.Square), inc=ss)
            st["v_sins"] = v
            gstate[g] = st

        def stage1b(g):
            par = g % 2
            st = gstate[g]
            s1b, c1b, s3b, c3b, ppb, pob = st["bufs"]
            uub, pwb = st["uub"], st["pwb"]
            wait("vector", ss, st["v_sins"])
            vv = em("vector", lambda e, c1b=c1b: e.tensor_scalar(
                out=c1b.v, in0=self.tmpP[0], scalar1=-2.0, scalar2=1.0,
                op0=TS.mult, op1=TS.add), inc=sv)
            wait("scalar", sv, vv)
            em("scalar", lambda e, s1b=s1b: e.activation(out=self.tmpP[0], in_=s1b.v, func=AF.Square))
            v = em("scalar", lambda e, c1b=c1b: e.activation(out=self.tmpP[1], in_=c1b.v,
                                                             func=AF.Square), inc=ss)
            wait("vector", ss, v)
            em("vector", lambda e, s3b=s3b: e.tensor_scalar(
                out=s3b.v, in0=self.tmpP[0], scalar1=-4.0, scalar2=3.0, op0=TS.mult, op1=TS.add))
            em("vector", lambda e, s3b=s3b, s1b=s1b: e.tensor_tensor(
                out=s3b.v, in0=s3b.v, in1=s1b.v, op=TS.mult))
            em("vector", lambda e, c3b=c3b: e.tensor_scalar(
                out=c3b.v, in0=self.tmpP[1], scalar1=4.0, scalar2=-3.0, op0=TS.mult, op1=TS.add))
            v = em("vector", lambda e, c3b=c3b, c1b=c1b: e.tensor_tensor(
                out=c3b.v, in0=c3b.v, in1=c1b.v, op=TS.mult), inc=sv)
            wait("scalar", sv, v)
            v = em("scalar", lambda e, uub=uub: e.activation(out=self.tmpP[0], in_=uub.v,
                                                             func=AF.Square), inc=ss)
            em("vector", lambda e, uub=uub: e.tensor_scalar(
                out=self.p1t, in0=uub.v, scalar1=self.w1v, scalar2=self.w0v,
                op0=TS.mult, op1=TS.add))
            wait("vector", ss, v)
            em("vector", lambda e, pob=pob: e.scalar_tensor_tensor(
                out=pob.v, in0=self.tmpP[0], scalar=self.w2nv, in1=self.p1t,
                op0=TS.mult, op1=TS.add))
            wait("vector", ss, st["v_pwE"])
            em("vector", lambda e, pob=pob, ppb=ppb, pwb=pwb: e.tensor_tensor(
                out=ppb.v, in0=pob.v, in1=pwb.v, op=TS.mult))
            sins = self.scin[par]
            for X, src in enumerate((c3b, c1b, s1b, s3b)):
                self.waitf("vector", sins[X].free)
                em("vector", lambda e, ppb=ppb, src=src, d=sins[X]: e.tensor_tensor(
                    out=d.v, in0=ppb.v, in1=src.v, op=TS.mult))
            vprod = em("vector", lambda e, ppb=ppb: e.tensor_copy(
                out=ppb.v[:, 0:1], in_=ppb.v[:, 0:1]), inc=sv)
            for X in range(4):
                sins[X].ready = vprod
            for bb in st["bufs"]:
                bb.free = (sv, vprod)
            uub.free = (sv, vprod)
            pwb.free = (sv, vprod)

        def stage2(g):
            par = g % 2
            st = gstate[g]
            sins = self.scin[par]
            souts = self.scout[par]
            auxY = 4 + (g + 1) % 2
            bkY = self.bank(auxY, "tensor")
            for X in range(4):
                self.waitf("vector", souts[X].free)
                em("vector", lambda e, X=X, g=g, d=souts[X], s=sins[X]: e.tensor_tensor_scan(
                    d.v, self.zrow[:, 0:TA], s.v, self.carry[:, 4 * g + X:4 * g + X + 1],
                    TS.add, TS.add))
                em("vector", lambda e: e.drain())
                v = em("vector", lambda e, X=X, g=g, d=souts[X]: e.tensor_copy(
                    out=self.carry[:, 4 * g + X:4 * g + X + 1], in_=d.v[:, TA - 1:TA]), inc=sv)
                sins[X].free = (sv, v)
                souts[X].ready = (sv, v)
            for X in range(4):
                sb = self.sqt[X % 2]
                self.waitf("scalar", sb.free)
                wait("scalar", sv, souts[X].ready[1])
                v = em("scalar", lambda e, sb=sb, d=souts[X]: e.activation(
                    out=sb.v, in_=d.v, func=AF.Square), inc=ss)
                wait("tensor", ss, v)
                v = em("tensor", lambda e, sb=sb, g=g, X=X: e.matmul(
                    self.psum[3][0:KH, 0:TA], self.selg[g], sb.v,
                    start=(g == 0 and X == 0), stop=(g == 7 and X == 3)), inc=st_)
                sb.free = (st_, v)
                wait("tensor", sv, souts[X].ready[1])
                v = em("tensor", lambda e, d=souts[X], X=X, bkY=bkY: e.matmul(
                    bkY[:, 0:TA], self.wdiag[X], d.v, start=(X == 0), stop=(X == 3)),
                    inc=(st_ if X == 3 else None))
                souts[X].free = (st_, st_.n)
            vY = st_.n
            wait("scalar", st_, vY)
            v = em("scalar", lambda e, bkY=bkY, g=g: e.copy(
                out=self.y_raw.v[:, TA * g:TA * (g + 1)], in_=bkY[:, 0:TA]), inc=ss)
            self.bank_consumed(auxY, ss, v)
            if b == 0 and ci == 0 and g == 0:
                self.dbg("uu0", self.uu[0].v)
                self.dbg("s1_0", self.s1t[0].v)
                self.dbg("c3_0", self.c3t[0].v)
                self.dbg("pp0", self.ppt[0].v)
                self.dbg("scoutA0", self.scout[0][0].v)
                self.dbg("scoutD0", self.scout[0][3].v)
                self.dbg("pwE0", self.pwE_sb[0].v)

        for gg in range(9):
            if gg < 8:
                stage1a(gg)
            if gg > 0:
                stage2(gg - 1)
                self._inproj_conv(b, 7 + gg, "gate", 0)
            if gg < 8:
                stage1b(gg)
        v_S_done = st_.n
        v_y_done = ss.n
        self.pwt.free = (st_, gstate[7]["last_pwE"])
        self.ha_hi.free = (st_, st_.n)
        self.ha_lo.free = (st_, st_.n)
        v_gates = ss.n

        em("vector", lambda e: e.tensor_scalar(out=self.g32[1][0:KH, :], in0=self.dent.v[0:KH, :],
                                               scalar1=1e-4, scalar2=None, op0=TS.max, op1=TS.bypass))
        em("vector", lambda e: e.drain())
        v = em("vector", lambda e: e.reciprocal(out=self.g32[1][0:KH, :], in_=self.g32[1][0:KH, :]),
               inc=sv)
        em("vector", lambda e: e.drain())
        self.dent.free = (sv, v)
        em("vector", lambda e: e.tensor_tensor(out=self.g32[2][0:KH, :], in0=self.g32[1][0:KH, :],
                                               in1=self.g32[1][0:KH, :], op=TS.mult))
        wait("vector", st_, v_S_done)
        v = em("vector", lambda e: e.tensor_tensor(out=self.g32[0][0:KH, :],
                                                   in0=self.psum[3][0:KH, 0:TA],
                                                   in1=self.g32[2][0:KH, :], op=TS.mult), inc=sv)
        self.bank_consumed(3, sv, v)
        wait("scalar", sv, v)
        em("scalar", lambda e: e.activation(out=self.g32[0][0:KH, :], in_=self.g32[0][0:KH, :],
                                            func=AF.Ln, scale=1.0 / 128, bias=1e-5))
        v = em("scalar", lambda e: e.activation(out=self.g32[0][0:KH, :], in_=self.g32[0][0:KH, :],
                                                func=AF.Exp, scale=-0.5), inc=ss)
        self.waitf("vector", self.gfin.free)
        wait("vector", ss, v)
        vgf = em("vector", lambda e: e.tensor_tensor(out=self.gfin.v[0:KH, :], in0=self.g32[0][0:KH, :],
                                                     in1=self.g32[1][0:KH, :], op=TS.mult), inc=sv)
        if b == 0 and ci == 0:
            self.dbg("yraw", self.y_raw.v)
            self.dbg("gfin", self.gfin.v[0:KH, :])
            self.dbg("silug", self.silu_g.v)

        wait("tensor", sv, vgf)
        wait("vector", ss, max(v_y_done, v_gates))
        self.waitf("vector", self.g_hi.free)
        self.waitf("vector", self.g_lo.free)
        for g in range(8):
            aux = 4 + g % 2
            bk = self.bank(aux, "tensor")
            v = em("tensor", lambda e, bk=bk, g=g: e.matmul(
                bk[:, 0:TA], self.Ee[g], self.gfin.v[0:KH, :], start=True, stop=True), inc=st_)
            wait("vector", st_, v)
            gfb = self.gft[g % 2]
            em("vector", lambda e, bk=bk, g=g, gfb=gfb: e.tensor_tensor(
                out=gfb, in0=self.y_raw.v[:, TA * g:TA * (g + 1)], in1=bk[:, 0:TA], op=TS.mult))
            vv = em("vector", lambda e, g=g, gfb=gfb: e.tensor_tensor(
                out=gfb, in0=gfb, in1=self.silu_g.v[:, TA * g:TA * (g + 1)], op=TS.mult), inc=sv)
            self.bank_consumed(aux, sv, vv)
            wait("scalar", sv, vv)
            vhi = em("scalar", lambda e, g=g, gfb=gfb: e.copy(
                out=self.g_hi.v[:, TA * g:TA * (g + 1)], in_=gfb), inc=ss)
            wait("vector", ss, vhi)
            em("vector", lambda e, g=g, gfb=gfb: e.tensor_tensor(
                out=self.g_lo.v[:, TA * g:TA * (g + 1)], in0=gfb,
                in1=self.g_hi.v[:, TA * g:TA * (g + 1)], op=TS.subtract))
        vgp = em("vector", lambda e: e.drain(),
                 inc=sv)
        self.gfin.free = (sv, vgp)
        self.silu_g.free = (sv, vgp)
        self.y_raw.free = (sv, vgp)

        wait("tensor", sv, vgp)
        for cg in range(2):
            bk = self.bank(6 + cg, "tensor")
            for k in range(8):
                sG = slice(TA * k, TA * (k + 1))
                sW = slice(D * k + 512 * cg, D * k + 512 * cg + 512)
                em("tensor", lambda e, bk=bk, sG=sG, sW=sW, k=k: e.matmul(
                    bk[:, :], self.g_hi.v[:, sG], self.WOhi[:, sW], start=(k == 0), stop=False))
                em("tensor", lambda e, bk=bk, sG=sG, sW=sW: e.matmul(
                    bk[:, :], self.g_lo.v[:, sG], self.WOhi[:, sW], start=False, stop=False))
                em("tensor", lambda e, bk=bk, sG=sG, sW=sW, k=k: e.matmul(
                    bk[:, :], self.g_hi.v[:, sG], self.WOlo[:, sW], start=False, stop=(k == 7)),
                   inc=(st_ if k == 7 else None))
        v_op = st_.n
        self.g_hi.free = (st_, v_op)
        self.g_lo.free = (st_, v_op)

        xr = self.xres_tm[ci % 2]
        self.waitf("vector", xr.free)
        wait("vector", st_, v_op)
        waitf("vector", t_gaB)
        if b == 0 and ci == 0:
            self.dbg("gaB", self.gaB)
        for cg in range(2):
            em("vector", lambda e, cg=cg, xr=xr: e.tensor_tensor(
                out=xr.v[:, 512 * cg:512 * cg + 512],
                in0=self.psum[6 + cg][:, :], in1=self.gaB[:, 512 * cg:512 * cg + 512], op=TS.mult))
            v = em("vector", lambda e, cg=cg, xr=xr, xb=xb: e.tensor_tensor(
                out=xr.v[:, 512 * cg:512 * cg + 512], in0=xr.v[:, 512 * cg:512 * cg + 512],
                in1=xb.v[:, 512 * cg:512 * cg + 512], op=TS.add), inc=sv)
            self.bank_consumed(6 + cg, sv, v)
        vxr = sv.n
        xb.free = (sv, vxr)
        wait("gpsimd", sv, vxr)
        if b == 0 and ci == 0:
            self.dbg("xres0", xr.v)
        vst = dma("gpsimd", None, self.xres_h[b, t0:t0 + TA, :], xr.v)

        wait("scalar", sv, vxr)
        v = em("scalar", lambda e, xr=xr: e.activation(out=self.sq_scr, in_=xr.v,
                                                       func=AF.Square), inc=ss)
        wait("vector", ss, v)
        em("vector", lambda e: e.tensor_reduce(out=self.lnrow[:, 6:7], in_=self.sq_scr,
                                               axis=AX.X, op=TS.add))
        em("vector", lambda e: e.drain())
        em("vector", lambda e, xr=xr: e.tensor_reduce(out=self.lnrow[:, 5:6], in_=xr.v,
                                                      axis=AX.X, op=TS.add))
        em("vector", lambda e: e.drain())
        em("vector", lambda e: e.tensor_scalar(out=self.lnrow[:, 5:6], in0=self.lnrow[:, 5:6],
                                               scalar1=1.0 / D, scalar2=None, op0=TS.mult, op1=TS.bypass))
        em("vector", lambda e: e.drain())
        em("vector", lambda e: e.tensor_tensor(out=self.lnrow[:, 7:8], in0=self.lnrow[:, 5:6],
                                               in1=self.lnrow[:, 5:6], op=TS.mult))
        em("vector", lambda e: e.drain())
        v = em("vector", lambda e: e.scalar_tensor_tensor(out=self.lnrow[:, 8:9], in0=self.lnrow[:, 6:7],
                                                          scalar=1.0 / D, in1=self.lnrow[:, 7:8],
                                                          op0=TS.mult, op1=TS.subtract), inc=sv)
        wait("scalar", sv, v)
        em("scalar", lambda e: e.activation(out=self.lnrow[:, 9:10], in_=self.lnrow[:, 8:9],
                                            func=AF.Ln, bias=1e-5))
        v = em("scalar", lambda e: e.activation(out=self.lnrow[:, 9:10], in_=self.lnrow[:, 9:10],
                                                func=AF.Exp, scale=-0.5), inc=ss)
        wait("vector", ss, v)
        em("vector", lambda e, it=itile: e.tensor_copy(out=self.stats[:, 2 * it:2 * it + 1],
                                                       in_=self.lnrow[:, 5:6]))
        em("vector", lambda e, it=itile: e.tensor_copy(out=self.stats[:, 2 * it + 1:2 * it + 2],
                                                       in_=self.lnrow[:, 9:10]))
        t2b = self.t2_tm
        self.waitf("vector", t2b.free)
        v2 = em("vector", lambda e, xr=xr, t2b=t2b: e.tensor_scalar(
            out=t2b.v, in0=xr.v, scalar1=self.lnrow[:, 5:6], scalar2=self.lnrow[:, 9:10],
            op0=TS.subtract, op1=TS.mult), inc=sv)
        self.lnrow_free = (sv, v2)
        xr.free = vst
        wait("tensor", sv, v2)
        self.waitf("tensor", self.t2fm.free)
        for r in range(2):
            bk = self.bank(0, "tensor")
            for j in range(4):
                jj = 4 * r + j
                em("tensor", lambda e, bk=bk, j=j, jj=jj, t2b=t2b: e.transpose(
                    bk[:, 128 * j:128 * j + 128], t2b.v[:, 128 * jj:128 * jj + 128], self.ident),
                   inc=(st_ if j == 3 else None))
            wait("scalar", st_, st_.n)
            v = em("scalar", lambda e, bk=bk, r=r: e.copy(
                out=self.t2fm.v[:, 512 * r:512 * r + 512], in_=bk[:, :]), inc=ss)
            self.bank_consumed(0, ss, v)
        t2b.free = (st_, st_.n)
        v_t2fm = ss.n
        wait("vector", ss, v_t2fm)
        for j in range(8):
            sl = slice(TA * j, TA * (j + 1))
            em("vector", lambda e, j=j, sl=sl: e.tensor_scalar(
                out=self.hbt.v, in0=self.t2fm.v[:, sl], scalar1=self.cbv[b][:, j:j + 1],
                scalar2=self.sbv[b][:, j:j + 1], op0=TS.mult, op1=TS.add))
            em("vector", lambda e, j=j: e.tensor_reduce(
                out=self.am8[:, j:j + 1], in_=self.hbt.v, axis=AX.X, op=TS.max,
                apply_absolute_value=True))
        em("vector", lambda e: e.drain())
        em("vector", lambda e: e.tensor_reduce(out=self.am8[:, 8:9], in_=self.am8[:, 0:8],
                                               axis=AX.X, op=TS.max))
        em("vector", lambda e: e.drain())
        em("vector", lambda e: e.tensor_tensor(out=self.amaxh, in0=self.amaxh,
                                               in1=self.am8[:, 8:9], op=TS.max))
        v = em("vector", lambda e: e.drain(), inc=sv)
        self.t2fm.free = (sv, v)
        return vst

    def exchange(self, which):
        em, wait, dma, waitf = self.em, self.wait, self.dma, self.waitf
        sv, ss, st_, cc = self.sv, self.ss, self.st_, self.cc
        TS = ALU
        amax_l = self.amaxh if which == 1 else self.amaxd
        ci, co = (self.cc1i_h, self.cc1o_h) if which == 1 else (self.cc2i_h, self.cc2o_h)
        sdst = self.sxv if which == 1 else self.shv
        v0 = self.em("vector", lambda e: e.drain(), inc=sv)
        self.dbg("amx%d" % which, amax_l)
        bk = self.bank(0, "tensor")
        wait("tensor", sv, v0)
        v = em("tensor", lambda e, bk=bk, amax_l=amax_l: e.transpose(
            bk[0:1, 0:128], amax_l, self.ident), inc=st_)
        wait("vector", st_, v)
        em("vector", lambda e, bk=bk: e.tensor_reduce(out=self.rowt[0:1, 0:1], in_=bk[0:1, 0:128],
                                                      axis=AX.X, op=TS.max))
        em("vector", lambda e: e.drain())
        em("vector", lambda e: e.tensor_scalar(out=self.rowt[0:1, 0:1], in0=self.rowt[0:1, 0:1],
                                               scalar1=1e-8, scalar2=None, op0=TS.max, op1=TS.bypass))
        em("vector", lambda e: e.drain())
        for q in range(1, 4):
            em("vector", lambda e, q=q: e.tensor_copy(out=self.rowt[0:1, q:q + 1],
                                                      in_=self.rowt[0:1, 0:1]))
        v = em("vector", lambda e: e.drain(), inc=sv)
        self.bank_consumed(0, sv, v)
        wait("gpsimd", sv, v)
        self.dbg("rowa%d" % which, self.rowt[0:1, 0:4])
        t_ = dma("gpsimd", None, ci[:, :], self.rowt[0:1, 0:4])
        waitf("gpsimd", t_)
        v = em("gpsimd", lambda e, ci=ci, co=co: e.collective_compute(
            "AllReduce", ALU.max, replica_groups=[list(range(NCORES_RUN[0]))],
            ins=[ci.opt()], outs=[co.opt()]), inc=cc)
        wait("gpsimd", cc, v)
        t_ = dma("gpsimd", None, self.rowt[0:1, 8:12], co[:, :])
        waitf("vector", t_)
        em("vector", lambda e: e.reciprocal(out=self.rowt[0:1, 13:14], in_=self.rowt[0:1, 8:9]))
        em("vector", lambda e: e.drain())
        em("vector", lambda e: e.tensor_scalar(out=self.rowt[0:1, 12:13], in0=self.rowt[0:1, 13:14],
                                               scalar1=32767.0, scalar2=None,
                                               op0=TS.mult, op1=TS.bypass))
        v = em("vector", lambda e: e.drain(), inc=sv)
        wait("gpsimd", sv, v)
        self.dbg("rowb%d" % which, self.rowt[0:1, 8:14])
        t_ = dma("gpsimd", None, self.bc_h[10 + which:11 + which, 0:1], self.rowt[0:1, 12:13])
        waitf("gpsimd", t_)
        t_ = dma("gpsimd", None, sdst,
                 self.bc_h[10 + which:11 + which, 0:1].partition_broadcast(128))
        waitf("vector", t_)
        TSx = ALU
        self.dbg("scl%d" % which, sdst)
        if which == 1:
            for b in range(2):
                em("vector", lambda e, b=b: e.tensor_scalar(
                    out=self.cbsxv[b], in0=self.cbv[b], scalar1=self.sxv[:, 0:1], scalar2=None,
                    op0=TSx.mult, op1=TSx.bypass))
                em("vector", lambda e, b=b: e.tensor_scalar(
                    out=self.sbsxv[b], in0=self.sbv[b], scalar1=self.sxv[:, 0:1], scalar2=None,
                    op0=TSx.mult, op1=TSx.bypass))
            em("vector", lambda e: e.tensor_scalar(
                out=self.iswx, in0=self.swv, scalar1=self.sxv[:, 0:1], scalar2=None,
                op0=TSx.mult, op1=TSx.bypass))
            em("vector", lambda e: e.drain())
            v = em("vector", lambda e: e.reciprocal(out=self.iswx, in_=self.iswx), inc=sv)
            em("vector", lambda e: e.drain())
        else:
            em("vector", lambda e: e.tensor_scalar(
                out=self.vrv, in0=self.swov, scalar1=self.shv[:, 0:1], scalar2=None,
                op0=TSx.mult, op1=TSx.bypass))
            em("vector", lambda e: e.drain())
            v = em("vector", lambda e: e.reciprocal(out=self.vrv, in_=self.vrv), inc=sv)
            em("vector", lambda e: e.drain())
        self.barrier()

    def alloc_B1(self):
        A = Arena(self.arena_t.ap(), self.AR, "B1", self.res_end)
        T = self.TB1
        NT = T // 128
        self.Wq1 = A.alloc(8 * DFF, BF16)
        self.b1_x = [A.buf(D) for _ in range(3)]
        self.b1_t = [A.buf(D) for _ in range(2)]
        self.b1_tf = A.buf(NT * 8 * 128)
        self.b1_w1 = A.alloc(T)
        self.b1_q = A.alloc(T)
        self.q_hi = A.buf(8 * T, BF16)
        self.q_lo = A.buf(8 * T, BF16)
        self.hidb = [A.buf(T) for _ in range(3)]
        assert A.off <= self.AR, A.off
        self.B1_end = A.off

    def b1_chunk(self, b, ci):
        T = self.TB1
        NT = T // 128
        t0 = ci * T
        em, wait, dma, waitf = self.em, self.wait, self.dma, self.waitf
        sv, ss, st_ = self.sv, self.ss, self.st_
        TS = ALU
        self.waitf("tensor", self.b1_tf.free)
        for i in range(NT):
            xb = self.b1_x[i % 3]
            self.waitf("sync", xb.free)
            t_x = dma("sync", None, xb.v, self.xres_h[b, t0 + 128 * i:t0 + 128 * (i + 1), :])
            it = (b * self.L + t0 + 128 * i) // 128
            tb = self.b1_t[i % 2]
            self.waitf("vector", tb.free)
            waitf("vector", t_x)
            vt = em("vector", lambda e, xb=xb, tb=tb, it=it: e.tensor_scalar(
                out=tb.v, in0=xb.v, scalar1=self.stats[:, 2 * it:2 * it + 1],
                scalar2=self.stats[:, 2 * it + 1:2 * it + 2],
                op0=TS.subtract, op1=TS.mult), inc=sv)
            xb.free = (sv, vt)
            wait("tensor", sv, vt)
            for r in range(2):
                bk = self.bank(0, "tensor")
                for j in range(4):
                    jj = 4 * r + j
                    em("tensor", lambda e, bk=bk, j=j, jj=jj, tb=tb: e.transpose(
                        bk[:, 128 * j:128 * j + 128], tb.v[:, 128 * jj:128 * jj + 128],
                        self.ident), inc=(st_ if j == 3 else None))
                wait("scalar", st_, st_.n)
                v = em("scalar", lambda e, bk=bk, r=r, i=i: e.copy(
                    out=self.b1_tf.v.rearrange("p (j t) -> p j t", j=8)[:, 4 * r:4 * r + 4,
                                                                       128 * i:128 * i + 128],
                    in_=bk[:, :].rearrange("p (j t) -> p j t", j=4)), inc=ss)
                self.bank_consumed(0, ss, v)
            tb.free = (st_, st_.n)
        v_tf = self.ss.n
        wait("vector", ss, v_tf)
        self.waitf("vector", self.q_hi.free)
        self.waitf("vector", self.q_lo.free)
        for j in range(8):
            sl = slice(T * j, T * (j + 1))
            em("vector", lambda e, j=j, sl=sl, b=b: e.tensor_scalar(
                out=self.b1_w1[:, 0:T], in0=self.b1_tf.v[:, sl], scalar1=self.cbsxv[b][:, j:j + 1],
                scalar2=self.sbsxv[b][:, j:j + 1], op0=TS.mult, op1=TS.add))
            em("vector", lambda e: e.drain())
            vqj = em("vector", lambda e, j=j: e.tensor_scalar(
                out=self.b1_q[:, 0:T], in0=self.b1_w1[:, 0:T], scalar1=RC, scalar2=RC,
                op0=TS.add, op1=TS.subtract), inc=sv)
            wait("scalar", sv, vqj)
            vhi = em("scalar", lambda e, j=j, sl=sl: e.copy(
                out=self.q_hi.v[:, sl], in_=self.b1_q[:, 0:T]), inc=ss)
            wait("vector", ss, vhi)
            em("vector", lambda e, j=j, sl=sl: e.tensor_tensor(
                out=self.q_lo.v[:, sl], in0=self.b1_q[:, 0:T],
                in1=self.q_hi.v[:, sl], op=TS.subtract))
        vq = em("vector", lambda e: e.drain(), inc=sv)
        self.b1_tf.free = (sv, vq)
        if b == 0 and ci == 0:
            self.dbg("b1tf", self.b1_tf.v[:, 0:1024])
            self.dbg("qhi", self.q_hi.v[:, 0:1024])
            self.dbg("qlo", self.q_lo.v[:, 0:1024])
        wait("tensor", sv, vq)
        NH = T // 512 if T >= 512 else 1
        NW = min(512, T)
        for ct in range(32):
            bis = [1 + (2 * ct + h) % 4 for h in range(NH)]
            bks = [self.bank(bi, "tensor") for bi in bis]
            for k in range(8):
                sW = slice(DFF * k + 128 * ct, DFF * k + 128 * ct + 128)
                for h in range(NH):
                    sQ = slice(T * k + NW * h, T * k + NW * (h + 1))
                    em("tensor", lambda e, bk=bks[h], sW=sW, sQ=sQ, k=k: e.matmul(
                        bk[:, 0:NW], self.Wq1[:, sW], self.q_hi.v[:, sQ],
                        start=(k == 0), stop=False))
                    em("tensor", lambda e, bk=bks[h], sW=sW, sQ=sQ, k=k: e.matmul(
                        bk[:, 0:NW], self.Wq1[:, sW], self.q_lo.v[:, sQ],
                        start=False, stop=(k == 7)), inc=(st_ if (k == 7 and h == NH - 1) else None))
            vmm = st_.n
            hb = self.hidb[ct % 3]
            self.waitf("scalar", hb.free)
            wait("scalar", st_, vmm)
            vg = ss.n
            for h in range(NH):
                vg = em("scalar", lambda e, hb=hb, bk=bks[h], ct=ct, h=h: e.activation(
                    out=hb.v[:, NW * h:NW * (h + 1)], in_=bk[:, 0:NW], func=AF.Gelu,
                    scale=self.iswx[:, ct:ct + 1], bias=self.biT[:, ct:ct + 1]), inc=ss)
                self.bank_consumed(bis[h], ss, vg)
            wait("gpsimd", ss, vg)
            if b == 0 and ci == 0 and ct == 0:
                self.dbg("hid0", hb.v)
            t_hid = dma("gpsimd", None, self.hid_h[128 * ct:128 * ct + 128,
                                                   b * self.L + t0:b * self.L + t0 + T], hb.v)
            wait("vector", ss, vg)
            em("vector", lambda e, hb=hb, ct=ct: e.tensor_reduce(
                out=self.am8[:, 16 + ct % 8:17 + ct % 8], in_=hb.v, axis=AX.X, op=TS.max,
                apply_absolute_value=True))
            em("vector", lambda e: e.drain())
            em("vector", lambda e, ct=ct: e.tensor_tensor(
                out=self.amaxd, in0=self.amaxd, in1=self.am8[:, 16 + ct % 8:17 + ct % 8],
                op=TS.max))
            v = em("vector", lambda e: e.drain(), inc=sv)
            hb.free = t_hid
        self.q_hi.free = (st_, st_.n)
        self.q_lo.free = (st_, st_.n)

    def alloc_B2(self):
        A = Arena(self.arena_t.ap(), self.AR, "B2", self.res_end)
        T = self.TB2
        self.Wq2 = A.alloc(32 * D, BF16)
        self.gbB = A.alloc(D)
        self.bor = A.alloc(D, rows=1)
        self.b2_h = [A.buf(T) for _ in range(3)]
        self.b2_a = A.alloc(T)
        self.a_hi = [A.buf(T, BF16) for _ in range(2)]
        self.a_lo = [A.buf(T, BF16) for _ in range(2)]
        self.b2_x = [A.buf(D) for _ in range(5)]
        self.b2_o = [A.buf(D) for _ in range(2)]
        assert A.off <= self.AR, A.off
        self.B2_end = A.off

    def b2_load_w(self):
        em, wait, dma = self.em, self.wait, self.dma
        self.wq2_toks = [dma("sync", None, self.bor, self.bo_h[:, :])]
        for s in range(32):
            self.wq2_toks.append(dma("sync", None, self.Wq2[:, D * s:D * (s + 1)],
                                     self.wqo_h[128 * s:128 * s + 128, :]))

    def b2_chunk(self, b, ci):
        T = self.TB2
        NT = T // 128
        t0 = ci * T
        gt0 = b * self.L + t0
        em, wait, dma, waitf = self.em, self.wait, self.dma, self.waitf
        sv, ss, st_ = self.sv, self.ss, self.st_
        TS = ALU
        if ci == 0 and b == 0:
            for t_ in self.wq2_toks:
                waitf("tensor", t_)
            if self.debug:
                for t_ in self.wq2_toks:
                    waitf("gpsimd", t_)
            self.dbg("wq2s0", self.Wq2[:, 0:1024])
            self.dbg("wq2s31", self.Wq2[:, 31 * D:32 * D])
            self.dbg("bor0", self.bor[0:1, :])
        if ci == 0:
            wait("gpsimd", sv, sv.n)
            self.t_gbB = dma("gpsimd", None, self.gbB,
                             self.bc_h[2 + b:3 + b, :].partition_broadcast(128))
        t_gbB = self.t_gbB
        xts = []
        for i in range(NT):
            xb = self.b2_x[(ci * NT + i) % 5]
            self.waitf("sync", xb.free)
            xb.ready = dma("sync", None, xb.v,
                           self.xres_h[b, t0 + 128 * i:t0 + 128 * (i + 1), :])
            xts.append(xb)
        banks = {}
        for tg in range(NT):
            for cg in range(2):
                bi = 2 * tg + cg
                banks[(tg, cg)] = self.bank(bi, "tensor")
        for k in range(32):
            hb = self.b2_h[k % 3]
            self.waitf("sync", hb.free)
            t_h = dma("sync", None, hb.v, self.hid_h[128 * k:128 * k + 128, gt0:gt0 + T])
            ah, al = self.a_hi[k % 2], self.a_lo[k % 2]
            self.waitf("vector", ah.free)
            self.waitf("vector", al.free)
            waitf("vector", t_h)
            em("vector", lambda e, hb=hb, k=k: e.tensor_scalar(
                out=self.b2_a, in0=hb.v, scalar1=self.shv[:, 0:1], scalar2=RC,
                op0=TS.mult, op1=TS.add))
            vq = em("vector", lambda e, k=k: e.tensor_scalar(
                out=self.b2_a, in0=self.b2_a, scalar1=RC, scalar2=self.vrv[:, k:k + 1],
                op0=TS.subtract, op1=TS.mult), inc=sv)
            hb.free = (sv, vq)
            wait("scalar", sv, vq)
            va = em("scalar", lambda e, ah=ah: e.copy(out=ah.v, in_=self.b2_a), inc=ss)
            wait("vector", ss, va)
            vl = em("vector", lambda e, ah=ah, al=al: e.tensor_tensor(
                out=al.v, in0=self.b2_a, in1=ah.v, op=TS.subtract), inc=sv)
            if b == 0 and ci == 0 and k == 0:
                self.dbg("b2a", self.b2_a)
            wait("tensor", ss, va)
            wait("tensor", sv, vl)
            for tg in range(NT):
                sA = slice(128 * tg, 128 * (tg + 1))
                for cg in range(2):
                    bk = banks[(tg, cg)]
                    sW = slice(D * k + 512 * cg, D * k + 512 * cg + 512)
                    em("tensor", lambda e, bk=bk, sA=sA, sW=sW, ah=ah, k=k: e.matmul(
                        bk[:, :], ah.v[:, sA], self.Wq2[:, sW], start=(k == 0), stop=False))
                    lastk = (tg == NT - 1 and cg == 1)
                    em("tensor", lambda e, bk=bk, sA=sA, sW=sW, al=al, k=k: e.matmul(
                        bk[:, :], al.v[:, sA], self.Wq2[:, sW], start=False, stop=False),
                       inc=(st_ if lastk else None))
            ah.free = (st_, st_.n)
            al.free = (st_, st_.n)
        for tg in range(NT):
            for cg in range(2):
                bk = banks[(tg, cg)]
                em("tensor", lambda e, bk=bk, cg=cg: e.matmul(
                    bk[:, :], self.onesr[0:1, 0:128], self.bor[0:1, 512 * cg:512 * cg + 512],
                    start=False, stop=True), inc=st_)
        v_mm = st_.n
        wait("vector", st_, v_mm)
        waitf("vector", t_gbB)
        if b == 0 and ci == 0:
            self.dbg("gbB0", self.gbB[:, 0:1024])
            self.dbg("xbB0", xts[0].v)
        for tg in range(NT):
            ob = self.b2_o[tg % 2]
            self.waitf("vector", ob.free)
            xb = xts[tg]
            waitf("vector", xb.ready)
            for cg in range(2):
                bi = 2 * tg + cg
                em("vector", lambda e, ob=ob, bi=bi, cg=cg: e.tensor_tensor(
                    out=ob.v[:, 512 * cg:512 * cg + 512], in0=self.psum[bi][:, :],
                    in1=self.gbB[:, 512 * cg:512 * cg + 512], op=TS.mult))
                v = em("vector", lambda e, ob=ob, xb=xb, cg=cg: e.tensor_tensor(
                    out=ob.v[:, 512 * cg:512 * cg + 512],
                    in0=ob.v[:, 512 * cg:512 * cg + 512],
                    in1=xb.v[:, 512 * cg:512 * cg + 512], op=TS.add), inc=sv)
                self.bank_consumed(bi, sv, v)
            vo = sv.n
            xb.free = (sv, vo)
            if b == 0 and ci == 0 and tg == 0:
                self.dbg("yout0", ob.v)
            wait("gpsimd", sv, vo)
            ob.free = dma("gpsimd", None, self.y_h[b, t0 + 128 * tg:t0 + 128 * (tg + 1), :],
                          ob.v)

    def finish(self):
        for s in self.pool["gpsimd"]:
            if s.n:
                self.wait("gpsimd", s, s.n)
        for s in self.pool["sync"]:
            if s.n:
                self.wait("sync", s, s.n)

    def build(self):
        self.prologue()
        self.alloc_A()
        nchunks = self.L // self.TA
        it = 0
        for b in range(B_PER_CORE):
            for ci in range(nchunks):
                self.a_chunk(b, ci, (b * self.L + ci * self.TA) // 128)
        self.barrier()
        self.exchange(1)
        self.alloc_B1()
        for k in range(8):
            t_ = self.dma("sync", None, self.Wq1[:, DFF * k:DFF * (k + 1)],
                          self.wqi_h[128 * k:128 * k + 128, :])
            self.waitf("tensor", t_)
        for b in range(B_PER_CORE):
            for ci in range(self.L // self.TB1):
                self.b1_chunk(b, ci)
        self.barrier()
        self.exchange(2)
        self.alloc_B2()
        self.b2_load_w()
        for b in range(B_PER_CORE):
            for ci in range(self.L // self.TB2):
                self.b2_chunk(b, ci)
        self.finish()
        nc = self.nc
        with nc.Block() as blk:
            @blk.sync
            def _(e):
                for f in self.S["sync"]:
                    f(e)

            @blk.vector
            def _(e):
                for f in self.S["vector"]:
                    f(e)

            @blk.scalar
            def _(e):
                for f in self.S["scalar"]:
                    f(e)

            @blk.tensor
            def _(e):
                for f in self.S["tensor"]:
                    f(e)

            @blk.gpsimd
            def _(e):
                for f in self.S["gpsimd"]:
                    f(e)
        self.stack.close()
        return nc


NCORES_RUN = [8]


def _host_inputs(inp, L):
    x = np.ascontiguousarray(inp["L0_attn_x"])
    c = np.ascontiguousarray(inp["L0_c"])
    cw = np.asarray(inp["L0_attn_conv_w"]).reshape(4, TOTAL)
    cwp = np.zeros((4, NCT * 128), np.float32)
    cwp[:, :TOTAL] = cw
    cw_l = np.ascontiguousarray(cwp.reshape(4, NCT, 128).transpose(2, 0, 1).reshape(128, 4 * NCT))
    cbp = np.zeros((NCT * 128,), np.float32)
    cbp[:TOTAL] = np.asarray(inp["L0_attn_conv_b"])
    cb_l = np.ascontiguousarray(cbp.reshape(NCT, 128).T)
    theta = np.asarray(inp["L0_attn_theta"])
    th2 = np.ascontiguousarray(theta[0, 0, 0, 0, 2:4].reshape(1, 2))
    nsc = np.ascontiguousarray(np.asarray(inp["L0_attn_norm_scale"]).reshape(2, 128).T)
    maps = []
    ncore = NCORES_RUN[0]
    for i in range(ncore):
        maps.append({
            "x": np.ascontiguousarray(x[2 * i:2 * i + 2]),
            "c2": np.ascontiguousarray(c[2 * i:2 * i + 2]),
            "w_inproj": np.ascontiguousarray(inp["L0_attn_in_proj_w"]),
            "conv_w_l": cw_l,
            "conv_b_l": cb_l,
            "theta2": th2,
            "dslope": np.ascontiguousarray(np.asarray(inp["L0_attn_decay_slopes"]).reshape(28, 1)),
            "aslope": np.ascontiguousarray(np.asarray(inp["L0_attn_anchor_slopes"]).reshape(4, 1)),
            "score": np.ascontiguousarray(np.asarray(inp["L0_attn_score_scale"]).reshape(KH, 1)),
            "deriv": np.ascontiguousarray(np.asarray(inp["L0_attn_deriv_logits"]).reshape(1, 3)),
            "nsc": nsc,
            "w_re": np.ascontiguousarray(inp["L0_attn_W_re"]),
            "w_im": np.ascontiguousarray(inp["L0_attn_W_im"]),
            "w_outproj": np.ascontiguousarray(inp["L0_attn_out_proj_w"]),
            "w_in_T": np.ascontiguousarray(np.asarray(inp["L0_ffn_W_in"]).T),
            "b_in_t": np.ascontiguousarray(np.asarray(inp["L0_ffn_b_in"]).reshape(32, 128).T),
            "w_out_f": np.ascontiguousarray(inp["L0_ffn_W_out"]),
            "b_out_r": np.ascontiguousarray(np.asarray(inp["L0_ffn_b_out"]).reshape(1, D)),
            "wmod": np.ascontiguousarray(inp["L0_wmod"]),
        })
    return maps


def kernel(**inputs):
    L = int(np.asarray(inputs["L0_attn_x"]).shape[1])
    kb = KB(L=L)
    nc = kb.build()
    maps = _host_inputs(inputs, L)
    res = run_bass_kernel_spmd(nc, maps, core_ids=list(range(NCORES_RUN[0])))
    out = np.concatenate([np.asarray(r["y"]) for r in res.results], axis=0)
    return out


# version 04
# speedup vs baseline: 1.0562x; 1.0562x over previous
import numpy as np
import concourse.bass as bass
import concourse.mybir as mybir
from concourse.bass_utils import run_bass_kernel_spmd
from concourse.alu_op_type import AluOpType as ALU
from contextlib import ExitStack

F32 = mybir.dt.float32
BF16 = mybir.dt.bfloat16
U8 = mybir.dt.uint8
AF = mybir.ActivationFunctionType
AX = mybir.AxisListType

RC = float(3 * 2 ** 22)
D = 1024
TOTAL = 2080
KH = 32
NCT = 17
DFF = 4096
B_PER_CORE = 2

def _rows(ct):
    return 32 if ct == NCT - 1 else 128


class Sem:
    def __init__(self, handle, step):
        self.h, self.step, self.n = handle, step, 0


class Buf:

    def __init__(self, view):
        self.v = view
        self.free = None


class Arena:
    def __init__(self, ap_u8, nbytes, name, off=0):
        self.ap, self.nbytes, self.off, self.name = ap_u8, nbytes, off, name

    def alloc(self, cols, dt=F32, rows=128):
        nbyt = cols * mybir.dt.size(dt)
        assert self.off + nbyt <= self.nbytes, (
            f"arena {self.name} overflow at +{nbyt}: {self.off} > {self.nbytes}")
        v = self.ap[0:rows, self.off:self.off + nbyt].bitcast(dt)
        self.off = (self.off + nbyt + 63) & ~63
        return v

    def buf(self, cols, dt=F32, rows=128):
        return Buf(self.alloc(cols, dt, rows))


class KB:
    def __init__(self, L=2048, TA=128, TB1=1024, TB2=512, debug=False, taps=None):
        self.debug = debug
        self.taps = taps
        self.dbg_names = []
        self.L, self.TA = L, TA
        self.TB1, self.TB2 = min(TB1, L), min(TB2, L)
        nc = bass.Bass("TRN2", target_bir_lowering=False, debug=False)
        self.nc = nc
        self.S = {k: [] for k in ("sync", "vector", "scalar", "tensor", "gpsimd")}
        self.stack = ExitStack()

        dr = lambda n, s, dt=F32, k="ExternalInput": nc.dram_tensor(n, s, dt, kind=k).ap()
        self.x_h = dr("x", [B_PER_CORE, L, D])
        self.c_h = dr("c2", [B_PER_CORE, D])
        self.wA_h = dr("w_inproj", [D, TOTAL])
        self.cw_h = dr("conv_w_l", [128, 4 * NCT])
        self.cb_h = dr("conv_b_l", [128, NCT])
        self.th_h = dr("theta2", [1, 2])
        self.dsl_h = dr("dslope", [28, 1])
        self.asl_h = dr("aslope", [4, 1])
        self.sc_h = dr("score", [KH, 1])
        self.dlg_h = dr("deriv", [1, 3])
        self.ns_h = dr("nsc", [128, 2])
        self.wre_h = dr("w_re", [128, KH])
        self.wim_h = dr("w_im", [128, KH])
        self.wO_h = dr("w_outproj", [D, D])
        self.wiT_h = dr("w_in_T", [DFF, D])
        self.bi_h = dr("b_in_t", [128, 32])
        self.wo_h = dr("w_out_f", [DFF, D])
        self.bo_h = dr("b_out_r", [1, D])
        self.wm_h = dr("wmod", [D, 6 * D])
        self.y_h = dr("y", [B_PER_CORE, L, D], F32, "ExternalOutput")
        self.xres_h = dr("xres_s", [B_PER_CORE, L, D], F32, "Internal")
        self.hid_h = dr("hidden_s", [DFF, B_PER_CORE * L], F32, "Internal")
        self.wqi_h = dr("wqi_s", [D, DFF], BF16, "Internal")
        self.wqo_h = dr("wqo_s", [DFF, D], BF16, "Internal")
        self.bc_h = dr("bcast_s", [16, D], F32, "Internal")
        self.cc1i_h = dr("cc1_in", [1, 4], F32, "Internal")
        self.cc1o_h = dr("cc1_out", [1, 4], F32, "Internal")
        self.cc2i_h = dr("cc2_in", [1, 4], F32, "Internal")
        self.cc2o_h = dr("cc2_out", [1, 4], F32, "Internal")
        self.tw_h = dr("tw_s", [KH, L], F32, "Internal")

        mks = lambda n, st: Sem(self.stack.enter_context(nc.semaphore(n)), st)
        self.sv = mks("sv", 1)
        self.ss = mks("ss", 1)
        self.st_ = mks("st", 1)
        self.sg = mks("sg", 1)
        self.cc = mks("ccs", 1)
        self.pool = {"sync": [mks(f"dw{i}", 16) for i in range(7)],
                     "gpsimd": [mks(f"dg{i}", 16) for i in range(7)]}
        self.pool_rr = {"sync": 0, "gpsimd": 0}
        self.allsems = ([self.sv, self.ss, self.st_, self.sg, self.cc]
                        + self.pool["sync"] + self.pool["gpsimd"])

        AR = 204 * 1024
        self.arena_t = self.stack.enter_context(nc.sbuf_tensor("arena", [128, AR], U8))
        self.AR = AR
        self.psum = [self.stack.enter_context(nc.psum_tensor(f"pb{i}", [128, 512], F32))
                     for i in range(8)]
        self.psum_free = [None] * 8
        self._residents()

    def em(self, eng, fn, inc=None):
        if inc is None:
            self.S[eng].append(fn)
            return None
        inc.n += inc.step
        h, s = inc.h, inc.step
        self.S[eng].append(lambda e: fn(e).then_inc(h, s))
        return inc.n

    def wait(self, eng, sem, val):
        assert 0 <= val <= sem.n, f"wait for future value {val} > {sem.n} ({eng})"
        if val > 0:
            self.S[eng].append(lambda e: e.wait_ge(sem.h, val))

    def waitf(self, eng, free):
        if free is not None:
            self.wait(eng, free[0], free[1])

    def dma(self, qeng, _sem_ignored, out, in_, slow=False):
        p = self.pool[qeng]
        i = self.pool_rr[qeng] % len(p)
        self.pool_rr[qeng] += 1
        s = p[i]
        if s.n > 0:
            self.wait(qeng, s, s.n)
        if slow:
            self.em(qeng, lambda e: e.dma_start(out=out, in_=in_,
                                                allow_slow_non_contiguous=True), inc=s)
        else:
            self.em(qeng, lambda e: e.dma_start(out=out, in_=in_), inc=s)
        return (s, s.n)

    def dbg(self, name, view, src_sem=None):
        if not self.debug:
            return
        if self.taps is not None and name not in self.taps:
            return
        rows, cols = view.shape[0], view.shape[1]
        dtt = view.dtype
        t = self.nc.dram_tensor("dbg_" + name, [rows, cols], dtt, kind="ExternalOutput").ap()
        self.dbg_names.append("dbg_" + name)
        for s in (self.sv, self.ss, self.st_, self.sg):
            if s.n:
                self.wait("gpsimd", s, s.n)
        tok = self.dma("gpsimd", None, t[:, :], view, slow=True)
        for eng in self.S:
            self.waitf(eng, tok)

    def barrier(self):
        for eng in self.S:
            for s in self.allsems:
                if s.n > 0:
                    self.wait(eng, s, s.n)

    def bank(self, i, eng, sem=None):
        self.waitf(eng, self.psum_free[i])
        return self.psum[i]

    def bank_consumed(self, i, sem, val):
        self.psum_free[i] = (sem, val)

    def _residents(self):
        A = Arena(self.arena_t.ap(), self.AR, "res")
        TA, L = self.TA, self.L
        self.ident = A.alloc(128)
        self.identb = A.alloc(128, BF16)
        self.zrow = A.alloc(max(TA, 64))
        self.iof1 = A.alloc(128)
        self.iop = A.alloc(1)
        self.k4v = A.alloc(1)
        self.selg = [A.alloc(KH) for _ in range(8)]
        self.Ee = [A.alloc(128, rows=32) for _ in range(8)]
        self.wdiag = [A.alloc(128) for _ in range(4)]
        self.cwt = A.alloc(4 * NCT)
        self.cbt = A.alloc(NCT)
        self.scv = A.alloc(1, rows=KH)
        self.w0v = A.alloc(1)
        self.w1v = A.alloc(1)
        self.w2nv = A.alloc(1)
        self.g2v = A.alloc(1)
        self.g2h = A.alloc(1)
        self.cav = [A.alloc(8) for _ in range(2)]
        self.sav = [A.alloc(8) for _ in range(2)]
        self.cbv = [A.alloc(8) for _ in range(2)]
        self.sbv = [A.alloc(8) for _ in range(2)]
        self.cbsxv = [A.alloc(8) for _ in range(2)]
        self.sbsxv = [A.alloc(8) for _ in range(2)]
        self.swv = A.alloc(32)
        self.swov = A.alloc(32)
        self.iswx = A.alloc(32)
        self.vrv = A.alloc(32)
        self.biT = A.alloc(32)
        self.sxv = A.alloc(1)
        self.shv = A.alloc(1)
        self.amaxh = A.alloc(1)
        self.amaxd = A.alloc(1)
        self.am8 = A.alloc(32)
        self.stats = A.alloc(2 * (B_PER_CORE * L // 128))
        self.carry = A.alloc(33)
        self.tails = A.alloc(3 * NCT)
        self.onesr = A.alloc(128, rows=1)
        self.rowt = A.alloc(D, rows=2)
        self.amt = A.alloc(1)
        self.amt2 = A.alloc(1)
        self.amt3 = A.alloc(1)
        self.consts = {}
        for val in (1e-5, -0.5, 1.0 / 128, 1e-4, 1.0, -2.0):
            t = A.alloc(1)
            self.nc.const_aps.register(F32, val, t[:, 0:1])
            self.consts[val] = t
        self.res_end = A.off
        self.gaB = A.alloc(D)
        self.WAhi = A.alloc(8 * TOTAL, BF16)
        self.WAlo = A.alloc(8 * TOTAL, BF16)
        self.WOhi = A.alloc(8 * D, BF16)
        self.WOlo = A.alloc(8 * D, BF16)
        self.resA_end = A.off

    def prologue(self):
        A = Arena(self.arena_t.ap(), self.AR, "prol", self.resA_end)
        nc, L = self.nc, self.L
        em, wait, dma, waitf = self.em, self.wait, self.dma, self.waitf
        sv, ss, st_, sg = self.sv, self.ss, self.st_, self.sg
        dW = dG = None

        wrem = [A.alloc(KH, rows=32) for _ in range(4)]
        wimm = [A.alloc(KH, rows=32) for _ in range(4)]
        ns1m = [A.alloc(1, rows=32) for _ in range(4)]
        ns2m = [A.alloc(1, rows=32) for _ in range(4)]
        wef0 = A.alloc(KH, rows=32)
        wef1 = A.alloc(KH, rows=32)
        wef2 = A.alloc(KH, rows=32)
        ef32 = A.alloc(128, rows=32)
        rowv32 = A.alloc(1, rows=32)
        posb = A.alloc(L, rows=64)
        dsl28 = A.alloc(1, rows=28)
        asl4_t = A.alloc(1, rows=36)
        asl4 = asl4_t[32:36, 0:1]
        spl28 = A.alloc(1, rows=28)
        bias28 = A.alloc(1, rows=28)
        spla4_t = A.alloc(1, rows=36)
        spla4 = spla4_t[32:36, 0:1]
        der3 = A.alloc(3, rows=1)
        sct2 = A.alloc(D, rows=2)
        scT = A.alloc(16)
        modsb = A.alloc(6 * D, rows=2)
        tmp1 = A.alloc(1)
        wmb = [A.buf(512) for _ in range(3)]
        wqb = [A.buf(D) for _ in range(2)]
        qtmp = A.alloc(D)
        qtb = qtmp.bitcast(BF16)
        stage = A.buf(D, BF16)
        self.prol_end = A.off

        loads = [(self.cwt, self.cw_h[:, :]), (self.cbt, self.cb_h[:, :]),
                 (dsl28, self.dsl_h[:, :]), (asl4, self.asl_h[:, :]),
                 (self.scv, self.sc_h[:, :]), (der3, self.dlg_h[:, :]),
                 (self.biT, self.bi_h[:, :]),
                 (sct2, self.c_h[:, :])]
        for m in range(4):
            loads += [(wrem[m], self.wre_h.rearrange("(h m) j -> h m j", m=4)[:, m, :]),
                      (wimm[m], self.wim_h.rearrange("(h m) j -> h m j", m=4)[:, m, :]),
                      (ns1m[m], self.ns_h.rearrange("(h m) c -> h m c", m=4)[:, m, 0:1]),
                      (ns2m[m], self.ns_h.rearrange("(h m) c -> h m c", m=4)[:, m, 1:2])]
        tok_loads = [dma("sync", None, dst, src, slow=True) for dst, src in loads]
        t_th = dma("gpsimd", None, self.bc_h[9:10, 0:2], self.th_h[:, :])
        waitf("gpsimd", t_th)
        t_g2 = dma("gpsimd", None, self.g2v, self.bc_h[9:10, 0:1].partition_broadcast(128))

        em("gpsimd", lambda e: e.iota(self.iof1, pattern=[[1, 128]], base=0,
                                      channel_multiplier=0, allow_small_or_imprecise_dtypes=True))
        em("gpsimd", lambda e: e.iota(self.iop, pattern=[[0, 1]], base=0,
                                      channel_multiplier=1, allow_small_or_imprecise_dtypes=True))
        em("gpsimd", lambda e: e.iota(ef32, pattern=[[1, 128]], base=0,
                                      channel_multiplier=0, allow_small_or_imprecise_dtypes=True))
        em("gpsimd", lambda e: e.iota(rowv32, pattern=[[0, 1]], base=0,
                                      channel_multiplier=1, allow_small_or_imprecise_dtypes=True))
        em("gpsimd", lambda e: e.iota(posb, pattern=[[1, L]], base=0,
                                      channel_multiplier=0, allow_small_or_imprecise_dtypes=True),
           inc=sg)
        v_iota = sg.n

        wait("vector", sg, v_iota)
        TS = ALU
        for val, t in self.consts.items():
            em("vector", lambda e, t=t, val=val: e.memset(t, float(val)))
        em("vector", lambda e: e.tensor_scalar(out=self.ident, in0=self.iof1, scalar1=self.iop,
                                               scalar2=None, op0=TS.is_equal, op1=TS.bypass))
        em("vector", lambda e: e.tensor_copy(out=self.identb, in_=self.ident))
        em("vector", lambda e: e.memset(self.zrow, 0.0))
        em("vector", lambda e: e.memset(self.onesr, 1.0))
        em("vector", lambda e: e.memset(self.amaxh, 0.0))
        em("vector", lambda e: e.memset(self.amaxd, 0.0))
        em("vector", lambda e: e.tensor_scalar(out=self.k4v, in0=self.iop, scalar1=1.0 / 32,
                                               scalar2=-0.484375, op0=TS.mult, op1=TS.add))
        em("vector", lambda e: e.drain())
        em("vector", lambda e: e.tensor_scalar(out=self.k4v, in0=self.k4v, scalar1=RC,
                                               scalar2=RC, op0=TS.add, op1=TS.subtract))
        em("vector", lambda e: e.drain())
        for g in range(8):
            em("vector", lambda e, g=g: e.tensor_scalar(out=tmp1, in0=self.k4v, scalar1=1.0,
                                                        scalar2=float(4 * g), op0=TS.mult, op1=TS.add))
            em("vector", lambda e: e.drain())
            em("vector", lambda e, g=g: e.tensor_scalar(out=self.selg[g], in0=self.iof1[:, 0:KH],
                                                        scalar1=tmp1, scalar2=None,
                                                        op0=TS.is_equal, op1=TS.bypass))
        em("vector", lambda e: e.tensor_scalar(out=ef32, in0=ef32, scalar1=1.0 / 32,
                                               scalar2=-0.484375, op0=TS.mult, op1=TS.add))
        em("vector", lambda e: e.drain())
        em("vector", lambda e: e.tensor_scalar(out=ef32, in0=ef32, scalar1=RC,
                                               scalar2=RC, op0=TS.add, op1=TS.subtract))
        em("vector", lambda e: e.drain())
        for g in range(8):
            em("vector", lambda e, g=g: e.tensor_scalar(out=self.Ee[g], in0=ef32, scalar1=1.0,
                                                        scalar2=float(4 * g), op0=TS.mult, op1=TS.add))
            em("vector", lambda e: e.drain())
            em("vector", lambda e, g=g: e.tensor_scalar(out=self.Ee[g], in0=self.Ee[g], scalar1=rowv32,
                                                        scalar2=None, op0=TS.is_equal, op1=TS.bypass))
        waitf("vector", t_g2)
        em("vector", lambda e: e.tensor_scalar(out=self.g2h, in0=self.g2v, scalar1=0.5,
                                               scalar2=None, op0=TS.mult, op1=TS.bypass))

        for t_ in tok_loads:
            waitf("vector", t_)
        place_toks = []
        for X, (wsrc, nssrc, ma, mb, sub) in enumerate([
                (wrem, ns1m, 0, 3, False), (wrem, ns1m, 1, 2, False),
                (wimm, ns2m, 2, 1, True), (wimm, ns2m, 3, 0, True)]):
            for t_ in place_toks:
                waitf("vector", t_)
            place_toks = []
            em("vector", lambda e, w=wsrc[ma], n=nssrc[ma]: e.tensor_scalar(
                out=wef0, in0=w, scalar1=n, scalar2=None, op0=TS.mult, op1=TS.bypass))
            em("vector", lambda e: e.drain())
            em("vector", lambda e, w=wsrc[mb], n=nssrc[mb]: e.tensor_scalar(
                out=wef1, in0=w, scalar1=n, scalar2=None, op0=TS.mult, op1=TS.bypass))
            em("vector", lambda e: e.drain())
            op = TS.subtract if sub else TS.add
            em("vector", lambda e, op=op: e.tensor_tensor(out=wef2, in0=wef0, in1=wef1,
                                                          op=op))
            v = em("vector", lambda e, X=X: e.memset(self.wdiag[X], 0.0), inc=sv)
            wait("gpsimd", sv, v)
            for r in range(4):
                place_toks.append(dma("gpsimd", None,
                                      self.wdiag[X][32 * r:32 * r + 32, 32 * r:32 * r + 32], wef2))

        for t_ in tok_loads:
            waitf("scalar", t_)
        wait("scalar", sg, v_iota)
        em("scalar", lambda e: e.activation(out=spl28, in_=dsl28, func=AF.Exp))
        em("scalar", lambda e: e.activation(out=spl28, in_=spl28, func=AF.Ln, bias=1.0))
        em("scalar", lambda e: e.activation(out=spla4, in_=asl4, func=AF.Exp))
        v = em("scalar", lambda e: e.activation(out=spla4, in_=spla4, func=AF.Ln, bias=1.0), inc=ss)
        wait("vector", ss, v)
        em("vector", lambda e: e.tensor_scalar(out=bias28, in0=spl28, scalar1=float(-(L - 1)),
                                               scalar2=None, op0=TS.mult, op1=TS.bypass))
        v = em("vector", lambda e: e.tensor_scalar(out=spla4, in0=spla4, scalar1=-1.0,
                                                   scalar2=None, op0=TS.mult, op1=TS.bypass), inc=sv)
        wait("scalar", sv, v)
        em("scalar", lambda e: e.activation(out=posb[0:28, :], in_=posb[0:28, :], func=AF.Exp,
                                            scale=spl28, bias=bias28))
        v = em("scalar", lambda e: e.activation(out=posb[32:36, :], in_=posb[32:36, :], func=AF.Exp,
                                                scale=spla4), inc=ss)
        wait("gpsimd", ss, v)
        dma("gpsimd", None, self.tw_h[0:28, :], posb[0:28, :])
        dma("gpsimd", None, self.tw_h[28:32, :], posb[32:36, :])

        der128 = A.alloc(4)
        der2 = A.alloc(4)
        t_der = dma("gpsimd", None, der128[:, 0:3], self.dlg_h[0:1, :].partition_broadcast(128))
        waitf("scalar", t_der)
        v = em("scalar", lambda e: e.activation(out=der128[:, 0:3], in_=der128[:, 0:3],
                                                func=AF.Exp), inc=ss)
        wait("vector", ss, v)
        em("vector", lambda e: e.tensor_reduce(out=der128[:, 3:4], in_=der128[:, 0:3],
                                               axis=AX.X, op=TS.add))
        em("vector", lambda e: e.drain())
        em("vector", lambda e: e.reciprocal(out=der128[:, 3:4], in_=der128[:, 3:4]))
        em("vector", lambda e: e.drain())
        em("vector", lambda e: e.tensor_scalar(out=self.w0v, in0=der128[:, 0:1],
                                               scalar1=der128[:, 3:4], scalar2=None,
                                               op0=TS.mult, op1=TS.bypass))
        em("vector", lambda e: e.tensor_scalar(out=self.w1v, in0=der128[:, 1:2],
                                               scalar1=der128[:, 3:4], scalar2=None,
                                               op0=TS.mult, op1=TS.bypass))
        em("vector", lambda e: e.tensor_scalar(out=der2[:, 0:1], in0=der128[:, 3:4],
                                               scalar1=-1.0, scalar2=None,
                                               op0=TS.mult, op1=TS.bypass))
        em("vector", lambda e: e.drain())
        em("vector", lambda e: e.tensor_scalar(out=self.w2nv, in0=der128[:, 2:3],
                                               scalar1=der2[:, 0:1], scalar2=None,
                                               op0=TS.mult, op1=TS.bypass))
        em("vector", lambda e: e.drain())
        self.dbg("softrow", der128)
        self.dbg("w2n_early", self.w2nv)

        for t_ in tok_loads:
            waitf("scalar", t_)
        v = em("scalar", lambda e: e.activation(out=sct2, in_=sct2, func=AF.Silu), inc=ss)
        wait("tensor", ss, v)
        wait("tensor", sv, sv.n)
        b0 = self.bank(0, "tensor")
        for j in range(8):
            em("tensor", lambda e, j=j, b0=b0: e.transpose(b0[0:128, 2 * j:2 * j + 2],
                                                           sct2[0:2, 128 * j:128 * j + 128],
                                                           self.ident[0:2, 0:2]),
               inc=(st_ if j == 7 else None))
        wait("scalar", st_, st_.n)
        v = em("scalar", lambda e, b0=b0: e.copy(out=scT, in_=b0[0:128, 0:16]), inc=ss)
        self.bank_consumed(0, ss, v)
        v_scT = ss.n
        self.dbg("scT", scT)
        modsbT = A.alloc(96)
        mview = modsbT.rearrange("p (c b) -> p c b", b=2)
        for cg in range(12):
            bis = [1 + q for q in range(4)]
            bks = [self.bank(bi, "tensor") for bi in bis]
            for k in range(8):
                bu = wmb[(cg * 8 + k) % 3]
                self.waitf("sync", bu.free)
                t_ = dma("sync", None, bu.v,
                         self.wm_h[128 * k:128 * k + 128, 512 * cg:512 * cg + 512])
                waitf("tensor", t_)
                if cg == 0 and k == 0:
                    wait("tensor", ss, v_scT)
                for q in range(4):
                    v = em("tensor", lambda e, bk=bks[q], k=k, bu=bu, q=q: e.matmul(
                        bk[0:128, 0:2], bu.v[:, 128 * q:128 * q + 128], scT[:, 2 * k:2 * k + 2],
                        start=(k == 0), stop=(k == 7)),
                        inc=(st_ if q == 3 else None))
                bu.free = (st_, st_.n)
            wait("scalar", st_, st_.n)
            for q in range(4):
                ctblk = 4 * cg + q
                v = em("scalar", lambda e, bk=bks[q], ctblk=ctblk: e.copy(
                    out=modsbT[:, 2 * ctblk:2 * ctblk + 2], in_=bk[0:128, 0:2]), inc=ss)
                self.bank_consumed(bis[q], ss, v)
        v_mod = ss.n
        self.dbg("modsbT", modsbT)

        wait("vector", ss, v_mod)
        t_gprev = None
        gacol = A.alloc(8)
        gacol2 = A.alloc(8)
        gacol3 = A.alloc(8)
        gacol4 = A.alloc(8)
        gacol5 = A.alloc(8)
        for b in range(2):
            for (dst, blk0, plus1) in ((self.cav[b], 8, True), (self.cbv[b], 32, True),
                                       (self.sav[b], 0, False), (self.sbv[b], 24, False)):
                em("vector", lambda e, dst=dst, blk0=blk0, p1=plus1, b=b: e.tensor_scalar(
                    out=dst, in0=mview[:, blk0:blk0 + 8, b],
                    scalar1=(1.0 if p1 else 0.0), scalar2=None, op0=TS.add, op1=TS.bypass))
            for (blk0, row) in ((16, b), (40, 2 + b)):
                em("vector", lambda e, b=b, blk0=blk0: e.tensor_scalar(
                    out=gacol2, in0=mview[:, blk0:blk0 + 8, b], scalar1=0.0,
                    scalar2=None, op0=TS.is_ge, op1=TS.bypass))
                em("vector", lambda e: e.drain())
                em("vector", lambda e: e.tensor_scalar(
                    out=gacol, in0=gacol2, scalar1=2.0, scalar2=-1.0,
                    op0=TS.mult, op1=TS.add))
                v = em("vector", lambda e: e.drain(), inc=sv)
                wait("scalar", sv, v)
                v = em("scalar", lambda e, b=b, blk0=blk0: e.activation(
                    out=gacol5, in_=mview[:, blk0:blk0 + 8, b], func=AF.Abs), inc=ss)
                wait("vector", ss, v)
                em("vector", lambda e: e.tensor_scalar(
                    out=gacol4, in0=gacol5, scalar1=0.25, scalar2=None,
                    op0=TS.max, op1=TS.bypass))
                em("vector", lambda e: e.drain())
                waitf("vector", t_gprev)
                em("vector", lambda e: e.tensor_tensor(
                    out=gacol3, in0=gacol, in1=gacol4, op=TS.mult))
                v = em("vector", lambda e: e.drain(), inc=sv)
                wait("gpsimd", sv, v)
                t_gprev = dma("gpsimd", None, self.bc_h[row:row + 1, :].rearrange(
                    "o (j p) -> o p j", p=128), gacol3, slow=True)

        for (w_h, out_h, svec, do_T) in ((self.wiT_h, self.wqi_h, self.swv, True),
                                         (self.wo_h, self.wqo_h, self.swov, False)):
            for s in range(32):
                bu = wqb[s % 2]
                self.waitf("sync", bu.free)
                t_ = dma("sync", None, bu.v, w_h[128 * s:128 * s + 128, :])
                waitf("vector", t_)
                waitf("vector", stage.free)
                em("vector", lambda e, bu=bu: e.tensor_reduce(
                    out=self.amt, in_=bu.v, axis=AX.X, op=TS.max, apply_absolute_value=True))
                em("vector", lambda e: e.drain())
                em("vector", lambda e: e.tensor_scalar(
                    out=self.amt2, in0=self.amt, scalar1=1e-8, scalar2=None, op0=TS.max, op1=TS.bypass))
                em("vector", lambda e: e.drain())
                em("vector", lambda e: e.reciprocal(out=self.amt3, in_=self.amt2))
                em("vector", lambda e: e.drain())
                em("vector", lambda e, svec=svec, s=s: e.tensor_scalar(
                    out=svec[:, s:s + 1], in0=self.amt3, scalar1=127.0, scalar2=None,
                    op0=TS.mult, op1=TS.bypass))
                em("vector", lambda e: e.drain())
                if s == 0 and do_T:
                    self.dbg("wslab0", bu.v)
                    self.dbg("amt0", self.amt3)
                em("vector", lambda e, bu=bu, svec=svec, s=s: e.tensor_scalar(
                    out=qtmp, in0=bu.v, scalar1=svec[:, s:s + 1], scalar2=RC,
                    op0=TS.mult, op1=TS.add))
                em("vector", lambda e: e.drain())
                em("vector", lambda e: e.tensor_scalar(
                    out=stage.v, in0=qtmp, scalar1=RC, scalar2=None,
                    op0=TS.subtract, op1=TS.bypass))
                v = em("vector", lambda e: e.drain(), inc=sv)
                bu.free = (sv, v)
                if not do_T:
                    wait("gpsimd", sv, v)
                    if s == 0:
                        self.dbg("woslab0", bu.v)
                        self.dbg("wostage0", stage.v)
                        self.dbg("wosvec", svec)
                    stage.free = dma("gpsimd", None, out_h[128 * s:128 * s + 128, :], stage.v)
                else:
                    wait("tensor", sv, v)
                    bk = self.bank(3, "tensor")
                    bkb = self.psum[3][:, :].bitcast(BF16)
                    for j in range(8):
                        em("tensor", lambda e, j=j, bkb=bkb: e.transpose(
                            bkb[:, 128 * j:128 * j + 128], stage.v[:, 128 * j:128 * j + 128],
                            self.identb), inc=(st_ if j == 7 else None))
                    vT = st_.n
                    stage.free = (st_, vT)
                    wait("scalar", st_, vT)
                    wait("scalar", sv, sv.n)
                    v2 = em("scalar", lambda e, bkb=bkb: e.copy(out=qtb[:, 0:1024], in_=bkb),
                            inc=ss)
                    self.bank_consumed(3, ss, v2)
                    wait("gpsimd", ss, v2)
                    qtoks = [dma("gpsimd", None,
                                 out_h[128 * j:128 * j + 128, 128 * s:128 * s + 128],
                                 qtb[:, 128 * j:128 * j + 128]) for j in range(8)]
                    for t_ in qtoks:
                        waitf("vector", t_)

        for (w_h, ncol, hi, lo) in ((self.wA_h, TOTAL, self.WAhi, self.WAlo),
                                    (self.wO_h, D, self.WOhi, self.WOlo)):
            for k in range(8):
                parts = [(o, min(1024, ncol - o)) for o in range(0, ncol, 1024)]
                for (o0, w) in parts:
                    bu = wqb[(k + o0) % 2]
                    self.waitf("sync", bu.free)
                    t_ = dma("sync", None, bu.v[:, 0:w], w_h[128 * k:128 * k + 128, o0:o0 + w])
                    waitf("scalar", t_)
                    vhi = em("scalar", lambda e, bu=bu, hi=hi, c0=ncol * k + o0, w=w: e.copy(
                        out=hi[:, c0:c0 + w], in_=bu.v[:, 0:w]), inc=ss)
                    wait("vector", ss, vhi)
                    v = em("vector", lambda e, bu=bu, hi=hi, lo=lo, c0=ncol * k + o0, w=w:
                           e.tensor_tensor(out=lo[:, c0:c0 + w], in0=bu.v[:, 0:w],
                                           in1=hi[:, c0:c0 + w], op=TS.subtract), inc=sv)
                    bu.free = (sv, v)

        self.dbg("cav0", self.cav[0])
        self.dbg("sav0", self.sav[0])
        self.dbg("cbv0", self.cbv[0])
        self.dbg("sbv0", self.sbv[0])
        self.dbg("swv", self.swv)
        self.dbg("swov", self.swov)
        self.dbg("w012", self.w0v)
        self.dbg("w1", self.w1v)
        self.dbg("w2n", self.w2nv)
        self.dbg("g2", self.g2v)
        self.dbg("wdiag0", self.wdiag[0])
        self.dbg("selg1", self.selg[1])
        self.dbg("Ee1", self.Ee[1])
        self.dbg("WAhi", self.WAhi[:, 0:TOTAL])
        self.dbg("WAlo", self.WAlo[:, 0:TOTAL])
        self.barrier()

    def alloc_A(self):
        A = Arena(self.arena_t.ap(), self.AR, "A", self.resA_end)
        TA = self.TA
        self.x_tm = [A.buf(D) for _ in range(2)]
        self.t_tm = [A.buf(D) for _ in range(2)]
        self.t2_tm = A.buf(D)
        self.t_fm = A.buf(8 * TA)
        self.t2fm = self.t_fm
        self.sq_scr = A.alloc(D)
        self.lnrow = A.alloc(12)
        self.twc = A.buf(TA, rows=KH)
        self.ha_hi = A.buf(8 * TA, BF16)
        self.ha_lo = A.buf(8 * TA, BF16)
        self.ha_t = A.alloc(TA)
        self.haf = A.buf(8 * TA)
        self.zx = [A.buf(TA + 3) for _ in range(3)]
        self.ctmp = [A.alloc(TA) for _ in range(3)]
        self.xv_u = [A.buf(TA) for _ in range(2)]
        self.uu = [A.buf(TA) for _ in range(2)]
        self.silu_g = A.buf(8 * TA)
        self.sraw_u = A.buf(TA, rows=KH)
        self.tmpP = [A.alloc(TA) for _ in range(2)]
        self.s1t = [A.buf(TA) for _ in range(2)]
        self.c1t = [A.buf(TA) for _ in range(2)]
        self.s3t = [A.buf(TA) for _ in range(2)]
        self.c3t = [A.buf(TA) for _ in range(2)]
        self.p1t = A.alloc(TA)
        self.polyt = [A.buf(TA) for _ in range(2)]
        self.ppt = [A.buf(TA) for _ in range(2)]
        self.scin = [[A.buf(TA) for _ in range(4)] for _ in range(2)]
        self.scout = [[A.buf(TA) for _ in range(4)] for _ in range(2)]
        self.sqt = [A.buf(TA) for _ in range(2)]
        self.pwt = A.buf(TA, rows=KH)
        self.dent = A.buf(TA, rows=KH)
        self.g32 = [A.alloc(TA, rows=KH) for _ in range(3)]
        self.gfin = A.buf(TA, rows=KH)
        self.pwE_sb = [A.buf(TA) for _ in range(2)]
        self.y_raw = A.buf(8 * TA)
        self.gft = [A.alloc(TA) for _ in range(2)]
        self.g_hi = A.buf(8 * TA, BF16)
        self.g_lo = A.buf(8 * TA, BF16)
        self.xres_tm = [A.buf(D) for _ in range(2)]
        self.hbt = A.buf(TA)
        assert A.off <= self.AR, A.off
        self.A_end = A.off

    def _inproj_conv(self, b, ct, dst_kind, par):
        TA = self.TA
        em, wait, dma, waitf = self.em, self.wait, self.dma, self.waitf
        sv, ss, st_ = self.sv, self.ss, self.st_
        TS = ALU
        w = _rows(ct)
        bi = 1 + (ct % 2)
        bk = self.bank(bi, "tensor")
        for k in range(8):
            sW = slice(TOTAL * k + 128 * ct, TOTAL * k + 128 * ct + w)
            sA = slice(TA * k, TA * (k + 1))
            em("tensor", lambda e, bk=bk, sW=sW, sA=sA, w=w, k=k: e.matmul(
                bk[0:w, 0:TA], self.WAhi[:, sW], self.ha_hi.v[:, sA],
                start=(k == 0), stop=False))
            em("tensor", lambda e, bk=bk, sW=sW, sA=sA, w=w: e.matmul(
                bk[0:w, 0:TA], self.WAhi[:, sW], self.ha_lo.v[:, sA],
                start=False, stop=False))
            em("tensor", lambda e, bk=bk, sW=sW, sA=sA, w=w, k=k: e.matmul(
                bk[0:w, 0:TA], self.WAlo[:, sW], self.ha_hi.v[:, sA],
                start=False, stop=(k == 7)), inc=(st_ if k == 7 else None))
        vmm = st_.n
        zb = self.zx[ct % 3]
        self.waitf("scalar", zb.free)
        wait("scalar", st_, vmm)
        vz = em("scalar", lambda e, zb=zb, bk=bk, w=w: e.copy(
            out=zb.v[0:w, 3:3 + TA], in_=bk[0:w, 0:TA]), inc=ss)
        self.bank_consumed(bi, ss, vz)
        if getattr(self, "_tap16", False) and ct == 16:
            self.dbg("z16", zb.v[0:32, :])
        wait("vector", ss, vz)
        em("vector", lambda e, zb=zb, ct=ct, w=w: e.tensor_copy(
            out=zb.v[0:w, 0:3], in_=self.tails[0:w, 3 * ct:3 * ct + 3]))
        c_a, c_b = self.ctmp[0], self.ctmp[1]
        em("vector", lambda e, zb=zb, ct=ct, w=w, c_a=c_a: e.tensor_scalar(
            out=c_a[0:w, :], in0=zb.v[0:w, 0:TA], scalar1=self.cwt[0:w, ct:ct + 1],
            scalar2=None, op0=TS.mult, op1=TS.bypass))
        em("vector", lambda e, zb=zb, ct=ct, w=w, c_a=c_a, c_b=c_b: e.scalar_tensor_tensor(
            out=c_b[0:w, :], in0=zb.v[0:w, 1:1 + TA], scalar=self.cwt[0:w, NCT + ct:NCT + ct + 1],
            in1=c_a[0:w, :], op0=TS.mult, op1=TS.add))
        em("vector", lambda e, zb=zb, ct=ct, w=w, c_a=c_a, c_b=c_b: e.scalar_tensor_tensor(
            out=c_a[0:w, :], in0=zb.v[0:w, 2:2 + TA], scalar=self.cwt[0:w, 2 * NCT + ct:2 * NCT + ct + 1],
            in1=c_b[0:w, :], op0=TS.mult, op1=TS.add))
        lastw = self.cwt[0:w, 3 * NCT + ct:3 * NCT + ct + 1]
        if dst_kind == "xv":
            dst = self.xv_u[par]
            self.waitf("vector", dst.free)
            vconv = em("vector", lambda e, zb=zb, w=w, c_a=c_a, dst=dst, lastw=lastw:
                       e.scalar_tensor_tensor(out=dst.v[0:w, :], in0=zb.v[0:w, 3:3 + TA],
                                              scalar=lastw, in1=c_a[0:w, :],
                                              op0=TS.mult, op1=TS.add), inc=sv)
            dst.ready = (sv, vconv)
        elif dst_kind == "gate":
            dst = self.ctmp[2]
            self.waitf("vector", getattr(self, "ctmp2_free", None))
            vconv = em("vector", lambda e, zb=zb, w=w, c_a=c_a, dst=dst, lastw=lastw:
                       e.scalar_tensor_tensor(out=dst[0:w, :], in0=zb.v[0:w, 3:3 + TA],
                                              scalar=lastw, in1=c_a[0:w, :],
                                              op0=TS.mult, op1=TS.add), inc=sv)
            wait("scalar", sv, vconv)
            self.waitf("scalar", self.silu_g.free)
            vsl = em("scalar", lambda e, ct=ct, dst=dst: e.activation(
                out=self.silu_g.v[:, TA * (ct - 8):TA * (ct - 7)], in_=dst,
                func=AF.Silu, bias=self.cbt[:, ct:ct + 1]), inc=ss)
            self.ctmp2_free = (ss, vsl)
            dst = None
        else:
            dst = self.sraw_u
            self.waitf("vector", dst.free)
            vconv = em("vector", lambda e, zb=zb, w=w, c_a=c_a, dst=dst, lastw=lastw:
                       e.scalar_tensor_tensor(out=dst.v, in0=zb.v[0:w, 3:3 + TA],
                                              scalar=lastw, in1=c_a[0:w, :],
                                              op0=TS.mult, op1=TS.add), inc=sv)
            dst.ready = (sv, vconv)
        vtl = em("vector", lambda e, zb=zb, ct=ct, w=w: e.tensor_copy(
            out=self.tails[0:w, 3 * ct:3 * ct + 3], in_=zb.v[0:w, TA:TA + 3]), inc=sv)
        zb.free = (sv, vtl)
        return dst

    def a_chunk(self, b, ci, itile):
        TA = self.TA
        t0 = ci * TA
        em, wait, dma, waitf = self.em, self.wait, self.dma, self.waitf
        sv, ss, st_, sg = self.sv, self.ss, self.st_, self.sg
        TS = ALU

        if ci == 0:
            em("vector", lambda e: e.memset(self.tails, 0.0))
            em("vector", lambda e: e.memset(self.carry, 0.0))
            wait("gpsimd", sv, sv.n)
            self.t_gaB = dma("gpsimd", None, self.gaB,
                             self.bc_h[b:b + 1, :].partition_broadcast(128))
        t_gaB = self.t_gaB

        xb = self.x_tm[ci % 2]
        self.waitf("sync", xb.free)
        t_x = dma("sync", None, xb.v, self.x_h[b, t0:t0 + TA, :])
        self.waitf("sync", self.twc.free)
        t_twc = dma("sync", None, self.twc.v[0:KH, :], self.tw_h[:, t0:t0 + TA])

        waitf("scalar", t_x)
        self.waitf("scalar", getattr(self, "lnrow_free", None))
        v = em("scalar", lambda e, xb=xb: e.activation(out=self.sq_scr, in_=xb.v,
                                                       func=AF.Square), inc=ss)
        waitf("vector", t_x)
        wait("vector", ss, v)
        em("vector", lambda e: e.tensor_reduce(out=self.lnrow[:, 1:2], in_=self.sq_scr,
                                               axis=AX.X, op=TS.add))
        em("vector", lambda e: e.drain())
        em("vector", lambda e, xb=xb: e.tensor_reduce(out=self.lnrow[:, 0:1], in_=xb.v,
                                                      axis=AX.X, op=TS.add))
        em("vector", lambda e: e.drain())
        em("vector", lambda e: e.tensor_scalar(out=self.lnrow[:, 0:1], in0=self.lnrow[:, 0:1],
                                               scalar1=1.0 / D, scalar2=None, op0=TS.mult, op1=TS.bypass))
        em("vector", lambda e: e.drain())
        em("vector", lambda e: e.tensor_tensor(out=self.lnrow[:, 2:3], in0=self.lnrow[:, 0:1],
                                               in1=self.lnrow[:, 0:1], op=TS.mult))
        em("vector", lambda e: e.drain())
        v = em("vector", lambda e: e.scalar_tensor_tensor(out=self.lnrow[:, 3:4], in0=self.lnrow[:, 1:2],
                                                          scalar=1.0 / D, in1=self.lnrow[:, 2:3],
                                                          op0=TS.mult, op1=TS.subtract), inc=sv)
        wait("scalar", sv, v)
        em("scalar", lambda e: e.activation(out=self.lnrow[:, 4:5], in_=self.lnrow[:, 3:4],
                                            func=AF.Ln, bias=1e-5))
        v = em("scalar", lambda e: e.activation(out=self.lnrow[:, 4:5], in_=self.lnrow[:, 4:5],
                                                func=AF.Exp, scale=-0.5), inc=ss)
        tb = self.t_tm[ci % 2]
        self.waitf("vector", tb.free)
        wait("vector", ss, v)
        if b == 0 and ci == 0:
            self.dbg("lnrow", self.lnrow)
            self.dbg("sqscr", self.sq_scr)
            self.dbg("xb0", xb.v)
        vt = em("vector", lambda e, xb=xb, tb=tb: e.tensor_scalar(
            out=tb.v, in0=xb.v, scalar1=self.lnrow[:, 0:1], scalar2=self.lnrow[:, 4:5],
            op0=TS.subtract, op1=TS.mult), inc=sv)

        wait("tensor", sv, vt)
        self.waitf("tensor", self.t_fm.free)
        for r in range(2):
            bk = self.bank(0, "tensor")
            for j in range(4):
                jj = 4 * r + j
                em("tensor", lambda e, bk=bk, j=j, jj=jj, tb=tb: e.transpose(
                    bk[:, 128 * j:128 * j + 128], tb.v[:, 128 * jj:128 * jj + 128], self.ident),
                   inc=(st_ if j == 3 else None))
            wait("scalar", st_, st_.n)
            v = em("scalar", lambda e, bk=bk, r=r: e.copy(
                out=self.t_fm.v[:, 512 * r:512 * r + 512], in_=bk[:, :]), inc=ss)
            self.bank_consumed(0, ss, v)
        tb.free = (self.st_, self.st_.n)
        v_tfm = ss.n
        if b == 0 and ci == 0:
            self.dbg("tfm", self.t_fm.v)

        wait("vector", ss, v_tfm)
        self.waitf("vector", self.ha_hi.free)
        self.waitf("vector", self.ha_lo.free)
        self.waitf("vector", self.haf.free)
        for j in range(8):
            sl = slice(TA * j, TA * (j + 1))
            em("vector", lambda e, j=j, sl=sl: e.tensor_scalar(
                out=self.haf.v[:, sl], in0=self.t_fm.v[:, sl], scalar1=self.cav[b][:, j:j + 1],
                scalar2=self.sav[b][:, j:j + 1], op0=TS.mult, op1=TS.add))
        vf = em("vector", lambda e: e.drain(),
                inc=sv)
        wait("scalar", sv, vf)
        vhi = em("scalar", lambda e: e.copy(out=self.ha_hi.v, in_=self.haf.v), inc=ss)
        wait("vector", ss, vhi)
        vha = em("vector", lambda e: e.tensor_tensor(
            out=self.ha_lo.v, in0=self.haf.v, in1=self.ha_hi.v, op=TS.subtract), inc=sv)
        self.haf.free = (sv, vha)
        self.t_fm.free = (sv, vf)
        wait("tensor", sv, vha)
        wait("tensor", ss, vhi)

        self._tap16 = (b == 0 and ci == 0)
        self._inproj_conv(b, 16, "sraw", 0)
        self._tap16 = False
        em("vector", lambda e: e.tensor_scalar(
            out=self.g32[0][0:KH, :], in0=self.sraw_u.v[0:KH, :],
            scalar1=self.cbt[0:KH, 16:17], scalar2=self.scv,
            op0=TS.add, op1=TS.mult))
        v = em("vector", lambda e: e.tensor_scalar(
            out=self.g32[0][0:KH, :], in0=self.g32[0][0:KH, :], scalar1=20.0, scalar2=-20.0,
            op0=TS.min, op1=TS.max), inc=sv)
        self.sraw_u.free = (sv, v)
        wait("scalar", sv, v)
        self.waitf("scalar", self.pwt.free)
        v = em("scalar", lambda e: e.activation(
            out=self.pwt.v[0:KH, :], in_=self.g32[0][0:KH, :], func=AF.Exp), inc=ss)
        wait("vector", ss, v)
        waitf("vector", t_twc)
        em("vector", lambda e: e.tensor_tensor(
            out=self.pwt.v[0:KH, :], in0=self.pwt.v[0:KH, :],
            in1=self.twc.v[0:KH, :], op=TS.mult))
        self.waitf("vector", self.dent.free)
        em("vector", lambda e: e.tensor_tensor_scan(
            self.dent.v[0:KH, :], self.zrow[0:KH, 0:TA], self.pwt.v[0:KH, :],
            self.carry[0:KH, 32:33], TS.add, TS.add))
        em("vector", lambda e: e.drain())
        v_pw = em("vector", lambda e: e.tensor_copy(
            out=self.carry[0:KH, 32:33], in_=self.dent.v[0:KH, TA - 1:TA]), inc=sv)
        self.twc.free = (sv, v_pw)
        if b == 0 and ci == 0:
            self.dbg("sraw", self.sraw_u.v[0:KH, :])
            self.dbg("pwt", self.pwt.v[0:KH, :])
            self.dbg("dent", self.dent.v[0:KH, :])

        self.bank(3, "tensor")
        gstate = {}

        def stage1a(g):
            par = g % 2
            st = {}
            ub = self._inproj_conv(b, g, "xv", par)
            aux = 4 + par
            bk = self.bank(aux, "tensor")
            wait("tensor", sv, v_pw)
            v = em("tensor", lambda e, bk=bk, g=g: e.matmul(
                bk[:, 0:TA], self.Ee[g], self.pwt.v[0:KH, :], start=True, stop=True), inc=st_)
            st["last_pwE"] = v
            wait("scalar", st_, v)
            pwb = self.pwE_sb[par]
            self.waitf("scalar", pwb.free)
            v = em("scalar", lambda e, bk=bk, pwb=pwb: e.copy(out=pwb.v, in_=bk[:, 0:TA]), inc=ss)
            self.bank_consumed(aux, ss, v)
            st["v_pwE"] = ss.n
            st["pwb"] = pwb
            uub = self.uu[par]
            self.waitf("vector", uub.free)
            wait("vector", sv, ub.ready[1])
            vu = em("vector", lambda e, g=g, ub=ub, uub=uub: e.tensor_scalar(
                out=uub.v, in0=ub.v, scalar1=self.cbt[:, g:g + 1], scalar2=None,
                op0=TS.add, op1=TS.bypass), inc=sv)
            ub.free = (sv, vu)
            st["uub"] = uub
            bufs = (self.s1t[par], self.c1t[par], self.s3t[par], self.c3t[par],
                    self.ppt[par], self.polyt[par])
            st["bufs"] = bufs
            s1b, c1b, s3b, c3b, ppb, pob = bufs
            for bb in bufs:
                self.waitf("scalar", bb.free)
            wait("scalar", sv, vu)
            em("scalar", lambda e, uub=uub, ppb=ppb: e.activation(
                out=ppb.v, in_=uub.v, func=AF.Sin, scale=self.g2h))
            em("scalar", lambda e, uub=uub, s1b=s1b: e.activation(
                out=s1b.v, in_=uub.v, func=AF.Sin, scale=self.g2v))
            em("scalar", lambda e, ppb=ppb: e.activation(out=ppb.v, in_=ppb.v,
                                                         func=AF.Square))
            em("scalar", lambda e, ppb=ppb, c1b=c1b: e.activation(
                out=c1b.v, in_=ppb.v, func=AF.Copy, scale=-2.0, bias=1.0))
            em("scalar", lambda e, s1b=s1b, s3b=s3b: e.activation(
                out=s3b.v, in_=s1b.v, func=AF.Square))
            em("scalar", lambda e, c1b=c1b, c3b=c3b: e.activation(
                out=c3b.v, in_=c1b.v, func=AF.Square))
            v = em("scalar", lambda e, uub=uub, pob=pob: e.activation(
                out=pob.v, in_=uub.v, func=AF.Square), inc=ss)
            st["v_sins"] = v
            gstate[g] = st

        def stage1b(g):
            par = g % 2
            st = gstate[g]
            s1b, c1b, s3b, c3b, ppb, pob = st["bufs"]
            uub, pwb = st["uub"], st["pwb"]
            wait("vector", ss, max(st["v_sins"], st["v_pwE"]))
            sins = self.scin[par]
            for X in range(4):
                self.waitf("vector", sins[X].free)
            em("vector", lambda e, uub=uub: e.tensor_scalar(
                out=self.p1t, in0=uub.v, scalar1=self.w1v, scalar2=self.w0v,
                op0=TS.mult, op1=TS.add))
            em("vector", lambda e, pob=pob: e.scalar_tensor_tensor(
                out=self.tmpP[0], in0=pob.v, scalar=self.w2nv, in1=self.p1t,
                op0=TS.mult, op1=TS.add))
            em("vector", lambda e, ppb=ppb, pwb=pwb: e.tensor_tensor(
                out=ppb.v, in0=self.tmpP[0], in1=pwb.v, op=TS.mult))
            em("vector", lambda e, ppb=ppb, c1b=c1b, d=sins[1]: e.tensor_tensor(
                out=d.v, in0=ppb.v, in1=c1b.v, op=TS.mult))
            em("vector", lambda e, ppb=ppb, s1b=s1b, d=sins[2]: e.tensor_tensor(
                out=d.v, in0=ppb.v, in1=s1b.v, op=TS.mult))
            em("vector", lambda e, s3b=s3b: e.tensor_scalar(
                out=self.tmpP[1], in0=s3b.v, scalar1=-4.0, scalar2=3.0, op0=TS.mult, op1=TS.add))
            em("vector", lambda e, s3b=s3b, s1b=s1b: e.tensor_tensor(
                out=s3b.v, in0=self.tmpP[1], in1=s1b.v, op=TS.mult))
            em("vector", lambda e, ppb=ppb, s3b=s3b, d=sins[3]: e.tensor_tensor(
                out=d.v, in0=ppb.v, in1=s3b.v, op=TS.mult))
            em("vector", lambda e, c3b=c3b: e.tensor_scalar(
                out=self.tmpP[1], in0=c3b.v, scalar1=4.0, scalar2=-3.0, op0=TS.mult, op1=TS.add))
            em("vector", lambda e, c3b=c3b, c1b=c1b: e.tensor_tensor(
                out=c3b.v, in0=self.tmpP[1], in1=c1b.v, op=TS.mult))
            em("vector", lambda e, ppb=ppb, c3b=c3b, d=sins[0]: e.tensor_tensor(
                out=d.v, in0=ppb.v, in1=c3b.v, op=TS.mult))
            vprod = em("vector", lambda e: e.drain(), inc=sv)
            for X in range(4):
                sins[X].ready = vprod
            for bb in st["bufs"]:
                bb.free = (sv, vprod)
            uub.free = (sv, vprod)
            pwb.free = (sv, vprod)

        def stage2(g):
            par = g % 2
            st = gstate[g]
            sins = self.scin[par]
            souts = self.scout[par]
            auxY = 4 + (g + 1) % 2
            bkY = self.bank(auxY, "tensor")
            for X in range(4):
                self.waitf("vector", souts[X].free)
                em("vector", lambda e, X=X, g=g, d=souts[X], s=sins[X]: e.tensor_tensor_scan(
                    d.v, self.zrow[:, 0:TA], s.v, self.carry[:, 4 * g + X:4 * g + X + 1],
                    TS.add, TS.add))
                em("vector", lambda e: e.drain())
                v = em("vector", lambda e, X=X, g=g, d=souts[X]: e.tensor_copy(
                    out=self.carry[:, 4 * g + X:4 * g + X + 1], in_=d.v[:, TA - 1:TA]), inc=sv)
                sins[X].free = (sv, v)
                souts[X].ready = (sv, v)
            for X in range(4):
                sb = self.sqt[X % 2]
                self.waitf("scalar", sb.free)
                wait("scalar", sv, souts[X].ready[1])
                v = em("scalar", lambda e, sb=sb, d=souts[X]: e.activation(
                    out=sb.v, in_=d.v, func=AF.Square), inc=ss)
                wait("tensor", ss, v)
                v = em("tensor", lambda e, sb=sb, g=g, X=X: e.matmul(
                    self.psum[3][0:KH, 0:TA], self.selg[g], sb.v,
                    start=(g == 0 and X == 0), stop=(g == 7 and X == 3)), inc=st_)
                sb.free = (st_, v)
                wait("tensor", sv, souts[X].ready[1])
                v = em("tensor", lambda e, d=souts[X], X=X, bkY=bkY: e.matmul(
                    bkY[:, 0:TA], self.wdiag[X], d.v, start=(X == 0), stop=(X == 3)),
                    inc=(st_ if X == 3 else None))
                souts[X].free = (st_, st_.n)
            vY = st_.n
            wait("scalar", st_, vY)
            v = em("scalar", lambda e, bkY=bkY, g=g: e.copy(
                out=self.y_raw.v[:, TA * g:TA * (g + 1)], in_=bkY[:, 0:TA]), inc=ss)
            self.bank_consumed(auxY, ss, v)
            if b == 0 and ci == 0 and g == 0:
                self.dbg("uu0", self.uu[0].v)
                self.dbg("s1_0", self.s1t[0].v)
                self.dbg("c3_0", self.c3t[0].v)
                self.dbg("pp0", self.ppt[0].v)
                self.dbg("scoutA0", self.scout[0][0].v)
                self.dbg("scoutD0", self.scout[0][3].v)
                self.dbg("pwE0", self.pwE_sb[0].v)

        for gg in range(9):
            if gg < 8:
                stage1a(gg)
            if gg > 0:
                stage2(gg - 1)
            if gg < 8:
                stage1b(gg)
        v_S_done = st_.n
        v_y_done = ss.n
        self.pwt.free = (st_, gstate[7]["last_pwE"])

        for ct in range(8, 16):
            self._inproj_conv(b, ct, "gate", 0)
        self.ha_hi.free = (st_, st_.n)
        self.ha_lo.free = (st_, st_.n)
        v_gates = ss.n

        em("vector", lambda e: e.tensor_scalar(out=self.g32[1][0:KH, :], in0=self.dent.v[0:KH, :],
                                               scalar1=1e-4, scalar2=None, op0=TS.max, op1=TS.bypass))
        em("vector", lambda e: e.drain())
        v = em("vector", lambda e: e.reciprocal(out=self.g32[1][0:KH, :], in_=self.g32[1][0:KH, :]),
               inc=sv)
        em("vector", lambda e: e.drain())
        self.dent.free = (sv, v)
        em("vector", lambda e: e.tensor_tensor(out=self.g32[2][0:KH, :], in0=self.g32[1][0:KH, :],
                                               in1=self.g32[1][0:KH, :], op=TS.mult))
        wait("vector", st_, v_S_done)
        v = em("vector", lambda e: e.tensor_tensor(out=self.g32[0][0:KH, :],
                                                   in0=self.psum[3][0:KH, 0:TA],
                                                   in1=self.g32[2][0:KH, :], op=TS.mult), inc=sv)
        self.bank_consumed(3, sv, v)
        wait("scalar", sv, v)
        em("scalar", lambda e: e.activation(out=self.g32[0][0:KH, :], in_=self.g32[0][0:KH, :],
                                            func=AF.Ln, scale=1.0 / 128, bias=1e-5))
        v = em("scalar", lambda e: e.activation(out=self.g32[0][0:KH, :], in_=self.g32[0][0:KH, :],
                                                func=AF.Exp, scale=-0.5), inc=ss)
        self.waitf("vector", self.gfin.free)
        wait("vector", ss, v)
        vgf = em("vector", lambda e: e.tensor_tensor(out=self.gfin.v[0:KH, :], in0=self.g32[0][0:KH, :],
                                                     in1=self.g32[1][0:KH, :], op=TS.mult), inc=sv)
        if b == 0 and ci == 0:
            self.dbg("yraw", self.y_raw.v)
            self.dbg("gfin", self.gfin.v[0:KH, :])
            self.dbg("silug", self.silu_g.v)

        wait("tensor", sv, vgf)
        wait("vector", ss, max(v_y_done, v_gates))
        self.waitf("vector", self.g_hi.free)
        self.waitf("vector", self.g_lo.free)
        for g in range(8):
            aux = 4 + g % 2
            bk = self.bank(aux, "tensor")
            v = em("tensor", lambda e, bk=bk, g=g: e.matmul(
                bk[:, 0:TA], self.Ee[g], self.gfin.v[0:KH, :], start=True, stop=True), inc=st_)
            wait("vector", st_, v)
            gfb = self.gft[g % 2]
            em("vector", lambda e, bk=bk, g=g, gfb=gfb: e.tensor_tensor(
                out=gfb, in0=self.y_raw.v[:, TA * g:TA * (g + 1)], in1=bk[:, 0:TA], op=TS.mult))
            vv = em("vector", lambda e, g=g, gfb=gfb: e.tensor_tensor(
                out=gfb, in0=gfb, in1=self.silu_g.v[:, TA * g:TA * (g + 1)], op=TS.mult), inc=sv)
            self.bank_consumed(aux, sv, vv)
            wait("scalar", sv, vv)
            vhi = em("scalar", lambda e, g=g, gfb=gfb: e.copy(
                out=self.g_hi.v[:, TA * g:TA * (g + 1)], in_=gfb), inc=ss)
            wait("vector", ss, vhi)
            em("vector", lambda e, g=g, gfb=gfb: e.tensor_tensor(
                out=self.g_lo.v[:, TA * g:TA * (g + 1)], in0=gfb,
                in1=self.g_hi.v[:, TA * g:TA * (g + 1)], op=TS.subtract))
        vgp = em("vector", lambda e: e.drain(),
                 inc=sv)
        self.gfin.free = (sv, vgp)
        self.silu_g.free = (sv, vgp)
        self.y_raw.free = (sv, vgp)

        wait("tensor", sv, vgp)
        for cg in range(2):
            bk = self.bank(6 + cg, "tensor")
            for k in range(8):
                sG = slice(TA * k, TA * (k + 1))
                sW = slice(D * k + 512 * cg, D * k + 512 * cg + 512)
                em("tensor", lambda e, bk=bk, sG=sG, sW=sW, k=k: e.matmul(
                    bk[:, :], self.g_hi.v[:, sG], self.WOhi[:, sW], start=(k == 0), stop=False))
                em("tensor", lambda e, bk=bk, sG=sG, sW=sW: e.matmul(
                    bk[:, :], self.g_lo.v[:, sG], self.WOhi[:, sW], start=False, stop=False))
                em("tensor", lambda e, bk=bk, sG=sG, sW=sW, k=k: e.matmul(
                    bk[:, :], self.g_hi.v[:, sG], self.WOlo[:, sW], start=False, stop=(k == 7)),
                   inc=(st_ if k == 7 else None))
        v_op = st_.n
        self.g_hi.free = (st_, v_op)
        self.g_lo.free = (st_, v_op)

        xr = self.xres_tm[ci % 2]
        self.waitf("vector", xr.free)
        wait("vector", st_, v_op)
        waitf("vector", t_gaB)
        if b == 0 and ci == 0:
            self.dbg("gaB", self.gaB)
        for cg in range(2):
            em("vector", lambda e, cg=cg, xr=xr: e.tensor_tensor(
                out=xr.v[:, 512 * cg:512 * cg + 512],
                in0=self.psum[6 + cg][:, :], in1=self.gaB[:, 512 * cg:512 * cg + 512], op=TS.mult))
            v = em("vector", lambda e, cg=cg, xr=xr, xb=xb: e.tensor_tensor(
                out=xr.v[:, 512 * cg:512 * cg + 512], in0=xr.v[:, 512 * cg:512 * cg + 512],
                in1=xb.v[:, 512 * cg:512 * cg + 512], op=TS.add), inc=sv)
            self.bank_consumed(6 + cg, sv, v)
        vxr = sv.n
        xb.free = (sv, vxr)
        wait("gpsimd", sv, vxr)
        if b == 0 and ci == 0:
            self.dbg("xres0", xr.v)
        vst = dma("gpsimd", None, self.xres_h[b, t0:t0 + TA, :], xr.v)

        wait("scalar", sv, vxr)
        v = em("scalar", lambda e, xr=xr: e.activation(out=self.sq_scr, in_=xr.v,
                                                       func=AF.Square), inc=ss)
        wait("vector", ss, v)
        em("vector", lambda e: e.tensor_reduce(out=self.lnrow[:, 6:7], in_=self.sq_scr,
                                               axis=AX.X, op=TS.add))
        em("vector", lambda e: e.drain())
        em("vector", lambda e, xr=xr: e.tensor_reduce(out=self.lnrow[:, 5:6], in_=xr.v,
                                                      axis=AX.X, op=TS.add))
        em("vector", lambda e: e.drain())
        em("vector", lambda e: e.tensor_scalar(out=self.lnrow[:, 5:6], in0=self.lnrow[:, 5:6],
                                               scalar1=1.0 / D, scalar2=None, op0=TS.mult, op1=TS.bypass))
        em("vector", lambda e: e.drain())
        em("vector", lambda e: e.tensor_tensor(out=self.lnrow[:, 7:8], in0=self.lnrow[:, 5:6],
                                               in1=self.lnrow[:, 5:6], op=TS.mult))
        em("vector", lambda e: e.drain())
        v = em("vector", lambda e: e.scalar_tensor_tensor(out=self.lnrow[:, 8:9], in0=self.lnrow[:, 6:7],
                                                          scalar=1.0 / D, in1=self.lnrow[:, 7:8],
                                                          op0=TS.mult, op1=TS.subtract), inc=sv)
        wait("scalar", sv, v)
        em("scalar", lambda e: e.activation(out=self.lnrow[:, 9:10], in_=self.lnrow[:, 8:9],
                                            func=AF.Ln, bias=1e-5))
        v = em("scalar", lambda e: e.activation(out=self.lnrow[:, 9:10], in_=self.lnrow[:, 9:10],
                                                func=AF.Exp, scale=-0.5), inc=ss)
        wait("vector", ss, v)
        em("vector", lambda e, it=itile: e.tensor_copy(out=self.stats[:, 2 * it:2 * it + 1],
                                                       in_=self.lnrow[:, 5:6]))
        em("vector", lambda e, it=itile: e.tensor_copy(out=self.stats[:, 2 * it + 1:2 * it + 2],
                                                       in_=self.lnrow[:, 9:10]))
        t2b = self.t2_tm
        self.waitf("vector", t2b.free)
        v2 = em("vector", lambda e, xr=xr, t2b=t2b: e.tensor_scalar(
            out=t2b.v, in0=xr.v, scalar1=self.lnrow[:, 5:6], scalar2=self.lnrow[:, 9:10],
            op0=TS.subtract, op1=TS.mult), inc=sv)
        self.lnrow_free = (sv, v2)
        xr.free = vst
        wait("tensor", sv, v2)
        self.waitf("tensor", self.t2fm.free)
        for r in range(2):
            bk = self.bank(0, "tensor")
            for j in range(4):
                jj = 4 * r + j
                em("tensor", lambda e, bk=bk, j=j, jj=jj, t2b=t2b: e.transpose(
                    bk[:, 128 * j:128 * j + 128], t2b.v[:, 128 * jj:128 * jj + 128], self.ident),
                   inc=(st_ if j == 3 else None))
            wait("scalar", st_, st_.n)
            v = em("scalar", lambda e, bk=bk, r=r: e.copy(
                out=self.t2fm.v[:, 512 * r:512 * r + 512], in_=bk[:, :]), inc=ss)
            self.bank_consumed(0, ss, v)
        t2b.free = (st_, st_.n)
        v_t2fm = ss.n
        wait("vector", ss, v_t2fm)
        for j in range(8):
            sl = slice(TA * j, TA * (j + 1))
            em("vector", lambda e, j=j, sl=sl: e.tensor_scalar(
                out=self.hbt.v, in0=self.t2fm.v[:, sl], scalar1=self.cbv[b][:, j:j + 1],
                scalar2=self.sbv[b][:, j:j + 1], op0=TS.mult, op1=TS.add))
            em("vector", lambda e, j=j: e.tensor_reduce(
                out=self.am8[:, j:j + 1], in_=self.hbt.v, axis=AX.X, op=TS.max,
                apply_absolute_value=True))
        em("vector", lambda e: e.drain())
        em("vector", lambda e: e.tensor_reduce(out=self.am8[:, 8:9], in_=self.am8[:, 0:8],
                                               axis=AX.X, op=TS.max))
        em("vector", lambda e: e.drain())
        em("vector", lambda e: e.tensor_tensor(out=self.amaxh, in0=self.amaxh,
                                               in1=self.am8[:, 8:9], op=TS.max))
        v = em("vector", lambda e: e.drain(), inc=sv)
        self.t2fm.free = (sv, v)
        return vst

    def exchange(self, which):
        em, wait, dma, waitf = self.em, self.wait, self.dma, self.waitf
        sv, ss, st_, cc = self.sv, self.ss, self.st_, self.cc
        TS = ALU
        amax_l = self.amaxh if which == 1 else self.amaxd
        ci, co = (self.cc1i_h, self.cc1o_h) if which == 1 else (self.cc2i_h, self.cc2o_h)
        sdst = self.sxv if which == 1 else self.shv
        v0 = self.em("vector", lambda e: e.drain(), inc=sv)
        self.dbg("amx%d" % which, amax_l)
        bk = self.bank(0, "tensor")
        wait("tensor", sv, v0)
        v = em("tensor", lambda e, bk=bk, amax_l=amax_l: e.transpose(
            bk[0:1, 0:128], amax_l, self.ident), inc=st_)
        wait("vector", st_, v)
        em("vector", lambda e, bk=bk: e.tensor_reduce(out=self.rowt[0:1, 0:1], in_=bk[0:1, 0:128],
                                                      axis=AX.X, op=TS.max))
        em("vector", lambda e: e.drain())
        em("vector", lambda e: e.tensor_scalar(out=self.rowt[0:1, 0:1], in0=self.rowt[0:1, 0:1],
                                               scalar1=1e-8, scalar2=None, op0=TS.max, op1=TS.bypass))
        em("vector", lambda e: e.drain())
        for q in range(1, 4):
            em("vector", lambda e, q=q: e.tensor_copy(out=self.rowt[0:1, q:q + 1],
                                                      in_=self.rowt[0:1, 0:1]))
        v = em("vector", lambda e: e.drain(), inc=sv)
        self.bank_consumed(0, sv, v)
        wait("gpsimd", sv, v)
        self.dbg("rowa%d" % which, self.rowt[0:1, 0:4])
        t_ = dma("gpsimd", None, ci[:, :], self.rowt[0:1, 0:4])
        waitf("gpsimd", t_)
        v = em("gpsimd", lambda e, ci=ci, co=co: e.collective_compute(
            "AllReduce", ALU.max, replica_groups=[list(range(NCORES_RUN[0]))],
            ins=[ci.opt()], outs=[co.opt()]), inc=cc)
        wait("gpsimd", cc, v)
        t_ = dma("gpsimd", None, self.rowt[0:1, 8:12], co[:, :])
        waitf("vector", t_)
        em("vector", lambda e: e.reciprocal(out=self.rowt[0:1, 13:14], in_=self.rowt[0:1, 8:9]))
        em("vector", lambda e: e.drain())
        em("vector", lambda e: e.tensor_scalar(out=self.rowt[0:1, 12:13], in0=self.rowt[0:1, 13:14],
                                               scalar1=32767.0, scalar2=None,
                                               op0=TS.mult, op1=TS.bypass))
        v = em("vector", lambda e: e.drain(), inc=sv)
        wait("gpsimd", sv, v)
        self.dbg("rowb%d" % which, self.rowt[0:1, 8:14])
        t_ = dma("gpsimd", None, self.bc_h[10 + which:11 + which, 0:1], self.rowt[0:1, 12:13])
        waitf("gpsimd", t_)
        t_ = dma("gpsimd", None, sdst,
                 self.bc_h[10 + which:11 + which, 0:1].partition_broadcast(128))
        waitf("vector", t_)
        TSx = ALU
        self.dbg("scl%d" % which, sdst)
        if which == 1:
            for b in range(2):
                em("vector", lambda e, b=b: e.tensor_scalar(
                    out=self.cbsxv[b], in0=self.cbv[b], scalar1=self.sxv[:, 0:1], scalar2=None,
                    op0=TSx.mult, op1=TSx.bypass))
                em("vector", lambda e, b=b: e.tensor_scalar(
                    out=self.sbsxv[b], in0=self.sbv[b], scalar1=self.sxv[:, 0:1], scalar2=None,
                    op0=TSx.mult, op1=TSx.bypass))
            em("vector", lambda e: e.tensor_scalar(
                out=self.iswx, in0=self.swv, scalar1=self.sxv[:, 0:1], scalar2=None,
                op0=TSx.mult, op1=TSx.bypass))
            em("vector", lambda e: e.drain())
            v = em("vector", lambda e: e.reciprocal(out=self.iswx, in_=self.iswx), inc=sv)
            em("vector", lambda e: e.drain())
        else:
            em("vector", lambda e: e.tensor_scalar(
                out=self.vrv, in0=self.swov, scalar1=self.shv[:, 0:1], scalar2=None,
                op0=TSx.mult, op1=TSx.bypass))
            em("vector", lambda e: e.drain())
            v = em("vector", lambda e: e.reciprocal(out=self.vrv, in_=self.vrv), inc=sv)
            em("vector", lambda e: e.drain())
        self.barrier()

    def alloc_B1(self):
        A = Arena(self.arena_t.ap(), self.AR, "B1", self.res_end)
        T = self.TB1
        NT = T // 128
        self.Wq1 = A.alloc(8 * DFF, BF16)
        self.b1_x = [A.buf(D) for _ in range(3)]
        self.b1_t = [A.buf(D) for _ in range(2)]
        self.b1_tf = A.buf(NT * 8 * 128)
        self.b1_w1 = A.alloc(T)
        self.b1_q = A.alloc(T)
        self.q_hi = A.buf(8 * T, BF16)
        self.q_lo = A.buf(8 * T, BF16)
        self.hidb = [A.buf(T) for _ in range(3)]
        assert A.off <= self.AR, A.off
        self.B1_end = A.off

    def b1_chunk(self, b, ci):
        T = self.TB1
        NT = T // 128
        t0 = ci * T
        em, wait, dma, waitf = self.em, self.wait, self.dma, self.waitf
        sv, ss, st_ = self.sv, self.ss, self.st_
        TS = ALU
        self.waitf("tensor", self.b1_tf.free)
        for i in range(NT):
            xb = self.b1_x[i % 3]
            self.waitf("sync", xb.free)
            t_x = dma("sync", None, xb.v, self.xres_h[b, t0 + 128 * i:t0 + 128 * (i + 1), :])
            it = (b * self.L + t0 + 128 * i) // 128
            tb = self.b1_t[i % 2]
            self.waitf("vector", tb.free)
            waitf("vector", t_x)
            vt = em("vector", lambda e, xb=xb, tb=tb, it=it: e.tensor_scalar(
                out=tb.v, in0=xb.v, scalar1=self.stats[:, 2 * it:2 * it + 1],
                scalar2=self.stats[:, 2 * it + 1:2 * it + 2],
                op0=TS.subtract, op1=TS.mult), inc=sv)
            xb.free = (sv, vt)
            wait("tensor", sv, vt)
            for r in range(2):
                bk = self.bank(0, "tensor")
                for j in range(4):
                    jj = 4 * r + j
                    em("tensor", lambda e, bk=bk, j=j, jj=jj, tb=tb: e.transpose(
                        bk[:, 128 * j:128 * j + 128], tb.v[:, 128 * jj:128 * jj + 128],
                        self.ident), inc=(st_ if j == 3 else None))
                wait("scalar", st_, st_.n)
                v = em("scalar", lambda e, bk=bk, r=r, i=i: e.copy(
                    out=self.b1_tf.v.rearrange("p (j t) -> p j t", j=8)[:, 4 * r:4 * r + 4,
                                                                       128 * i:128 * i + 128],
                    in_=bk[:, :].rearrange("p (j t) -> p j t", j=4)), inc=ss)
                self.bank_consumed(0, ss, v)
            tb.free = (st_, st_.n)
        v_tf = self.ss.n
        wait("vector", ss, v_tf)
        self.waitf("vector", self.q_hi.free)
        self.waitf("vector", self.q_lo.free)
        for j in range(8):
            sl = slice(T * j, T * (j + 1))
            em("vector", lambda e, j=j, sl=sl, b=b: e.tensor_scalar(
                out=self.b1_w1[:, 0:T], in0=self.b1_tf.v[:, sl], scalar1=self.cbsxv[b][:, j:j + 1],
                scalar2=self.sbsxv[b][:, j:j + 1], op0=TS.mult, op1=TS.add))
            em("vector", lambda e: e.drain())
            vqj = em("vector", lambda e, j=j: e.tensor_scalar(
                out=self.b1_q[:, 0:T], in0=self.b1_w1[:, 0:T], scalar1=RC, scalar2=RC,
                op0=TS.add, op1=TS.subtract), inc=sv)
            wait("scalar", sv, vqj)
            vhi = em("scalar", lambda e, j=j, sl=sl: e.copy(
                out=self.q_hi.v[:, sl], in_=self.b1_q[:, 0:T]), inc=ss)
            wait("vector", ss, vhi)
            em("vector", lambda e, j=j, sl=sl: e.tensor_tensor(
                out=self.q_lo.v[:, sl], in0=self.b1_q[:, 0:T],
                in1=self.q_hi.v[:, sl], op=TS.subtract))
        vq = em("vector", lambda e: e.drain(), inc=sv)
        self.b1_tf.free = (sv, vq)
        if b == 0 and ci == 0:
            self.dbg("b1tf", self.b1_tf.v[:, 0:1024])
            self.dbg("qhi", self.q_hi.v[:, 0:1024])
            self.dbg("qlo", self.q_lo.v[:, 0:1024])
        wait("tensor", sv, vq)
        NH = T // 512 if T >= 512 else 1
        NW = min(512, T)
        for ct in range(32):
            bis = [1 + (2 * ct + h) % 4 for h in range(NH)]
            bks = [self.bank(bi, "tensor") for bi in bis]
            for k in range(8):
                sW = slice(DFF * k + 128 * ct, DFF * k + 128 * ct + 128)
                for h in range(NH):
                    sQ = slice(T * k + NW * h, T * k + NW * (h + 1))
                    em("tensor", lambda e, bk=bks[h], sW=sW, sQ=sQ, k=k: e.matmul(
                        bk[:, 0:NW], self.Wq1[:, sW], self.q_hi.v[:, sQ],
                        start=(k == 0), stop=False))
                    em("tensor", lambda e, bk=bks[h], sW=sW, sQ=sQ, k=k: e.matmul(
                        bk[:, 0:NW], self.Wq1[:, sW], self.q_lo.v[:, sQ],
                        start=False, stop=(k == 7)), inc=(st_ if (k == 7 and h == NH - 1) else None))
            vmm = st_.n
            hb = self.hidb[ct % 3]
            self.waitf("scalar", hb.free)
            wait("scalar", st_, vmm)
            vg = ss.n
            for h in range(NH):
                vg = em("scalar", lambda e, hb=hb, bk=bks[h], ct=ct, h=h: e.activation(
                    out=hb.v[:, NW * h:NW * (h + 1)], in_=bk[:, 0:NW], func=AF.Gelu,
                    scale=self.iswx[:, ct:ct + 1], bias=self.biT[:, ct:ct + 1]), inc=ss)
                self.bank_consumed(bis[h], ss, vg)
            wait("gpsimd", ss, vg)
            if b == 0 and ci == 0 and ct == 0:
                self.dbg("hid0", hb.v)
            t_hid = dma("gpsimd", None, self.hid_h[128 * ct:128 * ct + 128,
                                                   b * self.L + t0:b * self.L + t0 + T], hb.v)
            wait("vector", ss, vg)
            em("vector", lambda e, hb=hb, ct=ct: e.tensor_reduce(
                out=self.am8[:, 16 + ct % 8:17 + ct % 8], in_=hb.v, axis=AX.X, op=TS.max,
                apply_absolute_value=True))
            em("vector", lambda e: e.drain())
            em("vector", lambda e, ct=ct: e.tensor_tensor(
                out=self.amaxd, in0=self.amaxd, in1=self.am8[:, 16 + ct % 8:17 + ct % 8],
                op=TS.max))
            v = em("vector", lambda e: e.drain(), inc=sv)
            hb.free = t_hid
        self.q_hi.free = (st_, st_.n)
        self.q_lo.free = (st_, st_.n)

    def alloc_B2(self):
        A = Arena(self.arena_t.ap(), self.AR, "B2", self.res_end)
        T = self.TB2
        self.Wq2 = A.alloc(32 * D, BF16)
        self.gbB = A.alloc(D)
        self.bor = A.alloc(D, rows=1)
        self.b2_h = [A.buf(T) for _ in range(3)]
        self.b2_a = A.alloc(T)
        self.a_hi = [A.buf(T, BF16) for _ in range(2)]
        self.a_lo = [A.buf(T, BF16) for _ in range(2)]
        self.b2_x = [A.buf(D) for _ in range(5)]
        self.b2_o = [A.buf(D) for _ in range(2)]
        assert A.off <= self.AR, A.off
        self.B2_end = A.off

    def b2_load_w(self):
        em, wait, dma = self.em, self.wait, self.dma
        self.wq2_toks = [dma("sync", None, self.bor, self.bo_h[:, :])]
        for s in range(32):
            self.wq2_toks.append(dma("sync", None, self.Wq2[:, D * s:D * (s + 1)],
                                     self.wqo_h[128 * s:128 * s + 128, :]))

    def b2_chunk(self, b, ci):
        T = self.TB2
        NT = T // 128
        t0 = ci * T
        gt0 = b * self.L + t0
        em, wait, dma, waitf = self.em, self.wait, self.dma, self.waitf
        sv, ss, st_ = self.sv, self.ss, self.st_
        TS = ALU
        if ci == 0 and b == 0:
            for t_ in self.wq2_toks:
                waitf("tensor", t_)
            if self.debug:
                for t_ in self.wq2_toks:
                    waitf("gpsimd", t_)
            self.dbg("wq2s0", self.Wq2[:, 0:1024])
            self.dbg("wq2s31", self.Wq2[:, 31 * D:32 * D])
            self.dbg("bor0", self.bor[0:1, :])
        if ci == 0:
            wait("gpsimd", sv, sv.n)
            self.t_gbB = dma("gpsimd", None, self.gbB,
                             self.bc_h[2 + b:3 + b, :].partition_broadcast(128))
        t_gbB = self.t_gbB
        xts = []
        for i in range(NT):
            xb = self.b2_x[(ci * NT + i) % 5]
            self.waitf("sync", xb.free)
            xb.ready = dma("sync", None, xb.v,
                           self.xres_h[b, t0 + 128 * i:t0 + 128 * (i + 1), :])
            xts.append(xb)
        banks = {}
        for tg in range(NT):
            for cg in range(2):
                bi = 2 * tg + cg
                banks[(tg, cg)] = self.bank(bi, "tensor")
        for k in range(32):
            hb = self.b2_h[k % 3]
            self.waitf("sync", hb.free)
            t_h = dma("sync", None, hb.v, self.hid_h[128 * k:128 * k + 128, gt0:gt0 + T])
            ah, al = self.a_hi[k % 2], self.a_lo[k % 2]
            self.waitf("vector", ah.free)
            self.waitf("vector", al.free)
            waitf("vector", t_h)
            em("vector", lambda e, hb=hb, k=k: e.tensor_scalar(
                out=self.b2_a, in0=hb.v, scalar1=self.shv[:, 0:1], scalar2=RC,
                op0=TS.mult, op1=TS.add))
            vq = em("vector", lambda e, k=k: e.tensor_scalar(
                out=self.b2_a, in0=self.b2_a, scalar1=RC, scalar2=self.vrv[:, k:k + 1],
                op0=TS.subtract, op1=TS.mult), inc=sv)
            hb.free = (sv, vq)
            wait("scalar", sv, vq)
            va = em("scalar", lambda e, ah=ah: e.copy(out=ah.v, in_=self.b2_a), inc=ss)
            wait("vector", ss, va)
            vl = em("vector", lambda e, ah=ah, al=al: e.tensor_tensor(
                out=al.v, in0=self.b2_a, in1=ah.v, op=TS.subtract), inc=sv)
            if b == 0 and ci == 0 and k == 0:
                self.dbg("b2a", self.b2_a)
            wait("tensor", ss, va)
            wait("tensor", sv, vl)
            for tg in range(NT):
                sA = slice(128 * tg, 128 * (tg + 1))
                for cg in range(2):
                    bk = banks[(tg, cg)]
                    sW = slice(D * k + 512 * cg, D * k + 512 * cg + 512)
                    em("tensor", lambda e, bk=bk, sA=sA, sW=sW, ah=ah, k=k: e.matmul(
                        bk[:, :], ah.v[:, sA], self.Wq2[:, sW], start=(k == 0), stop=False))
                    lastk = (tg == NT - 1 and cg == 1)
                    em("tensor", lambda e, bk=bk, sA=sA, sW=sW, al=al, k=k: e.matmul(
                        bk[:, :], al.v[:, sA], self.Wq2[:, sW], start=False, stop=False),
                       inc=(st_ if lastk else None))
            ah.free = (st_, st_.n)
            al.free = (st_, st_.n)
        for tg in range(NT):
            for cg in range(2):
                bk = banks[(tg, cg)]
                em("tensor", lambda e, bk=bk, cg=cg: e.matmul(
                    bk[:, :], self.onesr[0:1, 0:128], self.bor[0:1, 512 * cg:512 * cg + 512],
                    start=False, stop=True), inc=st_)
        v_mm = st_.n
        wait("vector", st_, v_mm)
        waitf("vector", t_gbB)
        if b == 0 and ci == 0:
            self.dbg("gbB0", self.gbB[:, 0:1024])
            self.dbg("xbB0", xts[0].v)
        for tg in range(NT):
            ob = self.b2_o[tg % 2]
            self.waitf("vector", ob.free)
            xb = xts[tg]
            waitf("vector", xb.ready)
            for cg in range(2):
                bi = 2 * tg + cg
                em("vector", lambda e, ob=ob, bi=bi, cg=cg: e.tensor_tensor(
                    out=ob.v[:, 512 * cg:512 * cg + 512], in0=self.psum[bi][:, :],
                    in1=self.gbB[:, 512 * cg:512 * cg + 512], op=TS.mult))
                v = em("vector", lambda e, ob=ob, xb=xb, cg=cg: e.tensor_tensor(
                    out=ob.v[:, 512 * cg:512 * cg + 512],
                    in0=ob.v[:, 512 * cg:512 * cg + 512],
                    in1=xb.v[:, 512 * cg:512 * cg + 512], op=TS.add), inc=sv)
                self.bank_consumed(bi, sv, v)
            vo = sv.n
            xb.free = (sv, vo)
            if b == 0 and ci == 0 and tg == 0:
                self.dbg("yout0", ob.v)
            wait("gpsimd", sv, vo)
            ob.free = dma("gpsimd", None, self.y_h[b, t0 + 128 * tg:t0 + 128 * (tg + 1), :],
                          ob.v)

    def finish(self):
        for s in self.pool["gpsimd"]:
            if s.n:
                self.wait("gpsimd", s, s.n)
        for s in self.pool["sync"]:
            if s.n:
                self.wait("sync", s, s.n)

    def build(self):
        self.prologue()
        self.alloc_A()
        nchunks = self.L // self.TA
        it = 0
        for b in range(B_PER_CORE):
            for ci in range(nchunks):
                self.a_chunk(b, ci, (b * self.L + ci * self.TA) // 128)
        self.barrier()
        self.exchange(1)
        self.alloc_B1()
        for k in range(8):
            t_ = self.dma("sync", None, self.Wq1[:, DFF * k:DFF * (k + 1)],
                          self.wqi_h[128 * k:128 * k + 128, :])
            self.waitf("tensor", t_)
        for b in range(B_PER_CORE):
            for ci in range(self.L // self.TB1):
                self.b1_chunk(b, ci)
        self.barrier()
        self.exchange(2)
        self.alloc_B2()
        self.b2_load_w()
        for b in range(B_PER_CORE):
            for ci in range(self.L // self.TB2):
                self.b2_chunk(b, ci)
        self.finish()
        nc = self.nc
        with nc.Block() as blk:
            @blk.sync
            def _(e):
                for f in self.S["sync"]:
                    f(e)

            @blk.vector
            def _(e):
                for f in self.S["vector"]:
                    f(e)

            @blk.scalar
            def _(e):
                for f in self.S["scalar"]:
                    f(e)

            @blk.tensor
            def _(e):
                for f in self.S["tensor"]:
                    f(e)

            @blk.gpsimd
            def _(e):
                for f in self.S["gpsimd"]:
                    f(e)
        self.stack.close()
        return nc


NCORES_RUN = [8]


def _host_inputs(inp, L):
    x = np.ascontiguousarray(inp["L0_attn_x"])
    c = np.ascontiguousarray(inp["L0_c"])
    cw = np.asarray(inp["L0_attn_conv_w"]).reshape(4, TOTAL)
    cwp = np.zeros((4, NCT * 128), np.float32)
    cwp[:, :TOTAL] = cw
    cw_l = np.ascontiguousarray(cwp.reshape(4, NCT, 128).transpose(2, 0, 1).reshape(128, 4 * NCT))
    cbp = np.zeros((NCT * 128,), np.float32)
    cbp[:TOTAL] = np.asarray(inp["L0_attn_conv_b"])
    cb_l = np.ascontiguousarray(cbp.reshape(NCT, 128).T)
    theta = np.asarray(inp["L0_attn_theta"])
    th2 = np.ascontiguousarray(theta[0, 0, 0, 0, 2:4].reshape(1, 2))
    nsc = np.ascontiguousarray(np.asarray(inp["L0_attn_norm_scale"]).reshape(2, 128).T)
    maps = []
    ncore = NCORES_RUN[0]
    for i in range(ncore):
        maps.append({
            "x": np.ascontiguousarray(x[2 * i:2 * i + 2]),
            "c2": np.ascontiguousarray(c[2 * i:2 * i + 2]),
            "w_inproj": np.ascontiguousarray(inp["L0_attn_in_proj_w"]),
            "conv_w_l": cw_l,
            "conv_b_l": cb_l,
            "theta2": th2,
            "dslope": np.ascontiguousarray(np.asarray(inp["L0_attn_decay_slopes"]).reshape(28, 1)),
            "aslope": np.ascontiguousarray(np.asarray(inp["L0_attn_anchor_slopes"]).reshape(4, 1)),
            "score": np.ascontiguousarray(np.asarray(inp["L0_attn_score_scale"]).reshape(KH, 1)),
            "deriv": np.ascontiguousarray(np.asarray(inp["L0_attn_deriv_logits"]).reshape(1, 3)),
            "nsc": nsc,
            "w_re": np.ascontiguousarray(inp["L0_attn_W_re"]),
            "w_im": np.ascontiguousarray(inp["L0_attn_W_im"]),
            "w_outproj": np.ascontiguousarray(inp["L0_attn_out_proj_w"]),
            "w_in_T": np.ascontiguousarray(np.asarray(inp["L0_ffn_W_in"]).T),
            "b_in_t": np.ascontiguousarray(np.asarray(inp["L0_ffn_b_in"]).reshape(32, 128).T),
            "w_out_f": np.ascontiguousarray(inp["L0_ffn_W_out"]),
            "b_out_r": np.ascontiguousarray(np.asarray(inp["L0_ffn_b_out"]).reshape(1, D)),
            "wmod": np.ascontiguousarray(inp["L0_wmod"]),
        })
    return maps


def kernel(**inputs):
    L = int(np.asarray(inputs["L0_attn_x"]).shape[1])
    kb = KB(L=L)
    nc = kb.build()
    maps = _host_inputs(inputs, L)
    res = run_bass_kernel_spmd(nc, maps, core_ids=list(range(NCORES_RUN[0])))
    out = np.concatenate([np.asarray(r["y"]) for r in res.results], axis=0)
    return out


# version 05 (final)
# speedup vs baseline: 1.1020x; 1.0433x over previous
import numpy as np
import concourse.bass as bass
import concourse.mybir as mybir
from concourse.bass_utils import run_bass_kernel_spmd
from concourse.alu_op_type import AluOpType as ALU
from contextlib import ExitStack

F32 = mybir.dt.float32
BF16 = mybir.dt.bfloat16
U8 = mybir.dt.uint8
AF = mybir.ActivationFunctionType
AX = mybir.AxisListType

RC = float(3 * 2 ** 22)
D = 1024
TOTAL = 2080
KH = 32
NCT = 17
DFF = 4096
B_PER_CORE = 2

def _rows(ct):
    return 32 if ct == NCT - 1 else 128


class Sem:
    def __init__(self, handle, step):
        self.h, self.step, self.n = handle, step, 0


class Buf:

    def __init__(self, view):
        self.v = view
        self.free = None


class Arena:
    def __init__(self, ap_u8, nbytes, name, off=0):
        self.ap, self.nbytes, self.off, self.name = ap_u8, nbytes, off, name

    def alloc(self, cols, dt=F32, rows=128):
        nbyt = cols * mybir.dt.size(dt)
        assert self.off + nbyt <= self.nbytes, (
            f"arena {self.name} overflow at +{nbyt}: {self.off} > {self.nbytes}")
        v = self.ap[0:rows, self.off:self.off + nbyt].bitcast(dt)
        self.off = (self.off + nbyt + 63) & ~63
        return v

    def buf(self, cols, dt=F32, rows=128):
        return Buf(self.alloc(cols, dt, rows))


class KB:
    def __init__(self, L=2048, TA=128, TB1=1024, TB2=512, debug=False, taps=None):
        self.debug = debug
        self.taps = taps
        self.dbg_names = []
        self.L, self.TA = L, TA
        self.TB1, self.TB2 = min(TB1, L), min(TB2, L)
        nc = bass.Bass("TRN2", target_bir_lowering=False, debug=False)
        self.nc = nc
        self.S = {k: [] for k in ("sync", "vector", "scalar", "tensor", "gpsimd")}
        self.stack = ExitStack()

        dr = lambda n, s, dt=F32, k="ExternalInput": nc.dram_tensor(n, s, dt, kind=k).ap()
        self.x_h = dr("x", [B_PER_CORE, L, D])
        self.c_h = dr("c2", [B_PER_CORE, D])
        self.wA_h = dr("w_inproj", [D, TOTAL])
        self.cw_h = dr("conv_w_l", [128, 4 * NCT])
        self.cb_h = dr("conv_b_l", [128, NCT])
        self.th_h = dr("theta2", [1, 2])
        self.dsl_h = dr("dslope", [28, 1])
        self.asl_h = dr("aslope", [4, 1])
        self.sc_h = dr("score", [KH, 1])
        self.dlg_h = dr("deriv", [1, 3])
        self.ns_h = dr("nsc", [128, 2])
        self.wre_h = dr("w_re", [128, KH])
        self.wim_h = dr("w_im", [128, KH])
        self.wO_h = dr("w_outproj", [D, D])
        self.wiT_h = dr("w_in_T", [DFF, D])
        self.bi_h = dr("b_in_t", [128, 32])
        self.wo_h = dr("w_out_f", [DFF, D])
        self.bo_h = dr("b_out_r", [1, D])
        self.wm_h = dr("wmod", [D, 6 * D])
        self.y_h = dr("y", [B_PER_CORE, L, D], F32, "ExternalOutput")
        self.xres_h = dr("xres_s", [B_PER_CORE, L, D], F32, "Internal")
        self.hid_h = dr("hidden_s", [DFF, B_PER_CORE * L], F32, "Internal")
        self.wqi_h = dr("wqi_s", [D, DFF], BF16, "Internal")
        self.wqo_h = dr("wqo_s", [DFF, D], BF16, "Internal")
        self.bc_h = dr("bcast_s", [16, D], F32, "Internal")
        self.cc1i_h = dr("cc1_in", [1, 4], F32, "Internal")
        self.cc1o_h = dr("cc1_out", [1, 4], F32, "Internal")
        self.cc2i_h = dr("cc2_in", [1, 4], F32, "Internal")
        self.cc2o_h = dr("cc2_out", [1, 4], F32, "Internal")
        self.tw_h = dr("tw_s", [KH, L], F32, "Internal")

        mks = lambda n, st: Sem(self.stack.enter_context(nc.semaphore(n)), st)
        self.sv = mks("sv", 1)
        self.ss = mks("ss", 1)
        self.st_ = mks("st", 1)
        self.sg = mks("sg", 1)
        self.cc = mks("ccs", 1)
        self.pool = {"sync": [mks(f"dw{i}", 16) for i in range(7)],
                     "gpsimd": [mks(f"dg{i}", 16) for i in range(7)]}
        self.pool_rr = {"sync": 0, "gpsimd": 0}
        self.allsems = ([self.sv, self.ss, self.st_, self.sg, self.cc]
                        + self.pool["sync"] + self.pool["gpsimd"])

        AR = 204 * 1024
        self.arena_t = self.stack.enter_context(nc.sbuf_tensor("arena", [128, AR], U8))
        self.AR = AR
        self.psum = [self.stack.enter_context(nc.psum_tensor(f"pb{i}", [128, 512], F32))
                     for i in range(8)]
        self.psum_free = [None] * 8
        self._residents()

    def em(self, eng, fn, inc=None):
        if inc is None:
            self.S[eng].append(fn)
            return None
        inc.n += inc.step
        h, s = inc.h, inc.step
        self.S[eng].append(lambda e: fn(e).then_inc(h, s))
        return inc.n

    def wait(self, eng, sem, val):
        assert 0 <= val <= sem.n, f"wait for future value {val} > {sem.n} ({eng})"
        if val > 0:
            self.S[eng].append(lambda e: e.wait_ge(sem.h, val))

    def waitf(self, eng, free):
        if free is not None:
            self.wait(eng, free[0], free[1])

    def dma(self, qeng, _sem_ignored, out, in_, slow=False):
        p = self.pool[qeng]
        i = self.pool_rr[qeng] % len(p)
        self.pool_rr[qeng] += 1
        s = p[i]
        if s.n > 0:
            self.wait(qeng, s, s.n)
        if slow:
            self.em(qeng, lambda e: e.dma_start(out=out, in_=in_,
                                                allow_slow_non_contiguous=True), inc=s)
        else:
            self.em(qeng, lambda e: e.dma_start(out=out, in_=in_), inc=s)
        return (s, s.n)

    def dbg(self, name, view, src_sem=None):
        if not self.debug:
            return
        if self.taps is not None and name not in self.taps:
            return
        rows, cols = view.shape[0], view.shape[1]
        dtt = view.dtype
        t = self.nc.dram_tensor("dbg_" + name, [rows, cols], dtt, kind="ExternalOutput").ap()
        self.dbg_names.append("dbg_" + name)
        for s in (self.sv, self.ss, self.st_, self.sg):
            if s.n:
                self.wait("gpsimd", s, s.n)
        tok = self.dma("gpsimd", None, t[:, :], view, slow=True)
        for eng in self.S:
            self.waitf(eng, tok)

    def barrier(self):
        for eng in self.S:
            for s in self.allsems:
                if s.n > 0:
                    self.wait(eng, s, s.n)

    def bank(self, i, eng, sem=None):
        self.waitf(eng, self.psum_free[i])
        return self.psum[i]

    def bank_consumed(self, i, sem, val):
        self.psum_free[i] = (sem, val)

    def _residents(self):
        A = Arena(self.arena_t.ap(), self.AR, "res")
        TA, L = self.TA, self.L
        self.ident = A.alloc(128)
        self.identb = A.alloc(128, BF16)
        self.zrow = A.alloc(max(TA, 64))
        self.iof1 = A.alloc(128)
        self.iop = A.alloc(1)
        self.k4v = A.alloc(1)
        self.selg = [A.alloc(KH) for _ in range(8)]
        self.Ee = [A.alloc(128, rows=32) for _ in range(8)]
        self.wdiag = [A.alloc(128) for _ in range(4)]
        self.cwt = A.alloc(4 * NCT)
        self.cbt = A.alloc(NCT)
        self.scv = A.alloc(1, rows=KH)
        self.w0v = A.alloc(1)
        self.w1v = A.alloc(1)
        self.w2nv = A.alloc(1)
        self.g2v = A.alloc(1)
        self.g2h = A.alloc(1)
        self.cav = [A.alloc(8) for _ in range(2)]
        self.sav = [A.alloc(8) for _ in range(2)]
        self.cbv = [A.alloc(8) for _ in range(2)]
        self.sbv = [A.alloc(8) for _ in range(2)]
        self.cbsxv = [A.alloc(8) for _ in range(2)]
        self.sbsxv = [A.alloc(8) for _ in range(2)]
        self.swv = A.alloc(32)
        self.swov = A.alloc(32)
        self.iswx = A.alloc(32)
        self.vrv = A.alloc(32)
        self.biT = A.alloc(32)
        self.sxv = A.alloc(1)
        self.shv = A.alloc(1)
        self.amaxh = A.alloc(1)
        self.amaxd = A.alloc(1)
        self.am8 = A.alloc(32)
        self.stats = A.alloc(2 * (B_PER_CORE * L // 128))
        self.carry = A.alloc(33)
        self.tails = A.alloc(3 * NCT)
        self.onesr = A.alloc(128, rows=1)
        self.rowt = A.alloc(D, rows=2)
        self.amt = A.alloc(1)
        self.amt2 = A.alloc(1)
        self.amt3 = A.alloc(1)
        self.consts = {}
        for val in (1e-5, -0.5, 1.0 / 128, 1e-4, 1.0, -2.0):
            t = A.alloc(1)
            self.nc.const_aps.register(F32, val, t[:, 0:1])
            self.consts[val] = t
        self.res_end = A.off
        self.gaB = A.alloc(D)
        self.WAhi = A.alloc(8 * TOTAL, BF16)
        self.WAlo = A.alloc(8 * TOTAL, BF16)
        self.WOhi = A.alloc(8 * D, BF16)
        self.WOlo = A.alloc(8 * D, BF16)
        self.resA_end = A.off

    def prologue(self):
        A = Arena(self.arena_t.ap(), self.AR, "prol", self.resA_end)
        nc, L = self.nc, self.L
        em, wait, dma, waitf = self.em, self.wait, self.dma, self.waitf
        sv, ss, st_, sg = self.sv, self.ss, self.st_, self.sg
        dW = dG = None

        wrem = [A.alloc(KH, rows=32) for _ in range(4)]
        wimm = [A.alloc(KH, rows=32) for _ in range(4)]
        ns1m = [A.alloc(1, rows=32) for _ in range(4)]
        ns2m = [A.alloc(1, rows=32) for _ in range(4)]
        wef0 = A.alloc(KH, rows=32)
        wef1 = A.alloc(KH, rows=32)
        wef2 = A.alloc(KH, rows=32)
        ef32 = A.alloc(128, rows=32)
        rowv32 = A.alloc(1, rows=32)
        posb = A.alloc(L, rows=64)
        dsl28 = A.alloc(1, rows=28)
        asl4_t = A.alloc(1, rows=36)
        asl4 = asl4_t[32:36, 0:1]
        spl28 = A.alloc(1, rows=28)
        bias28 = A.alloc(1, rows=28)
        spla4_t = A.alloc(1, rows=36)
        spla4 = spla4_t[32:36, 0:1]
        der3 = A.alloc(3, rows=1)
        sct2 = A.alloc(D, rows=2)
        scT = A.alloc(16)
        modsb = A.alloc(6 * D, rows=2)
        tmp1 = A.alloc(1)
        wmb = [A.buf(512) for _ in range(3)]
        wqb = [A.buf(D) for _ in range(2)]
        qtmp = A.alloc(D)
        qtb = qtmp.bitcast(BF16)
        stage = A.buf(D, BF16)
        self.prol_end = A.off

        loads = [(self.cwt, self.cw_h[:, :]), (self.cbt, self.cb_h[:, :]),
                 (dsl28, self.dsl_h[:, :]), (asl4, self.asl_h[:, :]),
                 (self.scv, self.sc_h[:, :]), (der3, self.dlg_h[:, :]),
                 (self.biT, self.bi_h[:, :]),
                 (sct2, self.c_h[:, :])]
        for m in range(4):
            loads += [(wrem[m], self.wre_h.rearrange("(h m) j -> h m j", m=4)[:, m, :]),
                      (wimm[m], self.wim_h.rearrange("(h m) j -> h m j", m=4)[:, m, :]),
                      (ns1m[m], self.ns_h.rearrange("(h m) c -> h m c", m=4)[:, m, 0:1]),
                      (ns2m[m], self.ns_h.rearrange("(h m) c -> h m c", m=4)[:, m, 1:2])]
        tok_loads = [dma("sync", None, dst, src, slow=True) for dst, src in loads]
        t_th = dma("gpsimd", None, self.bc_h[9:10, 0:2], self.th_h[:, :])
        waitf("gpsimd", t_th)
        t_g2 = dma("gpsimd", None, self.g2v, self.bc_h[9:10, 0:1].partition_broadcast(128))

        em("gpsimd", lambda e: e.iota(self.iof1, pattern=[[1, 128]], base=0,
                                      channel_multiplier=0, allow_small_or_imprecise_dtypes=True))
        em("gpsimd", lambda e: e.iota(self.iop, pattern=[[0, 1]], base=0,
                                      channel_multiplier=1, allow_small_or_imprecise_dtypes=True))
        em("gpsimd", lambda e: e.iota(ef32, pattern=[[1, 128]], base=0,
                                      channel_multiplier=0, allow_small_or_imprecise_dtypes=True))
        em("gpsimd", lambda e: e.iota(rowv32, pattern=[[0, 1]], base=0,
                                      channel_multiplier=1, allow_small_or_imprecise_dtypes=True))
        em("gpsimd", lambda e: e.iota(posb, pattern=[[1, L]], base=0,
                                      channel_multiplier=0, allow_small_or_imprecise_dtypes=True),
           inc=sg)
        v_iota = sg.n

        wait("vector", sg, v_iota)
        TS = ALU
        for val, t in self.consts.items():
            em("vector", lambda e, t=t, val=val: e.memset(t, float(val)))
        em("vector", lambda e: e.tensor_scalar(out=self.ident, in0=self.iof1, scalar1=self.iop,
                                               scalar2=None, op0=TS.is_equal, op1=TS.bypass))
        em("vector", lambda e: e.tensor_copy(out=self.identb, in_=self.ident))
        em("vector", lambda e: e.memset(self.zrow, 0.0))
        em("vector", lambda e: e.memset(self.onesr, 1.0))
        em("vector", lambda e: e.memset(self.amaxh, 0.0))
        em("vector", lambda e: e.memset(self.amaxd, 0.0))
        em("vector", lambda e: e.tensor_scalar(out=self.k4v, in0=self.iop, scalar1=1.0 / 32,
                                               scalar2=-0.484375, op0=TS.mult, op1=TS.add))
        em("vector", lambda e: e.drain())
        em("vector", lambda e: e.tensor_scalar(out=self.k4v, in0=self.k4v, scalar1=RC,
                                               scalar2=RC, op0=TS.add, op1=TS.subtract))
        em("vector", lambda e: e.drain())
        for g in range(8):
            em("vector", lambda e, g=g: e.tensor_scalar(out=tmp1, in0=self.k4v, scalar1=1.0,
                                                        scalar2=float(4 * g), op0=TS.mult, op1=TS.add))
            em("vector", lambda e: e.drain())
            em("vector", lambda e, g=g: e.tensor_scalar(out=self.selg[g], in0=self.iof1[:, 0:KH],
                                                        scalar1=tmp1, scalar2=None,
                                                        op0=TS.is_equal, op1=TS.bypass))
        em("vector", lambda e: e.tensor_scalar(out=ef32, in0=ef32, scalar1=1.0 / 32,
                                               scalar2=-0.484375, op0=TS.mult, op1=TS.add))
        em("vector", lambda e: e.drain())
        em("vector", lambda e: e.tensor_scalar(out=ef32, in0=ef32, scalar1=RC,
                                               scalar2=RC, op0=TS.add, op1=TS.subtract))
        em("vector", lambda e: e.drain())
        for g in range(8):
            em("vector", lambda e, g=g: e.tensor_scalar(out=self.Ee[g], in0=ef32, scalar1=1.0,
                                                        scalar2=float(4 * g), op0=TS.mult, op1=TS.add))
            em("vector", lambda e: e.drain())
            em("vector", lambda e, g=g: e.tensor_scalar(out=self.Ee[g], in0=self.Ee[g], scalar1=rowv32,
                                                        scalar2=None, op0=TS.is_equal, op1=TS.bypass))
        waitf("vector", t_g2)
        em("vector", lambda e: e.tensor_scalar(out=self.g2h, in0=self.g2v, scalar1=0.5,
                                               scalar2=None, op0=TS.mult, op1=TS.bypass))

        for t_ in tok_loads:
            waitf("vector", t_)
        place_toks = []
        for X, (wsrc, nssrc, ma, mb, sub) in enumerate([
                (wrem, ns1m, 0, 3, False), (wrem, ns1m, 1, 2, False),
                (wimm, ns2m, 2, 1, True), (wimm, ns2m, 3, 0, True)]):
            for t_ in place_toks:
                waitf("vector", t_)
            place_toks = []
            em("vector", lambda e, w=wsrc[ma], n=nssrc[ma]: e.tensor_scalar(
                out=wef0, in0=w, scalar1=n, scalar2=None, op0=TS.mult, op1=TS.bypass))
            em("vector", lambda e: e.drain())
            em("vector", lambda e, w=wsrc[mb], n=nssrc[mb]: e.tensor_scalar(
                out=wef1, in0=w, scalar1=n, scalar2=None, op0=TS.mult, op1=TS.bypass))
            em("vector", lambda e: e.drain())
            op = TS.subtract if sub else TS.add
            em("vector", lambda e, op=op: e.tensor_tensor(out=wef2, in0=wef0, in1=wef1,
                                                          op=op))
            v = em("vector", lambda e, X=X: e.memset(self.wdiag[X], 0.0), inc=sv)
            wait("gpsimd", sv, v)
            for r in range(4):
                place_toks.append(dma("gpsimd", None,
                                      self.wdiag[X][32 * r:32 * r + 32, 32 * r:32 * r + 32], wef2))

        for t_ in tok_loads:
            waitf("scalar", t_)
        wait("scalar", sg, v_iota)
        em("scalar", lambda e: e.activation(out=spl28, in_=dsl28, func=AF.Exp))
        em("scalar", lambda e: e.activation(out=spl28, in_=spl28, func=AF.Ln, bias=1.0))
        em("scalar", lambda e: e.activation(out=spla4, in_=asl4, func=AF.Exp))
        v = em("scalar", lambda e: e.activation(out=spla4, in_=spla4, func=AF.Ln, bias=1.0), inc=ss)
        wait("vector", ss, v)
        em("vector", lambda e: e.tensor_scalar(out=bias28, in0=spl28, scalar1=float(-(L - 1)),
                                               scalar2=None, op0=TS.mult, op1=TS.bypass))
        v = em("vector", lambda e: e.tensor_scalar(out=spla4, in0=spla4, scalar1=-1.0,
                                                   scalar2=None, op0=TS.mult, op1=TS.bypass), inc=sv)
        wait("scalar", sv, v)
        em("scalar", lambda e: e.activation(out=posb[0:28, :], in_=posb[0:28, :], func=AF.Exp,
                                            scale=spl28, bias=bias28))
        v = em("scalar", lambda e: e.activation(out=posb[32:36, :], in_=posb[32:36, :], func=AF.Exp,
                                                scale=spla4), inc=ss)
        wait("gpsimd", ss, v)
        dma("gpsimd", None, self.tw_h[0:28, :], posb[0:28, :])
        dma("gpsimd", None, self.tw_h[28:32, :], posb[32:36, :])

        der128 = A.alloc(4)
        der2 = A.alloc(4)
        t_der = dma("gpsimd", None, der128[:, 0:3], self.dlg_h[0:1, :].partition_broadcast(128))
        waitf("scalar", t_der)
        v = em("scalar", lambda e: e.activation(out=der128[:, 0:3], in_=der128[:, 0:3],
                                                func=AF.Exp), inc=ss)
        wait("vector", ss, v)
        em("vector", lambda e: e.tensor_reduce(out=der128[:, 3:4], in_=der128[:, 0:3],
                                               axis=AX.X, op=TS.add))
        em("vector", lambda e: e.drain())
        em("vector", lambda e: e.reciprocal(out=der128[:, 3:4], in_=der128[:, 3:4]))
        em("vector", lambda e: e.drain())
        em("vector", lambda e: e.tensor_scalar(out=self.w0v, in0=der128[:, 0:1],
                                               scalar1=der128[:, 3:4], scalar2=None,
                                               op0=TS.mult, op1=TS.bypass))
        em("vector", lambda e: e.tensor_scalar(out=self.w1v, in0=der128[:, 1:2],
                                               scalar1=der128[:, 3:4], scalar2=None,
                                               op0=TS.mult, op1=TS.bypass))
        em("vector", lambda e: e.tensor_scalar(out=der2[:, 0:1], in0=der128[:, 3:4],
                                               scalar1=-1.0, scalar2=None,
                                               op0=TS.mult, op1=TS.bypass))
        em("vector", lambda e: e.drain())
        em("vector", lambda e: e.tensor_scalar(out=self.w2nv, in0=der128[:, 2:3],
                                               scalar1=der2[:, 0:1], scalar2=None,
                                               op0=TS.mult, op1=TS.bypass))
        em("vector", lambda e: e.drain())
        self.dbg("softrow", der128)
        self.dbg("w2n_early", self.w2nv)

        for t_ in tok_loads:
            waitf("scalar", t_)
        v = em("scalar", lambda e: e.activation(out=sct2, in_=sct2, func=AF.Silu), inc=ss)
        wait("tensor", ss, v)
        wait("tensor", sv, sv.n)
        b0 = self.bank(0, "tensor")
        for j in range(8):
            em("tensor", lambda e, j=j, b0=b0: e.transpose(b0[0:128, 2 * j:2 * j + 2],
                                                           sct2[0:2, 128 * j:128 * j + 128],
                                                           self.ident[0:2, 0:2]),
               inc=(st_ if j == 7 else None))
        wait("scalar", st_, st_.n)
        v = em("scalar", lambda e, b0=b0: e.copy(out=scT, in_=b0[0:128, 0:16]), inc=ss)
        self.bank_consumed(0, ss, v)
        v_scT = ss.n
        self.dbg("scT", scT)
        modsbT = A.alloc(96)
        mview = modsbT.rearrange("p (c b) -> p c b", b=2)
        for cg in range(12):
            bis = [1 + q for q in range(4)]
            bks = [self.bank(bi, "tensor") for bi in bis]
            for k in range(8):
                bu = wmb[(cg * 8 + k) % 3]
                self.waitf("sync", bu.free)
                t_ = dma("sync", None, bu.v,
                         self.wm_h[128 * k:128 * k + 128, 512 * cg:512 * cg + 512])
                waitf("tensor", t_)
                if cg == 0 and k == 0:
                    wait("tensor", ss, v_scT)
                for q in range(4):
                    v = em("tensor", lambda e, bk=bks[q], k=k, bu=bu, q=q: e.matmul(
                        bk[0:128, 0:2], bu.v[:, 128 * q:128 * q + 128], scT[:, 2 * k:2 * k + 2],
                        start=(k == 0), stop=(k == 7)),
                        inc=(st_ if q == 3 else None))
                bu.free = (st_, st_.n)
            wait("scalar", st_, st_.n)
            for q in range(4):
                ctblk = 4 * cg + q
                v = em("scalar", lambda e, bk=bks[q], ctblk=ctblk: e.copy(
                    out=modsbT[:, 2 * ctblk:2 * ctblk + 2], in_=bk[0:128, 0:2]), inc=ss)
                self.bank_consumed(bis[q], ss, v)
        v_mod = ss.n
        self.dbg("modsbT", modsbT)

        wait("vector", ss, v_mod)
        t_gprev = None
        gacol = A.alloc(8)
        gacol2 = A.alloc(8)
        gacol3 = A.alloc(8)
        gacol4 = A.alloc(8)
        gacol5 = A.alloc(8)
        for b in range(2):
            for (dst, blk0, plus1) in ((self.cav[b], 8, True), (self.cbv[b], 32, True),
                                       (self.sav[b], 0, False), (self.sbv[b], 24, False)):
                em("vector", lambda e, dst=dst, blk0=blk0, p1=plus1, b=b: e.tensor_scalar(
                    out=dst, in0=mview[:, blk0:blk0 + 8, b],
                    scalar1=(1.0 if p1 else 0.0), scalar2=None, op0=TS.add, op1=TS.bypass))
            for (blk0, row) in ((16, b), (40, 2 + b)):
                em("vector", lambda e, b=b, blk0=blk0: e.tensor_scalar(
                    out=gacol2, in0=mview[:, blk0:blk0 + 8, b], scalar1=0.0,
                    scalar2=None, op0=TS.is_ge, op1=TS.bypass))
                em("vector", lambda e: e.drain())
                em("vector", lambda e: e.tensor_scalar(
                    out=gacol, in0=gacol2, scalar1=2.0, scalar2=-1.0,
                    op0=TS.mult, op1=TS.add))
                v = em("vector", lambda e: e.drain(), inc=sv)
                wait("scalar", sv, v)
                v = em("scalar", lambda e, b=b, blk0=blk0: e.activation(
                    out=gacol5, in_=mview[:, blk0:blk0 + 8, b], func=AF.Abs), inc=ss)
                wait("vector", ss, v)
                em("vector", lambda e: e.tensor_scalar(
                    out=gacol4, in0=gacol5, scalar1=0.25, scalar2=None,
                    op0=TS.max, op1=TS.bypass))
                em("vector", lambda e: e.drain())
                waitf("vector", t_gprev)
                em("vector", lambda e: e.tensor_tensor(
                    out=gacol3, in0=gacol, in1=gacol4, op=TS.mult))
                v = em("vector", lambda e: e.drain(), inc=sv)
                wait("gpsimd", sv, v)
                t_gprev = dma("gpsimd", None, self.bc_h[row:row + 1, :].rearrange(
                    "o (j p) -> o p j", p=128), gacol3, slow=True)

        for (w_h, out_h, svec, do_T) in ((self.wiT_h, self.wqi_h, self.swv, True),
                                         (self.wo_h, self.wqo_h, self.swov, False)):
            for s in range(32):
                bu = wqb[s % 2]
                self.waitf("sync", bu.free)
                t_ = dma("sync", None, bu.v, w_h[128 * s:128 * s + 128, :])
                waitf("vector", t_)
                waitf("vector", stage.free)
                em("vector", lambda e, bu=bu: e.tensor_reduce(
                    out=self.amt, in_=bu.v, axis=AX.X, op=TS.max, apply_absolute_value=True))
                em("vector", lambda e: e.drain())
                em("vector", lambda e: e.tensor_scalar(
                    out=self.amt2, in0=self.amt, scalar1=1e-8, scalar2=None, op0=TS.max, op1=TS.bypass))
                em("vector", lambda e: e.drain())
                em("vector", lambda e: e.reciprocal(out=self.amt3, in_=self.amt2))
                em("vector", lambda e: e.drain())
                em("vector", lambda e, svec=svec, s=s: e.tensor_scalar(
                    out=svec[:, s:s + 1], in0=self.amt3, scalar1=127.0, scalar2=None,
                    op0=TS.mult, op1=TS.bypass))
                em("vector", lambda e: e.drain())
                if s == 0 and do_T:
                    self.dbg("wslab0", bu.v)
                    self.dbg("amt0", self.amt3)
                em("vector", lambda e, bu=bu, svec=svec, s=s: e.tensor_scalar(
                    out=qtmp, in0=bu.v, scalar1=svec[:, s:s + 1], scalar2=RC,
                    op0=TS.mult, op1=TS.add))
                em("vector", lambda e: e.drain())
                em("vector", lambda e: e.tensor_scalar(
                    out=stage.v, in0=qtmp, scalar1=RC, scalar2=None,
                    op0=TS.subtract, op1=TS.bypass))
                v = em("vector", lambda e: e.drain(), inc=sv)
                bu.free = (sv, v)
                if not do_T:
                    wait("gpsimd", sv, v)
                    if s == 0:
                        self.dbg("woslab0", bu.v)
                        self.dbg("wostage0", stage.v)
                        self.dbg("wosvec", svec)
                    stage.free = dma("gpsimd", None, out_h[128 * s:128 * s + 128, :], stage.v)
                else:
                    wait("tensor", sv, v)
                    bk = self.bank(3, "tensor")
                    bkb = self.psum[3][:, :].bitcast(BF16)
                    for j in range(8):
                        em("tensor", lambda e, j=j, bkb=bkb: e.transpose(
                            bkb[:, 128 * j:128 * j + 128], stage.v[:, 128 * j:128 * j + 128],
                            self.identb), inc=(st_ if j == 7 else None))
                    vT = st_.n
                    stage.free = (st_, vT)
                    wait("scalar", st_, vT)
                    wait("scalar", sv, sv.n)
                    v2 = em("scalar", lambda e, bkb=bkb: e.copy(out=qtb[:, 0:1024], in_=bkb),
                            inc=ss)
                    self.bank_consumed(3, ss, v2)
                    wait("gpsimd", ss, v2)
                    qtoks = [dma("gpsimd", None,
                                 out_h[128 * j:128 * j + 128, 128 * s:128 * s + 128],
                                 qtb[:, 128 * j:128 * j + 128]) for j in range(8)]
                    for t_ in qtoks:
                        waitf("vector", t_)

        for (w_h, ncol, hi, lo) in ((self.wA_h, TOTAL, self.WAhi, self.WAlo),
                                    (self.wO_h, D, self.WOhi, self.WOlo)):
            for k in range(8):
                parts = [(o, min(1024, ncol - o)) for o in range(0, ncol, 1024)]
                for (o0, w) in parts:
                    bu = wqb[(k + o0) % 2]
                    self.waitf("sync", bu.free)
                    t_ = dma("sync", None, bu.v[:, 0:w], w_h[128 * k:128 * k + 128, o0:o0 + w])
                    waitf("scalar", t_)
                    vhi = em("scalar", lambda e, bu=bu, hi=hi, c0=ncol * k + o0, w=w: e.copy(
                        out=hi[:, c0:c0 + w], in_=bu.v[:, 0:w]), inc=ss)
                    wait("vector", ss, vhi)
                    v = em("vector", lambda e, bu=bu, hi=hi, lo=lo, c0=ncol * k + o0, w=w:
                           e.tensor_tensor(out=lo[:, c0:c0 + w], in0=bu.v[:, 0:w],
                                           in1=hi[:, c0:c0 + w], op=TS.subtract), inc=sv)
                    bu.free = (sv, v)

        self.dbg("cav0", self.cav[0])
        self.dbg("sav0", self.sav[0])
        self.dbg("cbv0", self.cbv[0])
        self.dbg("sbv0", self.sbv[0])
        self.dbg("swv", self.swv)
        self.dbg("swov", self.swov)
        self.dbg("w012", self.w0v)
        self.dbg("w1", self.w1v)
        self.dbg("w2n", self.w2nv)
        self.dbg("g2", self.g2v)
        self.dbg("wdiag0", self.wdiag[0])
        self.dbg("selg1", self.selg[1])
        self.dbg("Ee1", self.Ee[1])
        self.dbg("WAhi", self.WAhi[:, 0:TOTAL])
        self.dbg("WAlo", self.WAlo[:, 0:TOTAL])
        self.barrier()

    def alloc_A(self):
        A = Arena(self.arena_t.ap(), self.AR, "A", self.resA_end)
        TA = self.TA
        self.x_tm = [A.buf(D) for _ in range(2)]
        self.t_tm = [A.buf(D) for _ in range(2)]
        self.t2_tm = A.buf(D)
        self.t_fm = A.buf(8 * TA)
        self.t2fm = self.t_fm
        self.sq_scr = A.alloc(D)
        self.lnrow = A.alloc(12)
        self.twc = A.buf(TA, rows=KH)
        self.ha_hi = A.buf(8 * TA, BF16)
        self.ha_lo = A.buf(8 * TA, BF16)
        self.ha_t = A.alloc(TA)
        self.haf = A.buf(8 * TA)
        self.zx = [A.buf(TA + 3) for _ in range(3)]
        self.ctmp = [A.alloc(TA) for _ in range(3)]
        self.xv_u = [A.buf(TA) for _ in range(2)]
        self.uu = [A.buf(TA) for _ in range(2)]
        self.silu_g = A.buf(8 * TA)
        self.sraw_u = A.buf(TA, rows=KH)
        self.tmpP = [A.alloc(TA) for _ in range(2)]
        self.s1t = [A.buf(TA) for _ in range(2)]
        self.c1t = [A.buf(TA) for _ in range(2)]
        self.s3t = [A.buf(TA) for _ in range(2)]
        self.c3t = [A.buf(TA) for _ in range(2)]
        self.p1t = A.alloc(TA)
        self.polyt = [A.buf(TA) for _ in range(2)]
        self.ppt = [A.buf(TA) for _ in range(2)]
        self.scin = [[A.buf(TA) for _ in range(4)] for _ in range(2)]
        self.scout = [[A.buf(TA) for _ in range(4)] for _ in range(2)]
        self.sqt = [A.buf(TA) for _ in range(2)]
        self.pwt = A.buf(TA, rows=KH)
        self.dent = A.buf(TA, rows=KH)
        self.g32 = [A.alloc(TA, rows=KH) for _ in range(3)]
        self.gfin = A.buf(TA, rows=KH)
        self.pwE_sb = [A.buf(TA) for _ in range(2)]
        self.y_raw = A.buf(8 * TA)
        self.gft = [A.alloc(TA) for _ in range(2)]
        self.g_hi = A.buf(8 * TA, BF16)
        self.g_lo = A.buf(8 * TA, BF16)
        self.xres_tm = [A.buf(D) for _ in range(2)]
        self.hbt = A.buf(TA)
        assert A.off <= self.AR, A.off
        self.A_end = A.off

    def _inproj_mm(self, ct):
        TA = self.TA
        em, wait = self.em, self.wait
        ss, st_ = self.ss, self.st_
        w = _rows(ct)
        i = self._zq
        self._zq += 1
        bi = 1 + (i % 2)
        bk = self.bank(bi, "tensor")
        for k in range(8):
            sW = slice(TOTAL * k + 128 * ct, TOTAL * k + 128 * ct + w)
            sA = slice(TA * k, TA * (k + 1))
            em("tensor", lambda e, bk=bk, sW=sW, sA=sA, w=w, k=k: e.matmul(
                bk[0:w, 0:TA], self.WAhi[:, sW], self.ha_hi.v[:, sA],
                start=(k == 0), stop=False))
            em("tensor", lambda e, bk=bk, sW=sW, sA=sA, w=w: e.matmul(
                bk[0:w, 0:TA], self.WAhi[:, sW], self.ha_lo.v[:, sA],
                start=False, stop=False))
            em("tensor", lambda e, bk=bk, sW=sW, sA=sA, w=w, k=k: e.matmul(
                bk[0:w, 0:TA], self.WAlo[:, sW], self.ha_hi.v[:, sA],
                start=False, stop=(k == 7)), inc=(st_ if k == 7 else None))
        vmm = st_.n
        zb = self.zx[i % 3]
        self.waitf("scalar", zb.free)
        wait("scalar", st_, vmm)
        vz = em("scalar", lambda e, zb=zb, bk=bk, w=w: e.copy(
            out=zb.v[0:w, 3:3 + TA], in_=bk[0:w, 0:TA]), inc=ss)
        self.bank_consumed(bi, ss, vz)
        if getattr(self, "_tap16", False) and ct == 16:
            self.dbg("z16", zb.v[0:32, :])
        self.zpend[ct] = (zb, vz)

    def _conv_from_z(self, b, ct, dst_kind, par):
        TA = self.TA
        em, wait = self.em, self.wait
        sv, ss = self.sv, self.ss
        TS = ALU
        w = _rows(ct)
        zb, vz = self.zpend.pop(ct)
        wait("vector", ss, vz)
        em("vector", lambda e, zb=zb, ct=ct, w=w: e.tensor_copy(
            out=zb.v[0:w, 0:3], in_=self.tails[0:w, 3 * ct:3 * ct + 3]))
        c_a, c_b = self.ctmp[0], self.ctmp[1]
        em("vector", lambda e, zb=zb, ct=ct, w=w, c_a=c_a: e.tensor_scalar(
            out=c_a[0:w, :], in0=zb.v[0:w, 0:TA], scalar1=self.cwt[0:w, ct:ct + 1],
            scalar2=None, op0=TS.mult, op1=TS.bypass))
        em("vector", lambda e, zb=zb, ct=ct, w=w, c_a=c_a, c_b=c_b: e.scalar_tensor_tensor(
            out=c_b[0:w, :], in0=zb.v[0:w, 1:1 + TA], scalar=self.cwt[0:w, NCT + ct:NCT + ct + 1],
            in1=c_a[0:w, :], op0=TS.mult, op1=TS.add))
        em("vector", lambda e, zb=zb, ct=ct, w=w, c_a=c_a, c_b=c_b: e.scalar_tensor_tensor(
            out=c_a[0:w, :], in0=zb.v[0:w, 2:2 + TA], scalar=self.cwt[0:w, 2 * NCT + ct:2 * NCT + ct + 1],
            in1=c_b[0:w, :], op0=TS.mult, op1=TS.add))
        lastw = self.cwt[0:w, 3 * NCT + ct:3 * NCT + ct + 1]
        if dst_kind == "xv":
            dst = self.xv_u[par]
            self.waitf("vector", dst.free)
            vconv = em("vector", lambda e, zb=zb, w=w, c_a=c_a, dst=dst, lastw=lastw:
                       e.scalar_tensor_tensor(out=dst.v, in0=zb.v[0:w, 3:3 + TA],
                                              scalar=lastw, in1=c_a[0:w, :],
                                              op0=TS.mult, op1=TS.add), inc=sv)
            dst.ready = (sv, vconv)
        elif dst_kind == "gate":
            dst = self.ctmp[2]
            self.waitf("vector", getattr(self, "ctmp2_free", None))
            vconv = em("vector", lambda e, zb=zb, w=w, c_a=c_a, dst=dst, lastw=lastw:
                       e.scalar_tensor_tensor(out=dst[0:w, :], in0=zb.v[0:w, 3:3 + TA],
                                              scalar=lastw, in1=c_a[0:w, :],
                                              op0=TS.mult, op1=TS.add), inc=sv)
            wait("scalar", sv, vconv)
            self.waitf("scalar", self.silu_g.free)
            vsl = em("scalar", lambda e, ct=ct, dst=dst: e.activation(
                out=self.silu_g.v[:, TA * (ct - 8):TA * (ct - 7)], in_=dst,
                func=AF.Silu, bias=self.cbt[:, ct:ct + 1]), inc=ss)
            self.ctmp2_free = (ss, vsl)
            dst = None
        else:
            dst = self.sraw_u
            self.waitf("vector", dst.free)
            vconv = em("vector", lambda e, zb=zb, w=w, c_a=c_a, dst=dst, lastw=lastw:
                       e.scalar_tensor_tensor(out=dst.v, in0=zb.v[0:w, 3:3 + TA],
                                              scalar=lastw, in1=c_a[0:w, :],
                                              op0=TS.mult, op1=TS.add), inc=sv)
            dst.ready = (sv, vconv)
        vtl = em("vector", lambda e, zb=zb, ct=ct, w=w: e.tensor_copy(
            out=self.tails[0:w, 3 * ct:3 * ct + 3], in_=zb.v[0:w, TA:TA + 3]), inc=sv)
        zb.free = (sv, vtl)
        return dst

    def a_chunk(self, b, ci, itile):
        TA = self.TA
        t0 = ci * TA
        em, wait, dma, waitf = self.em, self.wait, self.dma, self.waitf
        sv, ss, st_, sg = self.sv, self.ss, self.st_, self.sg
        TS = ALU

        if ci == 0:
            em("vector", lambda e: e.memset(self.tails, 0.0))
            em("vector", lambda e: e.memset(self.carry, 0.0))
            wait("gpsimd", sv, sv.n)
            self.t_gaB = dma("gpsimd", None, self.gaB,
                             self.bc_h[b:b + 1, :].partition_broadcast(128))
        t_gaB = self.t_gaB

        xb = self.x_tm[ci % 2]
        self.waitf("sync", xb.free)
        t_x = dma("sync", None, xb.v, self.x_h[b, t0:t0 + TA, :])
        self.waitf("sync", self.twc.free)
        t_twc = dma("sync", None, self.twc.v[0:KH, :], self.tw_h[:, t0:t0 + TA])

        waitf("scalar", t_x)
        self.waitf("scalar", getattr(self, "lnrow_free", None))
        v = em("scalar", lambda e, xb=xb: e.activation(out=self.sq_scr, in_=xb.v,
                                                       func=AF.Square), inc=ss)
        waitf("vector", t_x)
        wait("vector", ss, v)
        em("vector", lambda e: e.tensor_reduce(out=self.lnrow[:, 1:2], in_=self.sq_scr,
                                               axis=AX.X, op=TS.add))
        em("vector", lambda e: e.drain())
        em("vector", lambda e, xb=xb: e.tensor_reduce(out=self.lnrow[:, 0:1], in_=xb.v,
                                                      axis=AX.X, op=TS.add))
        em("vector", lambda e: e.drain())
        em("vector", lambda e: e.tensor_scalar(out=self.lnrow[:, 0:1], in0=self.lnrow[:, 0:1],
                                               scalar1=1.0 / D, scalar2=None, op0=TS.mult, op1=TS.bypass))
        em("vector", lambda e: e.drain())
        em("vector", lambda e: e.tensor_tensor(out=self.lnrow[:, 2:3], in0=self.lnrow[:, 0:1],
                                               in1=self.lnrow[:, 0:1], op=TS.mult))
        em("vector", lambda e: e.drain())
        v = em("vector", lambda e: e.scalar_tensor_tensor(out=self.lnrow[:, 3:4], in0=self.lnrow[:, 1:2],
                                                          scalar=1.0 / D, in1=self.lnrow[:, 2:3],
                                                          op0=TS.mult, op1=TS.subtract), inc=sv)
        wait("scalar", sv, v)
        em("scalar", lambda e: e.activation(out=self.lnrow[:, 4:5], in_=self.lnrow[:, 3:4],
                                            func=AF.Ln, bias=1e-5))
        v = em("scalar", lambda e: e.activation(out=self.lnrow[:, 4:5], in_=self.lnrow[:, 4:5],
                                                func=AF.Exp, scale=-0.5), inc=ss)
        tb = self.t_tm[ci % 2]
        self.waitf("vector", tb.free)
        wait("vector", ss, v)
        if b == 0 and ci == 0:
            self.dbg("lnrow", self.lnrow)
            self.dbg("sqscr", self.sq_scr)
            self.dbg("xb0", xb.v)
        vt = em("vector", lambda e, xb=xb, tb=tb: e.tensor_scalar(
            out=tb.v, in0=xb.v, scalar1=self.lnrow[:, 0:1], scalar2=self.lnrow[:, 4:5],
            op0=TS.subtract, op1=TS.mult), inc=sv)

        wait("tensor", sv, vt)
        self.waitf("tensor", self.t_fm.free)
        for r in range(2):
            bk = self.bank(0, "tensor")
            for j in range(4):
                jj = 4 * r + j
                em("tensor", lambda e, bk=bk, j=j, jj=jj, tb=tb: e.transpose(
                    bk[:, 128 * j:128 * j + 128], tb.v[:, 128 * jj:128 * jj + 128], self.ident),
                   inc=(st_ if j == 3 else None))
            wait("scalar", st_, st_.n)
            v = em("scalar", lambda e, bk=bk, r=r: e.copy(
                out=self.t_fm.v[:, 512 * r:512 * r + 512], in_=bk[:, :]), inc=ss)
            self.bank_consumed(0, ss, v)
        tb.free = (self.st_, self.st_.n)
        v_tfm = ss.n
        if b == 0 and ci == 0:
            self.dbg("tfm", self.t_fm.v)

        wait("vector", ss, v_tfm)
        self.waitf("vector", self.ha_hi.free)
        self.waitf("vector", self.ha_lo.free)
        self.waitf("vector", self.haf.free)
        for j in range(8):
            sl = slice(TA * j, TA * (j + 1))
            em("vector", lambda e, j=j, sl=sl: e.tensor_scalar(
                out=self.haf.v[:, sl], in0=self.t_fm.v[:, sl], scalar1=self.cav[b][:, j:j + 1],
                scalar2=self.sav[b][:, j:j + 1], op0=TS.mult, op1=TS.add))
        vf = em("vector", lambda e: e.drain(),
                inc=sv)
        wait("scalar", sv, vf)
        vhi = em("scalar", lambda e: e.copy(out=self.ha_hi.v, in_=self.haf.v), inc=ss)
        wait("vector", ss, vhi)
        vha = em("vector", lambda e: e.tensor_tensor(
            out=self.ha_lo.v, in0=self.haf.v, in1=self.ha_hi.v, op=TS.subtract), inc=sv)
        self.haf.free = (sv, vha)
        self.t_fm.free = (sv, vf)
        wait("tensor", sv, vha)
        wait("tensor", ss, vhi)

        self._tap16 = (b == 0 and ci == 0)
        self._zq = 0
        self.zpend = {}
        PUMP = [16] + list(range(16))
        self._inproj_mm(16)
        self._tap16 = False
        self._inproj_mm(0)
        self._conv_from_z(b, 16, "sraw", 0)
        em("vector", lambda e: e.tensor_scalar(
            out=self.g32[0][0:KH, :], in0=self.sraw_u.v[0:KH, :],
            scalar1=self.cbt[0:KH, 16:17], scalar2=self.scv,
            op0=TS.add, op1=TS.mult))
        v = em("vector", lambda e: e.tensor_scalar(
            out=self.g32[0][0:KH, :], in0=self.g32[0][0:KH, :], scalar1=20.0, scalar2=-20.0,
            op0=TS.min, op1=TS.max), inc=sv)
        self.sraw_u.free = (sv, v)
        wait("scalar", sv, v)
        self.waitf("scalar", self.pwt.free)
        v = em("scalar", lambda e: e.activation(
            out=self.pwt.v[0:KH, :], in_=self.g32[0][0:KH, :], func=AF.Exp), inc=ss)
        wait("vector", ss, v)
        waitf("vector", t_twc)
        em("vector", lambda e: e.tensor_tensor(
            out=self.pwt.v[0:KH, :], in0=self.pwt.v[0:KH, :],
            in1=self.twc.v[0:KH, :], op=TS.mult))
        self.waitf("vector", self.dent.free)
        em("vector", lambda e: e.tensor_tensor_scan(
            self.dent.v[0:KH, :], self.zrow[0:KH, 0:TA], self.pwt.v[0:KH, :],
            self.carry[0:KH, 32:33], TS.add, TS.add))
        em("vector", lambda e: e.drain())
        v_pw = em("vector", lambda e: e.tensor_copy(
            out=self.carry[0:KH, 32:33], in_=self.dent.v[0:KH, TA - 1:TA]), inc=sv)
        self.twc.free = (sv, v_pw)
        if b == 0 and ci == 0:
            self.dbg("sraw", self.sraw_u.v[0:KH, :])
            self.dbg("pwt", self.pwt.v[0:KH, :])
            self.dbg("dent", self.dent.v[0:KH, :])

        self.bank(3, "tensor")
        gstate = {}

        def stage1a(g):
            par = g % 2
            st = {}
            ub = self._conv_from_z(b, g, "xv", par)
            aux = 4 + par
            bk = self.bank(aux, "tensor")
            wait("tensor", sv, v_pw)
            v = em("tensor", lambda e, bk=bk, g=g: e.matmul(
                bk[:, 0:TA], self.Ee[g], self.pwt.v[0:KH, :], start=True, stop=True), inc=st_)
            st["last_pwE"] = v
            wait("scalar", st_, v)
            pwb = self.pwE_sb[par]
            self.waitf("scalar", pwb.free)
            v = em("scalar", lambda e, bk=bk, pwb=pwb: e.copy(out=pwb.v, in_=bk[:, 0:TA]), inc=ss)
            self.bank_consumed(aux, ss, v)
            st["v_pwE"] = ss.n
            st["pwb"] = pwb
            uub = self.uu[par]
            self.waitf("vector", uub.free)
            wait("vector", sv, ub.ready[1])
            vu = em("vector", lambda e, g=g, ub=ub, uub=uub: e.tensor_scalar(
                out=uub.v, in0=ub.v, scalar1=self.cbt[:, g:g + 1], scalar2=None,
                op0=TS.add, op1=TS.bypass), inc=sv)
            ub.free = (sv, vu)
            st["uub"] = uub
            bufs = (self.s1t[par], self.c1t[par], self.s3t[par], self.c3t[par],
                    self.ppt[par], self.polyt[par])
            st["bufs"] = bufs
            s1b, c1b, s3b, c3b, ppb, pob = bufs
            for bb in bufs:
                self.waitf("scalar", bb.free)
            wait("scalar", sv, vu)
            em("scalar", lambda e, uub=uub, ppb=ppb: e.activation(
                out=ppb.v, in_=uub.v, func=AF.Sin, scale=self.g2h))
            em("scalar", lambda e, uub=uub, s1b=s1b: e.activation(
                out=s1b.v, in_=uub.v, func=AF.Sin, scale=self.g2v))
            em("scalar", lambda e, ppb=ppb: e.activation(out=ppb.v, in_=ppb.v,
                                                         func=AF.Square))
            em("scalar", lambda e, ppb=ppb, c1b=c1b: e.activation(
                out=c1b.v, in_=ppb.v, func=AF.Copy, scale=-2.0, bias=1.0))
            em("scalar", lambda e, s1b=s1b, s3b=s3b: e.activation(
                out=s3b.v, in_=s1b.v, func=AF.Square))
            em("scalar", lambda e, c1b=c1b, c3b=c3b: e.activation(
                out=c3b.v, in_=c1b.v, func=AF.Square))
            v = em("scalar", lambda e, uub=uub, pob=pob: e.activation(
                out=pob.v, in_=uub.v, func=AF.Square), inc=ss)
            st["v_sins"] = v
            gstate[g] = st

        def stage1b(g):
            par = g % 2
            st = gstate[g]
            s1b, c1b, s3b, c3b, ppb, pob = st["bufs"]
            uub, pwb = st["uub"], st["pwb"]
            wait("vector", ss, max(st["v_sins"], st["v_pwE"]))
            sins = self.scin[par]
            for X in range(4):
                self.waitf("vector", sins[X].free)
            em("vector", lambda e, uub=uub: e.tensor_scalar(
                out=self.p1t, in0=uub.v, scalar1=self.w1v, scalar2=self.w0v,
                op0=TS.mult, op1=TS.add))
            em("vector", lambda e, pob=pob: e.scalar_tensor_tensor(
                out=self.tmpP[0], in0=pob.v, scalar=self.w2nv, in1=self.p1t,
                op0=TS.mult, op1=TS.add))
            em("vector", lambda e, ppb=ppb, pwb=pwb: e.tensor_tensor(
                out=ppb.v, in0=self.tmpP[0], in1=pwb.v, op=TS.mult))
            em("vector", lambda e, ppb=ppb, c1b=c1b, d=sins[1]: e.tensor_tensor(
                out=d.v, in0=ppb.v, in1=c1b.v, op=TS.mult))
            em("vector", lambda e, ppb=ppb, s1b=s1b, d=sins[2]: e.tensor_tensor(
                out=d.v, in0=ppb.v, in1=s1b.v, op=TS.mult))
            em("vector", lambda e, s3b=s3b: e.tensor_scalar(
                out=self.tmpP[1], in0=s3b.v, scalar1=-4.0, scalar2=3.0, op0=TS.mult, op1=TS.add))
            em("vector", lambda e, s3b=s3b, s1b=s1b: e.tensor_tensor(
                out=s3b.v, in0=self.tmpP[1], in1=s1b.v, op=TS.mult))
            em("vector", lambda e, ppb=ppb, s3b=s3b, d=sins[3]: e.tensor_tensor(
                out=d.v, in0=ppb.v, in1=s3b.v, op=TS.mult))
            em("vector", lambda e, c3b=c3b: e.tensor_scalar(
                out=self.tmpP[1], in0=c3b.v, scalar1=4.0, scalar2=-3.0, op0=TS.mult, op1=TS.add))
            em("vector", lambda e, c3b=c3b, c1b=c1b: e.tensor_tensor(
                out=c3b.v, in0=self.tmpP[1], in1=c1b.v, op=TS.mult))
            em("vector", lambda e, ppb=ppb, c3b=c3b, d=sins[0]: e.tensor_tensor(
                out=d.v, in0=ppb.v, in1=c3b.v, op=TS.mult))
            vprod = em("vector", lambda e: e.drain(), inc=sv)
            for X in range(4):
                sins[X].ready = vprod
            for bb in st["bufs"]:
                bb.free = (sv, vprod)
            uub.free = (sv, vprod)
            pwb.free = (sv, vprod)

        def stage2(g):
            par = g % 2
            st = gstate[g]
            sins = self.scin[par]
            souts = self.scout[par]
            auxY = 4 + (g + 1) % 2
            bkY = self.bank(auxY, "tensor")
            for X in range(4):
                self.waitf("vector", souts[X].free)
                em("vector", lambda e, X=X, g=g, d=souts[X], s=sins[X]: e.tensor_tensor_scan(
                    d.v, self.zrow[:, 0:TA], s.v, self.carry[:, 4 * g + X:4 * g + X + 1],
                    TS.add, TS.add))
            em("vector", lambda e: e.drain())
            for X in range(4):
                em("vector", lambda e, X=X, g=g, d=souts[X]: e.tensor_copy(
                    out=self.carry[:, 4 * g + X:4 * g + X + 1], in_=d.v[:, TA - 1:TA]))
            v = em("vector", lambda e: e.drain(), inc=sv)
            for X in range(4):
                sins[X].free = (sv, v)
                souts[X].ready = (sv, v)
            for X in range(4):
                sb = self.sqt[X % 2]
                self.waitf("scalar", sb.free)
                wait("scalar", sv, souts[X].ready[1])
                v = em("scalar", lambda e, sb=sb, d=souts[X]: e.activation(
                    out=sb.v, in_=d.v, func=AF.Square), inc=ss)
                wait("tensor", ss, v)
                v = em("tensor", lambda e, sb=sb, g=g, X=X: e.matmul(
                    self.psum[3][0:KH, 0:TA], self.selg[g], sb.v,
                    start=(g == 0 and X == 0), stop=(g == 7 and X == 3)), inc=st_)
                sb.free = (st_, v)
                wait("tensor", sv, souts[X].ready[1])
                v = em("tensor", lambda e, d=souts[X], X=X, bkY=bkY: e.matmul(
                    bkY[:, 0:TA], self.wdiag[X], d.v, start=(X == 0), stop=(X == 3)),
                    inc=(st_ if X == 3 else None))
                souts[X].free = (st_, st_.n)
            vY = st_.n
            wait("scalar", st_, vY)
            v = em("scalar", lambda e, bkY=bkY, g=g: e.copy(
                out=self.y_raw.v[:, TA * g:TA * (g + 1)], in_=bkY[:, 0:TA]), inc=ss)
            self.bank_consumed(auxY, ss, v)
            if b == 0 and ci == 0 and g == 0:
                self.dbg("uu0", self.uu[0].v)
                self.dbg("s1_0", self.s1t[0].v)
                self.dbg("c3_0", self.c3t[0].v)
                self.dbg("pp0", self.ppt[0].v)
                self.dbg("scoutA0", self.scout[0][0].v)
                self.dbg("scoutD0", self.scout[0][3].v)
                self.dbg("pwE0", self.pwE_sb[0].v)

        for gg in range(9):
            if gg < 8:
                self._inproj_mm(PUMP[gg + 2])
                stage1a(gg)
            if gg > 0:
                stage2(gg - 1)
            if gg < 8:
                stage1b(gg)
        v_S_done = st_.n
        v_y_done = ss.n
        self.pwt.free = (st_, gstate[7]["last_pwE"])

        for ct in range(8, 16):
            if ct + 1 <= 15:
                self._inproj_mm(ct + 1)
            self._conv_from_z(b, ct, "gate", 0)
        self.ha_hi.free = (st_, st_.n)
        self.ha_lo.free = (st_, st_.n)
        v_gates = ss.n

        em("vector", lambda e: e.tensor_scalar(out=self.g32[1][0:KH, :], in0=self.dent.v[0:KH, :],
                                               scalar1=1e-4, scalar2=None, op0=TS.max, op1=TS.bypass))
        em("vector", lambda e: e.drain())
        v = em("vector", lambda e: e.reciprocal(out=self.g32[1][0:KH, :], in_=self.g32[1][0:KH, :]),
               inc=sv)
        em("vector", lambda e: e.drain())
        self.dent.free = (sv, v)
        em("vector", lambda e: e.tensor_tensor(out=self.g32[2][0:KH, :], in0=self.g32[1][0:KH, :],
                                               in1=self.g32[1][0:KH, :], op=TS.mult))
        wait("vector", st_, v_S_done)
        v = em("vector", lambda e: e.tensor_tensor(out=self.g32[0][0:KH, :],
                                                   in0=self.psum[3][0:KH, 0:TA],
                                                   in1=self.g32[2][0:KH, :], op=TS.mult), inc=sv)
        self.bank_consumed(3, sv, v)
        wait("scalar", sv, v)
        em("scalar", lambda e: e.activation(out=self.g32[0][0:KH, :], in_=self.g32[0][0:KH, :],
                                            func=AF.Ln, scale=1.0 / 128, bias=1e-5))
        v = em("scalar", lambda e: e.activation(out=self.g32[0][0:KH, :], in_=self.g32[0][0:KH, :],
                                                func=AF.Exp, scale=-0.5), inc=ss)
        self.waitf("vector", self.gfin.free)
        wait("vector", ss, v)
        vgf = em("vector", lambda e: e.tensor_tensor(out=self.gfin.v[0:KH, :], in0=self.g32[0][0:KH, :],
                                                     in1=self.g32[1][0:KH, :], op=TS.mult), inc=sv)
        if b == 0 and ci == 0:
            self.dbg("yraw", self.y_raw.v)
            self.dbg("gfin", self.gfin.v[0:KH, :])
            self.dbg("silug", self.silu_g.v)

        wait("tensor", sv, vgf)
        wait("vector", ss, max(v_y_done, v_gates))
        self.waitf("vector", self.g_hi.free)
        self.waitf("vector", self.g_lo.free)
        for g in range(8):
            aux = 4 + g % 2
            bk = self.bank(aux, "tensor")
            v = em("tensor", lambda e, bk=bk, g=g: e.matmul(
                bk[:, 0:TA], self.Ee[g], self.gfin.v[0:KH, :], start=True, stop=True), inc=st_)
            wait("vector", st_, v)
            gfb = self.gft[g % 2]
            em("vector", lambda e, bk=bk, g=g, gfb=gfb: e.tensor_tensor(
                out=gfb, in0=self.y_raw.v[:, TA * g:TA * (g + 1)], in1=bk[:, 0:TA], op=TS.mult))
            vv = em("vector", lambda e, g=g, gfb=gfb: e.tensor_tensor(
                out=gfb, in0=gfb, in1=self.silu_g.v[:, TA * g:TA * (g + 1)], op=TS.mult), inc=sv)
            self.bank_consumed(aux, sv, vv)
            wait("scalar", sv, vv)
            vhi = em("scalar", lambda e, g=g, gfb=gfb: e.copy(
                out=self.g_hi.v[:, TA * g:TA * (g + 1)], in_=gfb), inc=ss)
            wait("vector", ss, vhi)
            em("vector", lambda e, g=g, gfb=gfb: e.tensor_tensor(
                out=self.g_lo.v[:, TA * g:TA * (g + 1)], in0=gfb,
                in1=self.g_hi.v[:, TA * g:TA * (g + 1)], op=TS.subtract))
        vgp = em("vector", lambda e: e.drain(),
                 inc=sv)
        self.gfin.free = (sv, vgp)
        self.silu_g.free = (sv, vgp)
        self.y_raw.free = (sv, vgp)

        wait("tensor", sv, vgp)
        for cg in range(2):
            bk = self.bank(6 + cg, "tensor")
            for k in range(8):
                sG = slice(TA * k, TA * (k + 1))
                sW = slice(D * k + 512 * cg, D * k + 512 * cg + 512)
                em("tensor", lambda e, bk=bk, sG=sG, sW=sW, k=k: e.matmul(
                    bk[:, :], self.g_hi.v[:, sG], self.WOhi[:, sW], start=(k == 0), stop=False))
                em("tensor", lambda e, bk=bk, sG=sG, sW=sW: e.matmul(
                    bk[:, :], self.g_lo.v[:, sG], self.WOhi[:, sW], start=False, stop=False))
                em("tensor", lambda e, bk=bk, sG=sG, sW=sW, k=k: e.matmul(
                    bk[:, :], self.g_hi.v[:, sG], self.WOlo[:, sW], start=False, stop=(k == 7)),
                   inc=(st_ if k == 7 else None))
        v_op = st_.n
        self.g_hi.free = (st_, v_op)
        self.g_lo.free = (st_, v_op)

        xr = self.xres_tm[ci % 2]
        self.waitf("vector", xr.free)
        wait("vector", st_, v_op)
        waitf("vector", t_gaB)
        if b == 0 and ci == 0:
            self.dbg("gaB", self.gaB)
        for cg in range(2):
            em("vector", lambda e, cg=cg, xr=xr: e.tensor_tensor(
                out=xr.v[:, 512 * cg:512 * cg + 512],
                in0=self.psum[6 + cg][:, :], in1=self.gaB[:, 512 * cg:512 * cg + 512], op=TS.mult))
            v = em("vector", lambda e, cg=cg, xr=xr, xb=xb: e.tensor_tensor(
                out=xr.v[:, 512 * cg:512 * cg + 512], in0=xr.v[:, 512 * cg:512 * cg + 512],
                in1=xb.v[:, 512 * cg:512 * cg + 512], op=TS.add), inc=sv)
            self.bank_consumed(6 + cg, sv, v)
        vxr = sv.n
        xb.free = (sv, vxr)
        wait("gpsimd", sv, vxr)
        if b == 0 and ci == 0:
            self.dbg("xres0", xr.v)
        vst = dma("gpsimd", None, self.xres_h[b, t0:t0 + TA, :], xr.v)

        wait("scalar", sv, vxr)
        v = em("scalar", lambda e, xr=xr: e.activation(out=self.sq_scr, in_=xr.v,
                                                       func=AF.Square), inc=ss)
        wait("vector", ss, v)
        em("vector", lambda e: e.tensor_reduce(out=self.lnrow[:, 6:7], in_=self.sq_scr,
                                               axis=AX.X, op=TS.add))
        em("vector", lambda e: e.drain())
        em("vector", lambda e, xr=xr: e.tensor_reduce(out=self.lnrow[:, 5:6], in_=xr.v,
                                                      axis=AX.X, op=TS.add))
        em("vector", lambda e: e.drain())
        em("vector", lambda e: e.tensor_scalar(out=self.lnrow[:, 5:6], in0=self.lnrow[:, 5:6],
                                               scalar1=1.0 / D, scalar2=None, op0=TS.mult, op1=TS.bypass))
        em("vector", lambda e: e.drain())
        em("vector", lambda e: e.tensor_tensor(out=self.lnrow[:, 7:8], in0=self.lnrow[:, 5:6],
                                               in1=self.lnrow[:, 5:6], op=TS.mult))
        em("vector", lambda e: e.drain())
        v = em("vector", lambda e: e.scalar_tensor_tensor(out=self.lnrow[:, 8:9], in0=self.lnrow[:, 6:7],
                                                          scalar=1.0 / D, in1=self.lnrow[:, 7:8],
                                                          op0=TS.mult, op1=TS.subtract), inc=sv)
        wait("scalar", sv, v)
        em("scalar", lambda e: e.activation(out=self.lnrow[:, 9:10], in_=self.lnrow[:, 8:9],
                                            func=AF.Ln, bias=1e-5))
        v = em("scalar", lambda e: e.activation(out=self.lnrow[:, 9:10], in_=self.lnrow[:, 9:10],
                                                func=AF.Exp, scale=-0.5), inc=ss)
        wait("vector", ss, v)
        em("vector", lambda e, it=itile: e.tensor_copy(out=self.stats[:, 2 * it:2 * it + 1],
                                                       in_=self.lnrow[:, 5:6]))
        em("vector", lambda e, it=itile: e.tensor_copy(out=self.stats[:, 2 * it + 1:2 * it + 2],
                                                       in_=self.lnrow[:, 9:10]))
        t2b = self.t2_tm
        self.waitf("vector", t2b.free)
        v2 = em("vector", lambda e, xr=xr, t2b=t2b: e.tensor_scalar(
            out=t2b.v, in0=xr.v, scalar1=self.lnrow[:, 5:6], scalar2=self.lnrow[:, 9:10],
            op0=TS.subtract, op1=TS.mult), inc=sv)
        self.lnrow_free = (sv, v2)
        xr.free = vst
        wait("tensor", sv, v2)
        self.waitf("tensor", self.t2fm.free)
        for r in range(2):
            bk = self.bank(0, "tensor")
            for j in range(4):
                jj = 4 * r + j
                em("tensor", lambda e, bk=bk, j=j, jj=jj, t2b=t2b: e.transpose(
                    bk[:, 128 * j:128 * j + 128], t2b.v[:, 128 * jj:128 * jj + 128], self.ident),
                   inc=(st_ if j == 3 else None))
            wait("scalar", st_, st_.n)
            v = em("scalar", lambda e, bk=bk, r=r: e.copy(
                out=self.t2fm.v[:, 512 * r:512 * r + 512], in_=bk[:, :]), inc=ss)
            self.bank_consumed(0, ss, v)
        t2b.free = (st_, st_.n)
        v_t2fm = ss.n
        wait("vector", ss, v_t2fm)
        for j in range(8):
            sl = slice(TA * j, TA * (j + 1))
            em("vector", lambda e, j=j, sl=sl: e.tensor_scalar(
                out=self.hbt.v, in0=self.t2fm.v[:, sl], scalar1=self.cbv[b][:, j:j + 1],
                scalar2=self.sbv[b][:, j:j + 1], op0=TS.mult, op1=TS.add))
            em("vector", lambda e, j=j: e.tensor_reduce(
                out=self.am8[:, j:j + 1], in_=self.hbt.v, axis=AX.X, op=TS.max,
                apply_absolute_value=True))
        em("vector", lambda e: e.drain())
        em("vector", lambda e: e.tensor_reduce(out=self.am8[:, 8:9], in_=self.am8[:, 0:8],
                                               axis=AX.X, op=TS.max))
        em("vector", lambda e: e.drain())
        em("vector", lambda e: e.tensor_tensor(out=self.amaxh, in0=self.amaxh,
                                               in1=self.am8[:, 8:9], op=TS.max))
        v = em("vector", lambda e: e.drain(), inc=sv)
        self.t2fm.free = (sv, v)
        return vst

    def exchange(self, which):
        em, wait, dma, waitf = self.em, self.wait, self.dma, self.waitf
        sv, ss, st_, cc = self.sv, self.ss, self.st_, self.cc
        TS = ALU
        amax_l = self.amaxh if which == 1 else self.amaxd
        ci, co = (self.cc1i_h, self.cc1o_h) if which == 1 else (self.cc2i_h, self.cc2o_h)
        sdst = self.sxv if which == 1 else self.shv
        v0 = self.em("vector", lambda e: e.drain(), inc=sv)
        self.dbg("amx%d" % which, amax_l)
        bk = self.bank(0, "tensor")
        wait("tensor", sv, v0)
        v = em("tensor", lambda e, bk=bk, amax_l=amax_l: e.transpose(
            bk[0:1, 0:128], amax_l, self.ident), inc=st_)
        wait("vector", st_, v)
        em("vector", lambda e, bk=bk: e.tensor_reduce(out=self.rowt[0:1, 0:1], in_=bk[0:1, 0:128],
                                                      axis=AX.X, op=TS.max))
        em("vector", lambda e: e.drain())
        em("vector", lambda e: e.tensor_scalar(out=self.rowt[0:1, 0:1], in0=self.rowt[0:1, 0:1],
                                               scalar1=1e-8, scalar2=None, op0=TS.max, op1=TS.bypass))
        em("vector", lambda e: e.drain())
        for q in range(1, 4):
            em("vector", lambda e, q=q: e.tensor_copy(out=self.rowt[0:1, q:q + 1],
                                                      in_=self.rowt[0:1, 0:1]))
        v = em("vector", lambda e: e.drain(), inc=sv)
        self.bank_consumed(0, sv, v)
        wait("gpsimd", sv, v)
        self.dbg("rowa%d" % which, self.rowt[0:1, 0:4])
        t_ = dma("gpsimd", None, ci[:, :], self.rowt[0:1, 0:4])
        waitf("gpsimd", t_)
        v = em("gpsimd", lambda e, ci=ci, co=co: e.collective_compute(
            "AllReduce", ALU.max, replica_groups=[list(range(NCORES_RUN[0]))],
            ins=[ci.opt()], outs=[co.opt()]), inc=cc)
        wait("gpsimd", cc, v)
        t_ = dma("gpsimd", None, self.rowt[0:1, 8:12], co[:, :])
        waitf("vector", t_)
        em("vector", lambda e: e.reciprocal(out=self.rowt[0:1, 13:14], in_=self.rowt[0:1, 8:9]))
        em("vector", lambda e: e.drain())
        em("vector", lambda e: e.tensor_scalar(out=self.rowt[0:1, 12:13], in0=self.rowt[0:1, 13:14],
                                               scalar1=32767.0, scalar2=None,
                                               op0=TS.mult, op1=TS.bypass))
        v = em("vector", lambda e: e.drain(), inc=sv)
        wait("gpsimd", sv, v)
        self.dbg("rowb%d" % which, self.rowt[0:1, 8:14])
        t_ = dma("gpsimd", None, self.bc_h[10 + which:11 + which, 0:1], self.rowt[0:1, 12:13])
        waitf("gpsimd", t_)
        t_ = dma("gpsimd", None, sdst,
                 self.bc_h[10 + which:11 + which, 0:1].partition_broadcast(128))
        waitf("vector", t_)
        TSx = ALU
        self.dbg("scl%d" % which, sdst)
        if which == 1:
            for b in range(2):
                em("vector", lambda e, b=b: e.tensor_scalar(
                    out=self.cbsxv[b], in0=self.cbv[b], scalar1=self.sxv[:, 0:1], scalar2=None,
                    op0=TSx.mult, op1=TSx.bypass))
                em("vector", lambda e, b=b: e.tensor_scalar(
                    out=self.sbsxv[b], in0=self.sbv[b], scalar1=self.sxv[:, 0:1], scalar2=None,
                    op0=TSx.mult, op1=TSx.bypass))
            em("vector", lambda e: e.tensor_scalar(
                out=self.iswx, in0=self.swv, scalar1=self.sxv[:, 0:1], scalar2=None,
                op0=TSx.mult, op1=TSx.bypass))
            em("vector", lambda e: e.drain())
            v = em("vector", lambda e: e.reciprocal(out=self.iswx, in_=self.iswx), inc=sv)
            em("vector", lambda e: e.drain())
        else:
            em("vector", lambda e: e.tensor_scalar(
                out=self.vrv, in0=self.swov, scalar1=self.shv[:, 0:1], scalar2=None,
                op0=TSx.mult, op1=TSx.bypass))
            em("vector", lambda e: e.drain())
            v = em("vector", lambda e: e.reciprocal(out=self.vrv, in_=self.vrv), inc=sv)
            em("vector", lambda e: e.drain())
        self.barrier()

    def alloc_B1(self):
        A = Arena(self.arena_t.ap(), self.AR, "B1", self.res_end)
        T = self.TB1
        NT = T // 128
        self.Wq1 = A.alloc(8 * DFF, BF16)
        self.b1_x = [A.buf(D) for _ in range(3)]
        self.b1_t = [A.buf(D) for _ in range(2)]
        self.b1_tf = A.buf(NT * 8 * 128)
        self.b1_w1 = A.alloc(T)
        self.b1_q = A.alloc(T)
        self.q_hi = A.buf(8 * T, BF16)
        self.q_lo = A.buf(8 * T, BF16)
        self.hidb = [A.buf(T) for _ in range(3)]
        assert A.off <= self.AR, A.off
        self.B1_end = A.off

    def b1_chunk(self, b, ci):
        T = self.TB1
        NT = T // 128
        t0 = ci * T
        em, wait, dma, waitf = self.em, self.wait, self.dma, self.waitf
        sv, ss, st_ = self.sv, self.ss, self.st_
        TS = ALU
        self.waitf("tensor", self.b1_tf.free)
        for i in range(NT):
            xb = self.b1_x[i % 3]
            self.waitf("sync", xb.free)
            t_x = dma("sync", None, xb.v, self.xres_h[b, t0 + 128 * i:t0 + 128 * (i + 1), :])
            it = (b * self.L + t0 + 128 * i) // 128
            tb = self.b1_t[i % 2]
            self.waitf("vector", tb.free)
            waitf("vector", t_x)
            vt = em("vector", lambda e, xb=xb, tb=tb, it=it: e.tensor_scalar(
                out=tb.v, in0=xb.v, scalar1=self.stats[:, 2 * it:2 * it + 1],
                scalar2=self.stats[:, 2 * it + 1:2 * it + 2],
                op0=TS.subtract, op1=TS.mult), inc=sv)
            xb.free = (sv, vt)
            wait("tensor", sv, vt)
            for r in range(2):
                bk = self.bank(0, "tensor")
                for j in range(4):
                    jj = 4 * r + j
                    em("tensor", lambda e, bk=bk, j=j, jj=jj, tb=tb: e.transpose(
                        bk[:, 128 * j:128 * j + 128], tb.v[:, 128 * jj:128 * jj + 128],
                        self.ident), inc=(st_ if j == 3 else None))
                wait("scalar", st_, st_.n)
                v = em("scalar", lambda e, bk=bk, r=r, i=i: e.copy(
                    out=self.b1_tf.v.rearrange("p (j t) -> p j t", j=8)[:, 4 * r:4 * r + 4,
                                                                       128 * i:128 * i + 128],
                    in_=bk[:, :].rearrange("p (j t) -> p j t", j=4)), inc=ss)
                self.bank_consumed(0, ss, v)
            tb.free = (st_, st_.n)
        v_tf = self.ss.n
        wait("vector", ss, v_tf)
        self.waitf("vector", self.q_hi.free)
        self.waitf("vector", self.q_lo.free)
        for j in range(8):
            sl = slice(T * j, T * (j + 1))
            em("vector", lambda e, j=j, sl=sl, b=b: e.tensor_scalar(
                out=self.b1_w1[:, 0:T], in0=self.b1_tf.v[:, sl], scalar1=self.cbsxv[b][:, j:j + 1],
                scalar2=self.sbsxv[b][:, j:j + 1], op0=TS.mult, op1=TS.add))
            em("vector", lambda e: e.drain())
            vqj = em("vector", lambda e, j=j: e.tensor_scalar(
                out=self.b1_q[:, 0:T], in0=self.b1_w1[:, 0:T], scalar1=RC, scalar2=RC,
                op0=TS.add, op1=TS.subtract), inc=sv)
            wait("scalar", sv, vqj)
            vhi = em("scalar", lambda e, j=j, sl=sl: e.copy(
                out=self.q_hi.v[:, sl], in_=self.b1_q[:, 0:T]), inc=ss)
            wait("vector", ss, vhi)
            em("vector", lambda e, j=j, sl=sl: e.tensor_tensor(
                out=self.q_lo.v[:, sl], in0=self.b1_q[:, 0:T],
                in1=self.q_hi.v[:, sl], op=TS.subtract))
        vq = em("vector", lambda e: e.drain(), inc=sv)
        self.b1_tf.free = (sv, vq)
        if b == 0 and ci == 0:
            self.dbg("b1tf", self.b1_tf.v[:, 0:1024])
            self.dbg("qhi", self.q_hi.v[:, 0:1024])
            self.dbg("qlo", self.q_lo.v[:, 0:1024])
        wait("tensor", sv, vq)
        NH = T // 512 if T >= 512 else 1
        NW = min(512, T)
        for ct in range(32):
            bis = [1 + (2 * ct + h) % 4 for h in range(NH)]
            bks = [self.bank(bi, "tensor") for bi in bis]
            for k in range(8):
                sW = slice(DFF * k + 128 * ct, DFF * k + 128 * ct + 128)
                for h in range(NH):
                    sQ = slice(T * k + NW * h, T * k + NW * (h + 1))
                    em("tensor", lambda e, bk=bks[h], sW=sW, sQ=sQ, k=k: e.matmul(
                        bk[:, 0:NW], self.Wq1[:, sW], self.q_hi.v[:, sQ],
                        start=(k == 0), stop=False))
                    em("tensor", lambda e, bk=bks[h], sW=sW, sQ=sQ, k=k: e.matmul(
                        bk[:, 0:NW], self.Wq1[:, sW], self.q_lo.v[:, sQ],
                        start=False, stop=(k == 7)), inc=(st_ if (k == 7 and h == NH - 1) else None))
            vmm = st_.n
            hb = self.hidb[ct % 3]
            self.waitf("scalar", hb.free)
            wait("scalar", st_, vmm)
            vg = ss.n
            for h in range(NH):
                vg = em("scalar", lambda e, hb=hb, bk=bks[h], ct=ct, h=h: e.activation(
                    out=hb.v[:, NW * h:NW * (h + 1)], in_=bk[:, 0:NW], func=AF.Gelu,
                    scale=self.iswx[:, ct:ct + 1], bias=self.biT[:, ct:ct + 1]), inc=ss)
                self.bank_consumed(bis[h], ss, vg)
            wait("gpsimd", ss, vg)
            if b == 0 and ci == 0 and ct == 0:
                self.dbg("hid0", hb.v)
            t_hid = dma("gpsimd", None, self.hid_h[128 * ct:128 * ct + 128,
                                                   b * self.L + t0:b * self.L + t0 + T], hb.v)
            wait("vector", ss, vg)
            em("vector", lambda e, hb=hb, ct=ct: e.tensor_reduce(
                out=self.am8[:, 16 + ct % 8:17 + ct % 8], in_=hb.v, axis=AX.X, op=TS.max,
                apply_absolute_value=True))
            em("vector", lambda e: e.drain())
            em("vector", lambda e, ct=ct: e.tensor_tensor(
                out=self.amaxd, in0=self.amaxd, in1=self.am8[:, 16 + ct % 8:17 + ct % 8],
                op=TS.max))
            v = em("vector", lambda e: e.drain(), inc=sv)
            hb.free = t_hid
        self.q_hi.free = (st_, st_.n)
        self.q_lo.free = (st_, st_.n)

    def alloc_B2(self):
        A = Arena(self.arena_t.ap(), self.AR, "B2", self.res_end)
        T = self.TB2
        self.Wq2 = A.alloc(32 * D, BF16)
        self.gbB = A.alloc(D)
        self.bor = A.alloc(D, rows=1)
        self.b2_h = [A.buf(T) for _ in range(3)]
        self.b2_a = A.alloc(T)
        self.a_hi = [A.buf(T, BF16) for _ in range(2)]
        self.a_lo = [A.buf(T, BF16) for _ in range(2)]
        self.b2_x = [A.buf(D) for _ in range(5)]
        self.b2_o = [A.buf(D) for _ in range(2)]
        assert A.off <= self.AR, A.off
        self.B2_end = A.off

    def b2_load_w(self):
        em, wait, dma = self.em, self.wait, self.dma
        self.wq2_toks = [dma("sync", None, self.bor, self.bo_h[:, :])]
        for s in range(32):
            self.wq2_toks.append(dma("sync", None, self.Wq2[:, D * s:D * (s + 1)],
                                     self.wqo_h[128 * s:128 * s + 128, :]))

    def b2_chunk(self, b, ci):
        T = self.TB2
        NT = T // 128
        t0 = ci * T
        gt0 = b * self.L + t0
        em, wait, dma, waitf = self.em, self.wait, self.dma, self.waitf
        sv, ss, st_ = self.sv, self.ss, self.st_
        TS = ALU
        if ci == 0 and b == 0:
            for t_ in self.wq2_toks:
                waitf("tensor", t_)
            if self.debug:
                for t_ in self.wq2_toks:
                    waitf("gpsimd", t_)
            self.dbg("wq2s0", self.Wq2[:, 0:1024])
            self.dbg("wq2s31", self.Wq2[:, 31 * D:32 * D])
            self.dbg("bor0", self.bor[0:1, :])
        if ci == 0:
            wait("gpsimd", sv, sv.n)
            self.t_gbB = dma("gpsimd", None, self.gbB,
                             self.bc_h[2 + b:3 + b, :].partition_broadcast(128))
        t_gbB = self.t_gbB
        xts = []
        for i in range(NT):
            xb = self.b2_x[(ci * NT + i) % 5]
            self.waitf("sync", xb.free)
            xb.ready = dma("sync", None, xb.v,
                           self.xres_h[b, t0 + 128 * i:t0 + 128 * (i + 1), :])
            xts.append(xb)
        banks = {}
        for tg in range(NT):
            for cg in range(2):
                bi = 2 * tg + cg
                banks[(tg, cg)] = self.bank(bi, "tensor")
        for k in range(32):
            hb = self.b2_h[k % 3]
            self.waitf("sync", hb.free)
            t_h = dma("sync", None, hb.v, self.hid_h[128 * k:128 * k + 128, gt0:gt0 + T])
            ah, al = self.a_hi[k % 2], self.a_lo[k % 2]
            self.waitf("vector", ah.free)
            self.waitf("vector", al.free)
            waitf("vector", t_h)
            em("vector", lambda e, hb=hb, k=k: e.tensor_scalar(
                out=self.b2_a, in0=hb.v, scalar1=self.shv[:, 0:1], scalar2=RC,
                op0=TS.mult, op1=TS.add))
            vq = em("vector", lambda e, k=k: e.tensor_scalar(
                out=self.b2_a, in0=self.b2_a, scalar1=RC, scalar2=self.vrv[:, k:k + 1],
                op0=TS.subtract, op1=TS.mult), inc=sv)
            hb.free = (sv, vq)
            wait("scalar", sv, vq)
            va = em("scalar", lambda e, ah=ah: e.copy(out=ah.v, in_=self.b2_a), inc=ss)
            wait("vector", ss, va)
            vl = em("vector", lambda e, ah=ah, al=al: e.tensor_tensor(
                out=al.v, in0=self.b2_a, in1=ah.v, op=TS.subtract), inc=sv)
            if b == 0 and ci == 0 and k == 0:
                self.dbg("b2a", self.b2_a)
            wait("tensor", ss, va)
            wait("tensor", sv, vl)
            for tg in range(NT):
                sA = slice(128 * tg, 128 * (tg + 1))
                for cg in range(2):
                    bk = banks[(tg, cg)]
                    sW = slice(D * k + 512 * cg, D * k + 512 * cg + 512)
                    em("tensor", lambda e, bk=bk, sA=sA, sW=sW, ah=ah, k=k: e.matmul(
                        bk[:, :], ah.v[:, sA], self.Wq2[:, sW], start=(k == 0), stop=False))
                    lastk = (tg == NT - 1 and cg == 1)
                    em("tensor", lambda e, bk=bk, sA=sA, sW=sW, al=al, k=k: e.matmul(
                        bk[:, :], al.v[:, sA], self.Wq2[:, sW], start=False, stop=False),
                       inc=(st_ if lastk else None))
            ah.free = (st_, st_.n)
            al.free = (st_, st_.n)
        for tg in range(NT):
            for cg in range(2):
                bk = banks[(tg, cg)]
                em("tensor", lambda e, bk=bk, cg=cg: e.matmul(
                    bk[:, :], self.onesr[0:1, 0:128], self.bor[0:1, 512 * cg:512 * cg + 512],
                    start=False, stop=True), inc=st_)
        v_mm = st_.n
        wait("vector", st_, v_mm)
        waitf("vector", t_gbB)
        if b == 0 and ci == 0:
            self.dbg("gbB0", self.gbB[:, 0:1024])
            self.dbg("xbB0", xts[0].v)
        for tg in range(NT):
            ob = self.b2_o[tg % 2]
            self.waitf("vector", ob.free)
            xb = xts[tg]
            waitf("vector", xb.ready)
            for cg in range(2):
                bi = 2 * tg + cg
                em("vector", lambda e, ob=ob, bi=bi, cg=cg: e.tensor_tensor(
                    out=ob.v[:, 512 * cg:512 * cg + 512], in0=self.psum[bi][:, :],
                    in1=self.gbB[:, 512 * cg:512 * cg + 512], op=TS.mult))
                v = em("vector", lambda e, ob=ob, xb=xb, cg=cg: e.tensor_tensor(
                    out=ob.v[:, 512 * cg:512 * cg + 512],
                    in0=ob.v[:, 512 * cg:512 * cg + 512],
                    in1=xb.v[:, 512 * cg:512 * cg + 512], op=TS.add), inc=sv)
                self.bank_consumed(bi, sv, v)
            vo = sv.n
            xb.free = (sv, vo)
            if b == 0 and ci == 0 and tg == 0:
                self.dbg("yout0", ob.v)
            wait("gpsimd", sv, vo)
            ob.free = dma("gpsimd", None, self.y_h[b, t0 + 128 * tg:t0 + 128 * (tg + 1), :],
                          ob.v)

    def finish(self):
        for s in self.pool["gpsimd"]:
            if s.n:
                self.wait("gpsimd", s, s.n)
        for s in self.pool["sync"]:
            if s.n:
                self.wait("sync", s, s.n)

    def build(self):
        self.prologue()
        self.alloc_A()
        nchunks = self.L // self.TA
        it = 0
        for b in range(B_PER_CORE):
            for ci in range(nchunks):
                self.a_chunk(b, ci, (b * self.L + ci * self.TA) // 128)
        self.barrier()
        self.exchange(1)
        self.alloc_B1()
        for k in range(8):
            t_ = self.dma("sync", None, self.Wq1[:, DFF * k:DFF * (k + 1)],
                          self.wqi_h[128 * k:128 * k + 128, :])
            self.waitf("tensor", t_)
        for b in range(B_PER_CORE):
            for ci in range(self.L // self.TB1):
                self.b1_chunk(b, ci)
        self.barrier()
        self.exchange(2)
        self.alloc_B2()
        self.b2_load_w()
        for b in range(B_PER_CORE):
            for ci in range(self.L // self.TB2):
                self.b2_chunk(b, ci)
        self.finish()
        nc = self.nc
        with nc.Block() as blk:
            @blk.sync
            def _(e):
                for f in self.S["sync"]:
                    f(e)

            @blk.vector
            def _(e):
                for f in self.S["vector"]:
                    f(e)

            @blk.scalar
            def _(e):
                for f in self.S["scalar"]:
                    f(e)

            @blk.tensor
            def _(e):
                for f in self.S["tensor"]:
                    f(e)

            @blk.gpsimd
            def _(e):
                for f in self.S["gpsimd"]:
                    f(e)
        self.stack.close()
        return nc


NCORES_RUN = [8]


def _host_inputs(inp, L):
    x = np.ascontiguousarray(inp["L0_attn_x"])
    c = np.ascontiguousarray(inp["L0_c"])
    cw = np.asarray(inp["L0_attn_conv_w"]).reshape(4, TOTAL)
    cwp = np.zeros((4, NCT * 128), np.float32)
    cwp[:, :TOTAL] = cw
    cw_l = np.ascontiguousarray(cwp.reshape(4, NCT, 128).transpose(2, 0, 1).reshape(128, 4 * NCT))
    cbp = np.zeros((NCT * 128,), np.float32)
    cbp[:TOTAL] = np.asarray(inp["L0_attn_conv_b"])
    cb_l = np.ascontiguousarray(cbp.reshape(NCT, 128).T)
    theta = np.asarray(inp["L0_attn_theta"])
    th2 = np.ascontiguousarray(theta[0, 0, 0, 0, 2:4].reshape(1, 2))
    nsc = np.ascontiguousarray(np.asarray(inp["L0_attn_norm_scale"]).reshape(2, 128).T)
    maps = []
    ncore = NCORES_RUN[0]
    for i in range(ncore):
        maps.append({
            "x": np.ascontiguousarray(x[2 * i:2 * i + 2]),
            "c2": np.ascontiguousarray(c[2 * i:2 * i + 2]),
            "w_inproj": np.ascontiguousarray(inp["L0_attn_in_proj_w"]),
            "conv_w_l": cw_l,
            "conv_b_l": cb_l,
            "theta2": th2,
            "dslope": np.ascontiguousarray(np.asarray(inp["L0_attn_decay_slopes"]).reshape(28, 1)),
            "aslope": np.ascontiguousarray(np.asarray(inp["L0_attn_anchor_slopes"]).reshape(4, 1)),
            "score": np.ascontiguousarray(np.asarray(inp["L0_attn_score_scale"]).reshape(KH, 1)),
            "deriv": np.ascontiguousarray(np.asarray(inp["L0_attn_deriv_logits"]).reshape(1, 3)),
            "nsc": nsc,
            "w_re": np.ascontiguousarray(inp["L0_attn_W_re"]),
            "w_im": np.ascontiguousarray(inp["L0_attn_W_im"]),
            "w_outproj": np.ascontiguousarray(inp["L0_attn_out_proj_w"]),
            "w_in_T": np.ascontiguousarray(np.asarray(inp["L0_ffn_W_in"]).T),
            "b_in_t": np.ascontiguousarray(np.asarray(inp["L0_ffn_b_in"]).reshape(32, 128).T),
            "w_out_f": np.ascontiguousarray(inp["L0_ffn_W_out"]),
            "b_out_r": np.ascontiguousarray(np.asarray(inp["L0_ffn_b_out"]).reshape(1, D)),
            "wmod": np.ascontiguousarray(inp["L0_wmod"]),
        })
    return maps


def kernel(**inputs):
    L = int(np.asarray(inputs["L0_attn_x"]).shape[1])
    kb = KB(L=L)
    nc = kb.build()
    maps = _host_inputs(inputs, L)
    res = run_bass_kernel_spmd(nc, maps, core_ids=list(range(NCORES_RUN[0])))
    out = np.concatenate([np.asarray(r["y"]) for r in res.results], axis=0)
    return out
